# version 19
# baseline (speedup 1.0000x reference)
import contextlib
import numpy as np
import ml_dtypes
import concourse.bass as bass
import concourse.mybir as mybir
from concourse.bass_utils import run_bass_kernel_spmd

F32 = mybir.dt.float32
BF16 = mybir.dt.bfloat16
AF = mybir.ActivationFunctionType
ALU = mybir.AluOpType

NCORES = 8
D = 1024
NCH = 8
TCORE = 4096
H = 16
HD = 64
LN_EPS = 1e-5
ATT_EPS = 1e-6
BF = ml_dtypes.bfloat16

ENGINES = ("sync", "scalar", "vector", "gpsimd", "tensor")


class PB:

    def __init__(self, nc):
        self.nc = nc
        self.items = {e: [] for e in ENGINES}
        self.cnt = {}

    def n(self, name):
        return self.cnt.get(name, 0)

    def em(self, eng, fn, waits=(), inc=None, inc_n=1):
        w = [(s, v) for (s, v) in waits if v > 0]
        self.items[eng].append((w, fn, inc, inc_n))
        if inc is not None:
            self.cnt[inc] = self.cnt.get(inc, 0) + inc_n
        return self.cnt[inc] if inc is not None else None

    def dma(self, eng, out, in_, inc, waits=()):
        return self.em(eng, lambda e, o=out, i=in_: e.dma_start(out=o, in_=i),
                       waits=waits, inc=inc, inc_n=16)

    def drain(self, eng):
        self.em(eng, lambda e: e.drain())

    def build(self, ctx):
        sems = {name: ctx.enter_context(self.nc.semaphore(name))
                for name in self.cnt}
        block = ctx.enter_context(self.nc.Block())

        def mk(engname):
            def run(engine):
                for waits, fn, inc, inc_n in self.items[engname]:
                    for (s, v) in waits:
                        engine.wait_ge(sems[s], v)
                    instr = fn(engine)
                    if inc is not None:
                        instr.then_inc(sems[inc], inc_n)
            return run

        block.sync(mk("sync"))
        block.scalar(mk("scalar"))
        block.vector(mk("vector"))
        block.gpsimd(mk("gpsimd"))
        block.tensor(mk("tensor"))


def load_weight_bf16(pb, ctx, name, dram_ap, shape3, stage,
                     dma_eng="scalar", cv_eng="vector", pre="w", tile=None):
    nc = pb.nc
    p, kc, o = shape3
    t = tile if tile is not None else ctx.enter_context(
        nc.sbuf_tensor(name + "_bf", [p, kc * o], BF16))
    tv = t[:].rearrange("p (k o) -> p k o", k=kc) if kc > 1 else None
    src = dram_ap.rearrange("(k p) o -> p k o", p=p) if kc > 1 else dram_ap
    scols = stage.shape[1] // 2
    done = 0
    scv = pre + "cv"
    i = 0
    for k in range(kc):
        off = 0
        while off < o:
            w = min(scols, o - off)
            par = i % 2
            half = par * scols
            sld = pre + "ld%d" % par
            s_ap = src[:, k, off:off + w] if kc > 1 else dram_ap[:, off:off + w]
            d_ap = tv[:, k, off:off + w] if kc > 1 else t[:, off:off + w]
            nld = pb.dma(dma_eng, stage[0:p, half:half + w], s_ap, inc=sld,
                         waits=[(scv, pb.n(scv) if i == 0 else max(0, pb.n(scv) - 1))])
            done = pb.em(cv_eng, lambda e, dst=d_ap, sp=stage[0:p, half:half + w]:
                         e.tensor_copy(dst, sp),
                         waits=[(sld, nld)], inc=scv)
            off += w
            i += 1
    return t, done


P0_IN = ["X", "WK", "WV", "WIN", "EB", "W1F", "B1F", "W2BD", "WOUT", "BIN",
         "B2", "BOUTC", "I64", "CONSTC", "ONESR"]
P0_OUT = ["XN", "KFM", "VTM", "G0", "SZ", "SNP"]
P0_SHAPES = {
    "X": ([D, TCORE], F32), "WK": ([D, D], F32), "WV": ([D, D], F32),
    "WIN": ([D, 64], F32), "EB": ([64, D], F32), "W1F": ([128, 8], F32),
    "B1F": ([128, 8], F32), "W2BD": ([D, 64], F32), "WOUT": ([64, D], F32),
    "BIN": ([64, 1], F32), "B2": ([64, 1], F32), "BOUTC": ([128, 8], F32),
    "I64": ([128, 128], F32), "CONSTC": ([128, 4], F32), "ONESR": ([1, 128], F32),
    "XN": ([128, NCH * TCORE], BF16), "KFM": ([128, NCH * TCORE], BF16),
    "VTM": ([128, (TCORE // 128) * 1040], BF16), "G0": ([128, NCH * TCORE], BF16),
    "SZ": ([64, 1040], F32), "SNP": ([64, (TCORE // 128) * 1040], F32),
}


def emit_pass0(nc, pb, ctx, io, ST=512, STAGE=99, NSTLIM=99):
    NST = min(TCORE // ST, NSTLIM)
    CH_T = ST // 128
    (X, WK, WV, WIN, EB, W1F, B1F, W2BD, WOUT, BIN, B2, BOUTC, I64, CONSTC,
     ONESR) = [io[k] for k in P0_IN]
    XN, KFM, VTM, G0, SZ, SNP = [io[k] for k in P0_OUT]
    SZr = SZ.rearrange("p (g k q e) -> q p g k e", g=2, k=4, q=2, e=65)
    SNPr = SNP.rearrange("p (cg g k q e) -> cg q p g k e", cg=TCORE // 128, g=2, k=4, q=2, e=65)

    XNv = XN.rearrange("p (c t) -> p c t", c=NCH)
    KFMv = KFM.rearrange("p (c t) -> p c t", c=NCH)
    G0v = G0.rearrange("p (c t) -> p c t", c=NCH)
    VTMv = VTM.rearrange("p (c o) -> p c o", c=TCORE // 128)
    Xv = X.rearrange("(c p) t -> p c t", p=128)

    if True:
        sb = lambda nm, shp, dt=F32: ctx.enter_context(nc.sbuf_tensor(nm, shp, dt))
        ps = lambda nm, shp, dt=F32: ctx.enter_context(nc.psum_tensor(nm, shp, dt))

        stage = sb("stage", [128, 2048])
        xin = [sb("xin0", [128, NCH * ST])] * 2
        xbf = sb("xbf", [128, NCH * ST], BF16)
        xsq = sb("xsq", [128, NCH * ST], BF16)
        xnf = sb("xnf", [128, NCH * ST])
        xnb = sb("xnb", [128, NCH * ST], BF16)
        kfm = sb("kfm", [128, NCH * ST], BF16)
        efp = sb("efp", [128, ST])
        em1 = sb("em1", [128, ST])
        ktm = sb("ktm", [128, CH_T * 1024], BF16)
        vtm = sb("vtm", [128, CH_T * 1040], BF16)
        sfb = sb("sfb", [64, ST], BF16)
        hfb = sb("hfb", [128, NCH * ST], BF16)
        gfb = sb("gfb", [64, ST], BF16)
        g0b = sb("g0b", [128, NCH * ST], BF16)
        rows = sb("rows", [1, 4 * ST])
        statef = sb("statef", [128, 520])
        inv1024 = sb("inv1024", [128, 1], BF16)
        cns = sb("cns", [128, 4])
        ones1f = sb("ones1f", [1, 128])
        i64b = sb("i64b", [128, 128], BF16)
        i64f = sb("i64f", [128, 128])
        binc = sb("binc", [64, 1])
        b2c = sb("b2c", [64, 1])
        boutc = sb("boutc", [128, 8])
        w1f = sb("w1f", [128, 8])
        b1f = sb("b1f", [128, 8])

        p0 = ps("p0", [128, 512])
        p1 = ps("p1", [128, 512])
        p2 = ps("p2", [128, 512])
        p3 = ps("p3", [128, 512])
        p4 = ps("p4", [128, 1024], BF16)
        pst = [ps("pst0", [128, 512]), ps("pst1", [128, 512])]

        xbv = xbf[:].rearrange("p (c t) -> p c t", c=NCH)
        xqv = xsq[:].rearrange("p (c t) -> p c t", c=NCH)
        xnfv = xnf[:].rearrange("p (c t) -> p c t", c=NCH)
        xnbv = xnb[:].rearrange("p (c t) -> p c t", c=NCH)
        kfv = kfm[:].rearrange("p (c t) -> p c t", c=NCH)
        hfv = hfb[:].rearrange("p (c t) -> p c t", c=NCH)
        g0v = g0b[:].rearrange("p (c t) -> p c t", c=NCH)
        ktv = ktm[:].rearrange("p (c o) -> p c o", c=CH_T)
        vtv = vtm[:].rearrange("p (c o) -> p c o", c=CH_T)

        pb.dma("sync", cns[:], CONSTC, inc="ldc")
        pb.dma("sync", i64f[:], I64, inc="ldc")
        pb.dma("sync", ones1f[:], ONESR, inc="ldc")
        for dst, srcap in [(binc, BIN), (b2c, B2), (boutc, BOUTC), (w1f, W1F), (b1f, B1F)]:
            pb.dma("sync", dst[:], srcap, inc="ldc")
        LDC = pb.n("ldc")
        pb.em("scalar", lambda e: e.activation(out=inv1024[:], in_=cns[:, 0:1], func=AF.Copy),
              waits=[("ldc", LDC)], inc="act")
        cv_i64 = pb.em("scalar", lambda e: e.activation(out=i64b[:], in_=i64f[:], func=AF.Copy),
                       inc="act")
        ld_ones = LDC
        ld_small = LDC

        wk, cv_wk = load_weight_bf16(pb, ctx, "wk", WK, (128, 8, D), stage)
        wv, cv_wv = load_weight_bf16(pb, ctx, "wv", WV, (128, 8, D), stage)
        win, cv_win = load_weight_bf16(pb, ctx, "win", WIN, (128, 8, 64), stage)
        eb, cv_eb = load_weight_bf16(pb, ctx, "eb", EB, (64, 1, D), stage)
        w2bd, cv_bd = load_weight_bf16(pb, ctx, "w2bd", W2BD, (128, 8, 64), stage)
        wout, cv_wo = load_weight_bf16(pb, ctx, "wout", WOUT, (64, 1, D), stage)
        if io.get("_preload") is not None:
            io["_preload"](pb)
        wkv = wk[:].rearrange("p (k o) -> p k o", k=8)
        wvv = wv[:].rearrange("p (k o) -> p k o", k=8)
        winv = win[:].rearrange("p (k o) -> p k o", k=8)
        w2bdv = w2bd[:].rearrange("p (k o) -> p k o", k=8)

        n_ones = pb.em("vector", lambda e: e.memset(
            vtm[:].rearrange("p (c h o) -> p c h o", c=CH_T, h=16)[:, :, :, 64:65], 1.0), inc="dve")
        pb.drain("vector")
        zc = ctx.enter_context(nc.sbuf_tensor("zc", [1, 640], BF16))
        nz = pb.em("vector", lambda e: e.memset(zc[:], 0.0), inc="dve")
        pb.drain("vector")
        nz = pb.em("vector", lambda e: e.memset(statef[:], 0.0), inc="dve")
        pb.drain("vector")

        first_m4 = [True, True, True]

        def state_slot(h):
            g, s_ = h // 8, h % 8
            return g, 64 * (s_ % 2), 65 * (s_ // 2)

        mark = {"xn_st": 0, "kfm_st": 0, "vtm_st": 0, "g0_st": 0,
                "xn_dve": 0, "state_dve": nz, "ktm_act": 0, "ffn_pe": 0,
                "ffnb_pe_0": 0, "ffnb_pe_1": 0}

        def front0(s):
            T0 = ST * s
            T0 = ST * s
            xb = xin[s % 2]
            xv = xb[:].rearrange("p (c t) -> p c t", c=NCH)
            lx = "lx%d" % (s % 2)
            nld = pb.dma("sync", xv[:, :, :], Xv[:, :, T0:T0 + ST], inc=lx,
                         waits=[("dve", mark["xn_dve"])])
            na1 = pb.em("scalar", lambda e, xb=xb: e.activation(out=xbf[:], in_=xb[:], func=AF.Copy),
                        waits=[(lx, nld), ("pe", pb.n("pe"))], inc="act")
            na2 = pb.em("scalar", lambda e, xb=xb: e.activation(out=xsq[:], in_=xb[:], func=AF.Square),
                        inc="act")
            for ch in range(NCH):
                pb.em("tensor", lambda e, ch=ch: e.matmul(
                    p2[0:1, 0:ST], inv1024[:], xbv[:, ch, :], start=(ch == 0), stop=(ch == NCH - 1)),
                    waits=([("act", na1), ("dve", pb.n("dve"))] if ch == 0 else []), inc="pe")
            for ch in range(NCH):
                npe_st = pb.em("tensor", lambda e, ch=ch: e.matmul(
                    p3[0:1, 0:ST], inv1024[:], xqv[:, ch, :], start=(ch == 0), stop=(ch == NCH - 1)),
                    waits=([("act", na2)] if ch == 0 else []), inc="pe")
            m_row = rows[0:1, 0:ST]
            r_row = rows[0:1, ST:2 * ST]
            v_row = rows[0:1, 2 * ST:3 * ST]
            t_row = rows[0:1, 3 * ST:4 * ST]
            na_m = pb.em("scalar", lambda e: e.activation(out=m_row, in_=p2[0:1, 0:ST], func=AF.Copy),
                         waits=[("pe", npe_st)], inc="act")
            pb.em("vector", lambda e: e.tensor_tensor(out=t_row, in0=m_row, in1=m_row, op=ALU.mult),
                  waits=[("act", na_m)], inc="dve")
            pb.drain("vector")
            nd_var = pb.em("vector", lambda e: e.tensor_tensor(out=v_row, in0=p3[0:1, 0:ST], in1=t_row, op=ALU.subtract),
                           inc="dve")
            pb.em("scalar", lambda e: e.activation(out=t_row, in_=v_row, func=AF.Ln, bias=cns[0:1, 2:3]),
                  waits=[("ldc", ld_ones), ("dve", nd_var)], inc="act")
            pb.drain("scalar")
            na_rows = pb.em("scalar", lambda e: e.activation(out=r_row, in_=t_row, func=AF.Exp, scale=-0.5),
                            inc="act")
            pb.em("tensor", lambda e: e.matmul(p2[:, 0:ST], ones1f[:], m_row, start=True, stop=True),
                  waits=[("act", na_rows), ("ldc", ld_ones)], inc="pe")
            npe_bc = pb.em("tensor", lambda e: e.matmul(p3[:, 0:ST], ones1f[:], r_row, start=True, stop=True),
                           inc="pe")
            for ch in range(NCH):
                pb.em("vector", lambda e, ch=ch, xv=xv: e.tensor_tensor(
                    out=xnfv[:, ch, :], in0=xv[:, ch, :], in1=p2[:, 0:ST], op=ALU.subtract),
                    waits=([("pe", npe_bc)] if ch == 0 else []), inc="dve")
            pb.drain("vector")
            xn_ch = {}
            for ch in range(NCH):
                nd_xn = pb.em("vector", lambda e, ch=ch: e.tensor_tensor(
                    out=xnbv[:, ch, :], in0=xnfv[:, ch, :], in1=p3[:, 0:ST], op=ALU.mult),
                    waits=([("sxn", mark["xn_st"]), ("pe", pb.n("pe"))] if ch == 0 else []), inc="dve")
                xn_ch[ch] = nd_xn
            mark["xn_dve"] = nd_xn
            mark["xn_st"] = pb.dma("gpsimd", XNv[:, :, T0:T0 + ST], xnbv[:, :, :], inc="sxn",
                                   waits=[("dve", nd_xn)])
            return xn_ch

        xn_ch = front0(0)
        for s in range(NST):
            T0 = ST * s
            if STAGE < 2:
                continue
            kevac = {}
            for mc in range(NCH):
                bank = [p0, p1][mc % 2]
                if mc in (0, 1):
                    w0 = [("wcv", cv_wk), ("act", pb.n("act"))]
                else:
                    w0 = [("dve", kevac[mc - 2])]
                for kc in range(NCH):
                    wk_ = (w0 if kc == 0 else [])
                    if mc in (0, 1):
                        wk_ = wk_ + [("dve", xn_ch[kc])]
                    npe = pb.em("tensor", lambda e, kc=kc, mc=mc, bank=bank: e.matmul(
                        bank[:, 0:ST], wkv[:, kc, 128 * mc:128 * mc + 128], xnbv[:, kc, :],
                        start=(kc == 0), stop=(kc == NCH - 1)),
                        waits=wk_, inc="pe")
                na = pb.em("scalar", lambda e, bank=bank: e.activation(out=efp[:], in_=bank[:, 0:ST], func=AF.Exp),
                           waits=[("pe", npe), ("dve", kevac.get(mc - 1, 0))], inc="act")
                na = pb.em("scalar", lambda e, bank=bank: e.activation(out=em1[:], in_=bank[:, 0:ST], func=AF.Relu),
                           inc="act")
                krelu = krelu if mc else {}
                krelu[mc] = na
                kevac[mc] = pb.em("vector", lambda e, mc=mc: e.scalar_tensor_tensor(
                    out=kfv[:, mc, :], in0=efp[:], scalar=1.0, in1=em1[:],
                    op0=ALU.min, op1=ALU.add),
                    waits=[("act", na), ("skf", mark["kfm_st"]) if mc == 0 else ("act", 0)], inc="dve")
            mark["kfm_st"] = pb.dma("gpsimd", KFMv[:, :, T0:T0 + ST], kfv[:, :, :], inc="skf",
                                    waits=[("dve", kevac[NCH - 1])])
            if STAGE < 3:
                continue
            for c in range(CH_T):
                for j in range(NCH):
                    trw = [("dve", kevac[j])] if c == 0 else []
                    if j == 0:
                        trw = trw + [("act", mark["ktm_act"])]
                    npe = pb.em("tensor", lambda e, c=c, j=j: e.transpose(
                        p4[:, 128 * j:128 * j + 128],
                        kfv[:, j, 128 * c:128 * c + 128],
                        i64b[:]),
                        waits=trw, inc="pe")
                mark["ktm_act"] = pb.em("scalar", lambda e, c=c: e.activation(
                    out=ktv[:, c, :], in_=p4[:, 0:1024], func=AF.Copy),
                    waits=[("pe", npe)], inc="act")
            if STAGE < 4:
                continue
            vevac = {}
            for c in range(CH_T):
                for half in range(2):
                    bank = [p0, p1][half]
                    if c == 0:
                        w0 = [("wcv", cv_wv), ("dve", kevac[NCH - 1]), ("act", pb.n("act"))]
                    else:
                        w0 = [("act", vevac[(c - 1, half)])]
                    for kc in range(NCH):
                        npe = pb.em("tensor", lambda e, kc=kc, c=c, half=half, bank=bank: e.matmul(
                            bank[:, 0:512],
                            xnbv[:, kc, 128 * c:128 * c + 128],
                            wvv[:, kc, 512 * half:512 * half + 512],
                            start=(kc == 0), stop=(kc == NCH - 1)),
                            waits=(w0 if kc == 0 else []), inc="pe")
                    vevac[(c, half)] = pb.em("scalar", lambda e, c=c, half=half, bank=bank: e.activation(
                        out=vtv[:, c, :].rearrange("p (h o) -> p h o", h=16)[:, 8 * half:8 * half + 8, 0:64],
                        in_=bank[:, 0:512].rearrange("p (h o) -> p h o", h=8),
                        func=AF.Copy),
                        waits=[("pe", npe)] + ([("svt", mark["vtm_st"]), ("dve", n_ones)] if (c == 0 and half == 0) else []),
                        inc="act")
            mark["vtm_st"] = pb.dma("gpsimd", VTMv[:, CH_T * s:CH_T * (s + 1), :], vtv[:, :, :],
                                    inc="svt", waits=[("act", pb.n("act"))])
            if STAGE < 5:
                continue
            for c in range(CH_T):
                cg = CH_T * s + c
                n_sn = 0
                for q in range(2):
                    n_sn = pb.dma(
                        "gpsimd", SNPr[cg, q],
                        statef[64 * q:64 * q + 64, :].rearrange("p (g k e) -> p g k e", g=2, k=4),
                        inc="ssn", waits=([("dve", mark["state_dve"])] if q == 0 else []))
                for g in range(2):
                    pb.em("tensor", lambda e, g=g: e.matmul(
                        pst[g][:, 0:512], zc[0:1, 0:128], zc[0:1, 128:640],
                        start=True, stop=False),
                        waits=([("dve", mark["state_dve"]), ("act", pb.n("act"))] if g == 0 else []),
                        inc="pe")
                for h in range(H):
                    g, prow, pcol = state_slot(h)
                    pb.em("tensor", lambda e, c=c, h=h, g=g, prow=prow, pcol=pcol: e.matmul(
                        pst[g][prow:prow + 64, pcol:pcol + 65],
                        ktv[:, c, 64 * h:64 * h + 64],
                        vtv[:, c, 65 * h:65 * h + 65],
                        start=False, stop=False), inc="pe")
                npe_cl = 0
                for g in range(2):
                    npe_cl = pb.em("tensor", lambda e, g=g: e.matmul(
                        pst[g][:, 0:512], zc[0:1, 0:128], zc[0:1, 128:640],
                        start=False, stop=True), inc="pe")
                for g in range(2):
                    mark["state_dve"] = pb.em("vector", lambda e, g=g: e.tensor_tensor(
                        out=statef[:, 260 * g:260 * g + 260], in0=pst[g][:, 0:260],
                        in1=statef[:, 260 * g:260 * g + 260], op=ALU.add),
                        waits=([("pe", npe_cl), ("ssn", n_sn)] if g == 0 else []), inc="dve")
                pb.drain("vector")
            if s + 1 < NST:
                xn_ch = front0(s + 1)
            pass


        for q in range(2):
            pb.dma("gpsimd", SZr[q],
                   statef[64 * q:64 * q + 64, :].rearrange("p (g k e) -> p g k e", g=2, k=4),
                   inc="ssz", waits=([("dve", mark["state_dve"])] if q == 0 else []))
        if io.get("_mid_hook") is not None:
            io["_mid_hook"](pb)
        xnb2vs = [xnbv, kfv]
        if STAGE >= 6:
          for s in range(NST):
            T0 = ST * s
            xnb2v = xnb2vs[s % 2]
            if s < 2:
                wlb = [("pe", pb.n("pe")), ("sxn", pb.n("sxn")), ("skf", pb.n("skf"))]
            else:
                wlb = [("pe", mark["ffnb_pe_%d" % (s % 2)])]
            nld_b = pb.dma("sync", xnb2v[:, :, :], XNv[:, :, T0:T0 + ST],
                           inc="lxb%d" % (s % 2), waits=wlb)
            lxb_sem = "lxb%d" % (s % 2)
            for kc in range(NCH):
                npe = pb.em("tensor", lambda e, kc=kc, xnb2v=xnb2v: e.matmul(
                    p2[0:64, 0:ST], winv[:, kc, :], xnb2v[:, kc, :],
                    start=(kc == 0), stop=(kc == NCH - 1)),
                    waits=([("wcv", cv_win), ("dve", pb.n("dve")), (lxb_sem, nld_b)] if kc == 0 else []), inc="pe")
            na_s = pb.em("vector", lambda e: e.tensor_scalar(out=sfb[:], in0=p2[0:64, 0:ST], scalar1=binc[:], scalar2=None, op0=ALU.add),
                         waits=[("pe", npe), ("ldc", ld_small)], inc="dve")
            hevac = {}
            for ch in range(NCH):
                bank = [p0, p1][ch % 2]
                npe = pb.em("tensor", lambda e, ch=ch, bank=bank: e.matmul(
                    bank[:, 0:ST], eb[0:64, 128 * ch:128 * ch + 128], sfb[:],
                    start=True, stop=True),
                    waits=([("dve", na_s), ("wcv", cv_eb), ("act", pb.n("act"))] if ch < 2 else [("act", hevac[ch - 2])]), inc="pe")
                hevac[ch] = pb.em("scalar", lambda e, ch=ch, bank=bank: e.activation(
                    out=hfv[:, ch, :], in_=bank[:, 0:ST], func=AF.Relu,
                    scale=w1f[:, ch:ch + 1], bias=b1f[:, ch:ch + 1]),
                    waits=[("pe", npe)], inc="act")
            for ch in range(NCH):
                npe = pb.em("tensor", lambda e, ch=ch: e.matmul(
                    p2[0:64, 0:ST], w2bdv[:, ch, :], hfv[:, ch, :],
                    start=(ch == 0), stop=(ch == NCH - 1)),
                    waits=([("act", hevac[NCH - 1]), ("wcv", cv_bd)] if ch == 0 else []), inc="pe")
            na_g = pb.em("vector", lambda e: e.tensor_scalar(out=gfb[:], in0=p2[0:64, 0:ST], scalar1=b2c[:], scalar2=None, op0=ALU.add),
                         waits=[("pe", npe)], inc="dve")
            gevac = {}
            for ch in range(NCH):
                bank = [p0, p1][ch % 2]
                npe = pb.em("tensor", lambda e, ch=ch, bank=bank: e.matmul(
                    bank[:, 0:ST], wout[0:64, 128 * ch:128 * ch + 128], gfb[:],
                    start=True, stop=True),
                    waits=([("dve", na_g), ("wcv", cv_wo)] if ch < 2 else [("act", gevac[ch - 2])]), inc="pe")
                gevac[ch] = pb.em("scalar", lambda e, ch=ch, bank=bank: e.activation(
                    out=g0v[:, ch, :], in_=bank[:, 0:ST], func=AF.Silu, bias=boutc[:, ch:ch + 1]),
                    waits=[("pe", npe)] + ([("sg0", mark["g0_st"])] if ch == 0 else []), inc="act")
            mark["g0_st"] = pb.dma("gpsimd", G0v[:, :, T0:T0 + ST], g0v[:, :, :], inc="sg0",
                                   waits=[("act", gevac[NCH - 1])])
            mark["ffnb_pe_%d" % (s % 2)] = pb.n("pe")

        pb.em("sync", lambda e: e.nop(), waits=[(snm, pb.n(snm)) for snm in
              ("sxn", "skf", "svt", "sg0", "ssz", "ssn") if pb.n(snm) > 0])


def build_pass0(ST=512, STAGE=99, NSTLIM=99):
    nc = bass.Bass("TRN2", target_bir_lowering=False)
    io = {k: nc.dram_tensor(k, *P0_SHAPES[k], kind="ExternalInput").ap() for k in P0_IN}
    io.update({k: nc.dram_tensor(k, *P0_SHAPES[k], kind="ExternalOutput").ap() for k in P0_OUT})
    pb = PB(nc)
    with contextlib.ExitStack() as ctx:
        emit_pass0(nc, pb, ctx, io, ST=ST, STAGE=STAGE, NSTLIM=NSTLIM)
        pb.build(ctx)
    return nc



P1_IN = ["X", "XN", "KFM", "VTM", "G0", "SNP", "SZG", "MK", "WQ", "WO", "WG",
         "WV1", "WD", "BGC", "BVC", "BDC", "MASKUT", "I128", "CONSTC", "ONESR"]
P1_OUT = ["X1", "XN2", "S1", "PP", "AH"]
P1_SHAPES = {
    "X": ([D, TCORE], F32), "XN": ([128, NCH * TCORE], BF16),
    "KFM": ([128, NCH * TCORE], BF16), "VTM": ([128, (TCORE // 128) * 1040], BF16),
    "G0": ([128, NCH * TCORE], BF16), "SNP": ([64, (TCORE // 128) * 1040], F32),
    "SZG": ([256, 1040], F32), "MK": ([128, 16], F32),
    "WQ": ([D, D], F32), "WO": ([D, D], F32), "WG": ([D, D], F32),
    "WV1": ([D, D], F32), "WD": ([D, D], F32),
    "BGC": ([128, 8], F32), "BVC": ([128, 8], F32), "BDC": ([128, 8], F32),
    "MASKUT": ([128, 128], F32), "I128": ([128, 128], F32),
    "CONSTC": ([128, 4], F32), "ONESR": ([1, 128], F32),
    "X1": ([128, NCH * TCORE], F32), "XN2": ([128, NCH * TCORE], BF16),
    "S1": ([128, NCH * TCORE], F32), "PP": ([128, NCH * TCORE], F32),
    "AH": ([128, 16], F32),
}


def emit_pass1(nc, pb, ctx, io, ST=256, STAGE=99, NSTLIM=99):
    NST = min(TCORE // ST, NSTLIM)
    CH_T = ST // 128
    NCHK = TCORE // 128
    (X, XN, KFM, VTM, G0, SNP, SZG, MK, WQ, WO, WG, WV1, WD, BGC, BVC, BDC,
     MASKUT, I128, CONSTC, ONESR) = [io[k] for k in P1_IN]
    X1, XN2, S1, PP, AH = [io[k] for k in P1_OUT]
    WCV = "pcv" if io.get("_preW") is not None else "wcv"

    Xv = X.rearrange("(c p) t -> p c t", p=128)
    XNv = XN.rearrange("p (c t) -> p c t", c=NCH)
    KFMh = KFM.rearrange("(q p) (c t) -> p c q t", q=2, p=64, c=NCH)
    VTMv = VTM.rearrange("p (c o) -> p c o", c=NCHK)
    SNPv = SNP.rearrange("p (c o) -> p c o", c=NCHK)
    G0v = G0.rearrange("p (c t) -> p c t", c=NCH)
    X1v = X1.rearrange("p (c t) -> p c t", c=NCH)
    XN2v = XN2.rearrange("p (c t) -> p c t", c=NCH)
    S1v = S1.rearrange("p (c t) -> p c t", c=NCH)
    PPv = PP.rearrange("p (c t) -> p c t", c=NCH)

    if True:
        sb = lambda nm, shp, dt=F32: ctx.enter_context(nc.sbuf_tensor(nm + "_1", shp, dt))
        ps = lambda nm, shp, dt=F32: ctx.enter_context(nc.psum_tensor(nm + "_1", shp, dt))

        stage = None if io.get("_preW") is not None else sb("stage", [128, 512])
        xin = sb("xin", [128, NCH * ST])
        xnb = sb("xnb", [128, NCH * ST], BF16)
        kfma = sb("kfma", [64, H * 128], BF16)
        kfmb = sb("kfmb", [64, H * 128], BF16)
        vtm = sb("vtm", [128, CH_T * 1040], BF16)
        vtm2 = sb("vtm2", [128, CH_T * 1040], BF16)
        g0b = sb("g0b", [128, NCH * ST], BF16)
        qfma = sb("qfma", [64, H * ST], BF16)
        szg = sb("szg", [64, 1040])
        sprev = sb("sprev", [64, 1040])
        snp = sprev
        snapb = sb("snapb", [64, 1040], BF16)
        sprevb = sb("sprevb", [64, 1040], BF16)
        smbf = sb("smbf", [128, 2 * 256], BF16)
        ytm = sb("ytm", [128, 1024], BF16)
        efp = ytm[0:64, 0:2 * ST]
        em1 = ytm[0:64, 2 * ST:4 * ST]
        yfm = sb("yfm", [128, NCH * ST], BF16)
        denr = sb("denr", [128, 16])
        x1f = sb("x1f", [128, NCH * ST])
        xn2b = sb("xn2b", [128, NCH * ST], BF16)
        x1b = xn2b
        x1q = yfm
        tg = sb("tg", [128, NCH * ST])
        tv = sb("tv", [128, NCH * ST])
        td = sb("td", [128, NCH * ST])
        gbuf = sb("gbuf", [128, NCH * ST])
        s1b = sb("s1b", [128, NCH * ST])
        ppb = sb("ppb", [128, NCH * ST])
        hprev = sb("hprev", [128, 8])
        pprev = sb("pprev", [128, 8])
        zer = sb("zer", [128, ST])
        zcol = sb("zcol", [128, 1])
        rows = szg
        cns = sb("cns", [128, 4])
        inv1024b = sb("inv1024b", [128, 1], BF16)
        ones1f = sb("ones1f", [1, 128])
        i128f = sb("i128f", [128, 128])
        i128b = sb("i128b", [128, 128], BF16)
        maskut = sb("maskut", [128, 256])
        bgc = sb("bgc", [128, 8])
        bvc = sb("bvc", [128, 8])
        bdc = sb("bdc", [128, 8])
        bgh = sb("bgh", [128, 8])
        bdh = sb("bdh", [128, 8])
        mk = sb("mk", [128, 16])
        ahsb = sb("ahsb", [128, 16])

        p0 = ps("p0", [128, 512])
        p1 = ps("p1", [128, 512])
        p2 = ps("p2", [128, 1024], BF16)
        pya = ps("pya", [128, 512])
        pyb = ps("pyb", [128, 512])
        pyc = ps("pyc", [128, 512])
        pyd = ps("pyd", [128, 512])

        xv = xin[:].rearrange("p (c t) -> p c t", c=NCH)
        xnv = xnb[:].rearrange("p (c t) -> p c t", c=NCH)
        kfvs = [kfma[:].rearrange("p (h t) -> p h t", h=H),
                kfmb[:].rearrange("p (h t) -> p h t", h=H)]
        kfbufs = [kfma, kfmb]
        qfv = qfma[:].rearrange("p (h t) -> p h t", h=H)
        vtvs = [vtm[:].rearrange("p (c o) -> p c o", c=CH_T),
                vtm2[:].rearrange("p (c o) -> p c o", c=CH_T)]
        g0v = g0b[:].rearrange("p (c t) -> p c t", c=NCH)
        yfv = yfm[:].rearrange("p (c t) -> p c t", c=NCH)
        x1vv = x1f[:].rearrange("p (c t) -> p c t", c=NCH)
        x1bv = x1b[:].rearrange("p (c t) -> p c t", c=NCH)
        x1qv = x1q[:].rearrange("p (c t) -> p c t", c=NCH)
        xn2v = xn2b[:].rearrange("p (c t) -> p c t", c=NCH)
        tgv = tg[:].rearrange("p (c t) -> p c t", c=NCH)
        tvv = tv[:].rearrange("p (c t) -> p c t", c=NCH)
        tdv = td[:].rearrange("p (c t) -> p c t", c=NCH)
        gbv = gbuf[:].rearrange("p (c t) -> p c t", c=NCH)
        s1vv = s1b[:].rearrange("p (c t) -> p c t", c=NCH)
        ppvv = ppb[:].rearrange("p (c t) -> p c t", c=NCH)

        pb.dma("sync", cns[:], CONSTC, inc="ldc")
        pb.dma("sync", i128f[:], I128, inc="ldc")
        pb.dma("sync", ones1f[:], ONESR, inc="ldc")
        pb.dma("sync", maskut[:, 0:128], MASKUT, inc="ldc")
        pb.dma("sync", maskut[:, 128:256], MASKUT, inc="ldc")
        for dst, srcap in [(bgc, BGC), (bvc, BVC), (bdc, BDC), (mk, MK)]:
            pb.dma("sync", dst[:], srcap, inc="ldc")
        LDC = pb.n("ldc")
        pb.em("scalar", lambda e: e.activation(out=i128b[:], in_=i128f[:], func=AF.Copy),
              waits=[("ldc", LDC)], inc="act")
        pb.em("scalar", lambda e: e.activation(out=inv1024b[:], in_=cns[:, 0:1], func=AF.Copy),
              inc="act")
        nd = pb.em("vector", lambda e: e.tensor_scalar(out=bgh[:], in0=bgc[:], scalar1=0.5,
                   scalar2=None, op0=ALU.mult), waits=[("ldc", LDC)], inc="dve")
        nd = pb.em("vector", lambda e: e.tensor_scalar(out=bdh[:], in0=bdc[:], scalar1=0.5,
                   scalar2=None, op0=ALU.mult), inc="dve")
        nd = 0
        for r in range(4):
            nl = pb.dma("sync", szg[:], SZG[64 * r:64 * (r + 1), :], inc="lsz",
                        waits=[("dve", nd)] + (list(io.get("_szg_waits", [])) if r == 0 else []))
            if r == 0:
                nd = pb.em("vector", lambda e: e.tensor_scalar(
                    out=sprev[:], in0=szg[:], scalar1=mk[0:64, 0:1], scalar2=None, op0=ALU.mult),
                    waits=[("lsz", nl), ("ldc", LDC)], inc="dve")
            else:
                nd = pb.em("vector", lambda e, r=r: e.scalar_tensor_tensor(
                    out=sprev[:], in0=szg[:], scalar=mk[0:64, r:r + 1],
                    in1=sprev[:], op0=ALU.mult, op1=ALU.add),
                    waits=[("lsz", nl)], inc="dve")
            pb.drain("vector")
        na_spv = pb.em("scalar", lambda e: e.activation(out=sprevb[:], in_=sprev[:], func=AF.Copy),
                       waits=[("dve", nd)], inc="act")
        nd = pb.em("vector", lambda e: e.memset(zer[:], 0.0), inc="dve")
        nd = pb.em("vector", lambda e: e.memset(zcol[:], 0.0), inc="dve")
        nd = pb.em("vector", lambda e: e.memset(pprev[:], 1.0), inc="dve")
        nd_pro = pb.em("vector", lambda e: e.memset(hprev[:], 0.0), inc="dve")
        pb.drain("vector")

        if io.get("_preW") is not None:
            (wq, cv_wq), (wo, cv_wo), (wg, cv_wg), (wv1, cv_wv1), (wd, cv_wd) = io["_preW"]
        else:
            wq, cv_wq = load_weight_bf16(pb, ctx, "wq", WQ, (128, 8, D), stage)
            wo, cv_wo = load_weight_bf16(pb, ctx, "wo", WO, (128, 8, D), stage)
            wg, cv_wg = load_weight_bf16(pb, ctx, "wg", WG, (128, 8, D), stage)
            wv1, cv_wv1 = load_weight_bf16(pb, ctx, "wv1", WV1, (128, 8, D), stage)
            wd, cv_wd = load_weight_bf16(pb, ctx, "wd", WD, (128, 8, D), stage)
        wqv = wq[:].rearrange("p (k o) -> p k o", k=8)
        wov = wo[:].rearrange("p (k o) -> p k o", k=8)
        wgv = wg[:].rearrange("p (k o) -> p k o", k=8)
        wv1v = wv1[:].rearrange("p (k o) -> p k o", k=8)
        wdv = wd[:].rearrange("p (k o) -> p k o", k=8)

        mark = {"x_pool": 0, "xn_pe": 0, "g0_dve": 0,
                "m1pe_0": 0, "m1pe_1": 0, "vt_pe_0": 0, "vt_pe_1": 0,
                "snp_act": na_spv, "m3_pe": 0, "sx1": 0, "sxn2": 0, "ss1": 0,
                "spp": 0, "scan_dve": nd_pro, "hp_dve": nd_pro}

        def emit_front(s):
            T0 = ST * s
            T0 = ST * s
            n_xn = pb.dma("sync", xnv[:, :, :], XNv[:, :, T0:T0 + ST], inc="lxn",
                          waits=[("pe", mark["xn_pe"])])
            n_vt = pb.dma("sync", vtvs[s % 2][:, :, :], VTMv[:, CH_T * s:CH_T * (s + 1), :],
                          inc="lvt", waits=[("pe", mark["vt_pe_%d" % (s % 2)])])
            qevac = {}
            for pj in range(8):
                bank = [p0, p1][pj % 2]
                if pj in (0, 1):
                    w0 = [(WCV, cv_wq), ("lxn", n_xn), ("dve", pb.n("dve")),
                          ("act", pb.n("act"))]
                else:
                    w0 = [("pool", qevac[pj - 2])]
                for hh in range(2):
                    h_ = 2 * pj + hh
                    for kc in range(NCH):
                        npe = pb.em("tensor", lambda e, kc=kc, h_=h_, hh=hh, bank=bank: e.matmul(
                            bank[0:64, ST * hh:ST * hh + ST],
                            wqv[:, kc, 64 * h_:64 * h_ + 64], xnv[:, kc, :],
                            start=(kc == 0), stop=(kc == NCH - 1)),
                            waits=(w0 if (hh == 0 and kc == 0) else []), inc="pe")
                na = pb.em("scalar", lambda e, bank=bank: e.activation(
                    out=efp, in_=bank[0:64, 0:2 * ST], func=AF.Exp),
                    waits=[("pe", npe), ("pool", qevac.get(pj - 1, 0))], inc="act")
                na = pb.em("scalar", lambda e, bank=bank: e.activation(
                    out=em1, in_=bank[0:64, 0:2 * ST], func=AF.Relu),
                    inc="act")
                qrelu = qrelu if pj else {}
                qrelu[pj] = na
                pb.em("gpsimd", lambda e, pj=pj: e.tensor_scalar(
                    out=qfv[:, 2 * pj:2 * pj + 2, :], in0=efp.rearrange(
                        "p (h t) -> p h t", h=2), scalar1=1.0, scalar2=None, op0=ALU.min),
                    waits=[("act", na)] + ([("pe", mark["m3_pe"])] if pj == 0 else []), inc="pool")
                pb.drain("gpsimd")
                qevac[pj] = pb.em("gpsimd", lambda e, pj=pj: e.tensor_tensor(
                    out=qfv[:, 2 * pj:2 * pj + 2, :], in0=qfv[:, 2 * pj:2 * pj + 2, :],
                    in1=em1.rearrange("p (h t) -> p h t", h=2), op=ALU.add),
                    inc="pool")
            mark["xn_pe"] = pb.n("pe")
            return dict(n_xn=n_xn, n_vt=n_vt, qevac=qevac, qrelu=qrelu)

        def emit_scans(s, T0, np3):
            pb.drain("vector")
            nds = pb.n("dve")
            for ch in range(NCH):
                init = zcol[:, 0:1] if s == 0 else hprev[:, ch:ch + 1]
                nds = pb.em("vector", lambda e, ch=ch, init=init: e.tensor_tensor_scan(
                    out=s1vv[:, ch, :], data0=tgv[:, ch, :], data1=tvv[:, ch, :],
                    initial=init, op0=ALU.mult, op1=ALU.add),
                    waits=([("pool", np3), ("ss1", mark["ss1"]), ("dve", mark["hp_dve"])] if ch == 0 else []),
                    inc="dve")
            for ch in range(NCH):
                init = cns[:, 1:2] if s == 0 else pprev[:, ch:ch + 1]
                nds = pb.em("vector", lambda e, ch=ch, init=init: e.tensor_tensor_scan(
                    out=ppvv[:, ch, :], data0=tgv[:, ch, :], data1=zer[:, 0:ST],
                    initial=init, op0=ALU.mult, op1=ALU.add),
                    waits=([("spp", mark["spp"])] if ch == 0 else []), inc="dve")
            pb.drain("vector")
            nd_h = pb.em("vector", lambda e: e.tensor_copy(
                hprev[:], s1vv[:, :, ST - 1:ST].rearrange("p c t -> p (c t)")), inc="dve")
            nd_p = pb.em("vector", lambda e: e.tensor_copy(
                pprev[:], ppvv[:, :, ST - 1:ST].rearrange("p c t -> p (c t)")), inc="dve")
            pb.drain("vector")
            mark["hp_dve"] = nd_p
            mark["scan_dve"] = nd_p
            mark["ss1"] = pb.dma("gpsimd", S1v[:, :, T0:T0 + ST], s1vv[:, :, :], inc="ss1",
                                 waits=[("dve", nd_p)])
            mark["spp"] = pb.dma("gpsimd", PPv[:, :, T0:T0 + ST], ppvv[:, :, :], inc="spp",
                                 waits=[("dve", nd_p)])
            if s == NST - 1:
                nd_ah = pb.em("vector", lambda e: e.tensor_copy(ahsb[:, 0:8], pprev[:]),
                              waits=[("dve", nd_p)], inc="dve")
                nd_ah = pb.em("vector", lambda e: e.tensor_copy(ahsb[:, 8:16], hprev[:]), inc="dve")
                pb.dma("gpsimd", AH, ahsb[:], inc="sah", waits=[("dve", nd_ah)])

        scan_args = None
        n_x = pb.dma("sync", xv[:, :, :], Xv[:, :, 0:ST], inc="lx",
                     waits=[("pool", 0)])
        n_g0 = pb.dma("sync", g0v[:, :, :], G0v[:, :, 0:ST], inc="lg0",
                      waits=[("dve", 0)])
        mark["kf_ld"] = ("lkf0", pb.dma("sync",
                               kfma[:].rearrange("p (c q t) -> p c q t", c=NCH, q=2),
                               KFMh[:, :, :, 0:128], inc="lkf0", waits=[]))
        fr = emit_front(0)
        for s in range(NST):
            T0 = ST * s
            n_xn, n_vt = fr["n_xn"], fr["n_vt"]
            qevac, qrelu = fr["qevac"], fr["qrelu"]
            if STAGE < 2:
                continue
            vtv_c = vtvs[s % 2]
            for c in range(CH_T):
                cg = CH_T * s + c
                T0c = T0 + 128 * c
                n_kf = mark["kf_ld"]
                kfv_c = kfvs[cg % 2]
                n_sn = pb.dma("sync", snp[:], SNPv[:, cg, :], inc="lsn",
                              waits=[("act", mark["snp_act"])])
                na_sn = pb.em("scalar", lambda e: e.activation(
                    out=snapb[:], in_=snp[:], func=AF.Copy),
                    waits=[("lsn", n_sn), ("pe", mark["m3_pe"])], inc="act")
                mark["snp_act"] = na_sn
                mevac = {}

                def emit_m1_pair(pr):
                    scb = [p0, p1][pr % 2]
                    w0 = [("pool", qevac[pr])]
                    if pr - 2 in mevac:
                        w0 += [("dve", mevac[pr - 2])]
                    if pr in (0, 1) and c == 0:
                        w0 += [("act", qrelu[6 + pr])]
                    if pr == 0:
                        w0 += [n_kf]
                    npe1 = 0
                    for hh in range(2):
                        j = 2 * pr + hh
                        npe1 = pb.em("tensor", lambda e, j=j, c=c, scb=scb, hh=hh, kfv=kfv_c: e.matmul(
                            scb[:, 128 * hh:128 * hh + 128], kfv[:, j, :],
                            qfv[:, j, 128 * c:128 * c + 128], start=True, stop=True),
                            waits=(w0 if hh == 0 else []), inc="pe")
                    mevac[pr] = pb.em("vector", lambda e, pr=pr, scb=scb: e.tensor_tensor(
                        out=smbf[:, 256 * (pr % 2):256 * (pr % 2) + 256], in0=scb[:, 0:256],
                        in1=maskut[:], op=ALU.mult),
                        waits=[("pe", npe1), ("ldc", LDC)], inc="dve")

                emit_m1_pair(0)
                for j in range(H):
                    rr, jj = j // 8, j % 8
                    ypair = [(pya, pyb), (pyc, pyd)][rr % 2]
                    yb = ypair[0] if jj < 4 else ypair[1]
                    ycol = 128 * (jj % 4)
                    if j % 2 == 0 and j // 2 + 1 < 8:
                        emit_m1_pair(j // 2 + 1)
                        if j == 12:
                            mark["m1pe_%d" % (cg % 2)] = pb.n("pe")
                            if cg + 1 < NST * CH_T:
                                kp = (cg + 1) % 2
                                mark["kf_ld"] = ("lkf%d" % kp, pb.dma(
                                    "sync",
                                    kfbufs[kp][:].rearrange(
                                        "p (c q t) -> p c q t", c=NCH, q=2),
                                    KFMh[:, :, :, 128 * (cg + 1):128 * (cg + 1) + 128],
                                    inc="lkf%d" % kp,
                                    waits=[("pe", mark["m1pe_%d" % kp])]))
                    w2 = [("dve", mevac[j // 2]), ("lvt", n_vt)]
                    if jj in (0, 4):
                        w2 += [("dve", mark.get("yevac_dve_%d" % (rr % 2), 0)),
                               ("act", mark.get("yevac_act_%d" % (rr % 2), 0))]
                    npe2 = pb.em("tensor", lambda e, j=j, c=c, yb=yb, ycol=ycol, vtv=vtv_c: e.matmul(
                        yb[:, ycol:ycol + 65],
                        smbf[:, 256 * ((j // 2) % 2) + 128 * (j % 2):
                             256 * ((j // 2) % 2) + 128 * (j % 2) + 128],
                        vtv[:, c, 65 * j:65 * j + 65], start=True, stop=False),
                        waits=w2, inc="pe")
                    pb.em("tensor", lambda e, j=j, c=c, yb=yb, ycol=ycol: e.matmul(
                        yb[:, ycol:ycol + 65], qfv[:, j, 128 * c:128 * c + 128],
                        snapb[:, 65 * j:65 * j + 65], start=False, stop=False),
                        waits=([("act", na_sn)] if j == 0 else []), inc="pe")
                    npe3 = pb.em("tensor", lambda e, j=j, c=c, yb=yb, ycol=ycol: e.matmul(
                        yb[:, ycol:ycol + 65], qfv[:, j, 128 * c:128 * c + 128],
                        sprevb[:, 65 * j:65 * j + 65], start=False, stop=True),
                        inc="pe")
                    if jj == 7:
                        ya_, yb_ = ypair
                        nda = pb.em("vector", lambda e, ya_=ya_, rr=rr: e.tensor_scalar(
                            out=denr[:, 8 * rr:8 * rr + 4],
                            in0=ya_[:].rearrange("p (k t) -> p k t", k=4)[:, :, 64:65],
                            scalar1=ATT_EPS, scalar2=None, op0=ALU.add),
                            waits=[("pe", npe3)], inc="dve")
                        nda = pb.em("vector", lambda e, yb_=yb_, rr=rr: e.tensor_scalar(
                            out=denr[:, 8 * rr + 4:8 * rr + 8],
                            in0=yb_[:].rearrange("p (k t) -> p k t", k=4)[:, :, 64:65],
                            scalar1=ATT_EPS, scalar2=None, op0=ALU.add), inc="dve")
                        pb.drain("vector")
                        nda = pb.em("vector", lambda e, rr=rr: e.reciprocal(
                            out=denr[:, 8 * rr:8 * rr + 8], in_=denr[:, 8 * rr:8 * rr + 8]), inc="dve")
                        pb.drain("vector")
                        nact_y = 0
                        for j2 in range(8):
                            h2 = 8 * rr + j2
                            yb2 = ypair[0] if j2 < 4 else ypair[1]
                            yc2 = 128 * (j2 % 4)
                            if j2 % 2 == 0:
                                wy = ([("pool", qevac[7])] if (rr == 0 and c == 0 and j2 == 0)
                                      else [])
                                nda = pb.em("vector", lambda e, h2=h2, yb2=yb2, yc2=yc2: e.tensor_scalar(
                                    out=ytm[:, 64 * h2:64 * h2 + 64], in0=yb2[:, yc2:yc2 + 64],
                                    scalar1=denr[:, h2:h2 + 1], scalar2=None, op0=ALU.mult),
                                    waits=wy, inc="dve")
                            else:
                                nact_y = pb.em("scalar", lambda e, h2=h2, yb2=yb2, yc2=yc2: e.activation(
                                    out=ytm[:, 64 * h2:64 * h2 + 64], in_=yb2[:, yc2:yc2 + 64],
                                    func=AF.Copy, scale=denr[:, h2:h2 + 1]),
                                    waits=([("dve", nda)] if j2 == 1 else []), inc="act")
                        mark["yevac_dve_%d" % (rr % 2)] = nda
                        mark["yevac_act_%d" % (rr % 2)] = nact_y
                        mark["yevac_dve"] = nda
                for j2 in range(NCH):
                    npe = pb.em("tensor", lambda e, j2=j2: e.transpose(
                        p2[:, 128 * j2:128 * j2 + 128], ytm[:, 128 * j2:128 * j2 + 128], i128b[:]),
                        waits=([("dve", mark["yevac_dve"]),
                                ("act", max(mark.get("tre_act", 0), mark.get("yevac_act_0", 0),
                                            mark.get("yevac_act_1", 0)))] if j2 == 0 else []),
                        inc="pe")
                mark["tre_act"] = pb.em("scalar", lambda e, c=c: e.activation(
                    out=yfv[:, :, 128 * c:128 * c + 128],
                    in_=p2[:].rearrange("p (c t) -> p c t", c=NCH), func=AF.Copy),
                    waits=[("pe", npe)], inc="act")
                mark["m3_pe"] = pb.n("pe")
            mark["vt_pe_%d" % (s % 2)] = pb.n("pe")
            if scan_args is not None:
                emit_scans(*scan_args)
                scan_args = None
            if s + 1 < NST:
                fr_next = emit_front(s + 1)
            T0 = ST * s
            if STAGE < 3:
                continue
            woevac = {}
            for oc in range(NCH):
                bank = [p0, p1][oc % 2]
                if oc in (0, 1):
                    w0 = [(WCV, cv_wo), ("act", mark["tre_act"]), ("dve", pb.n("dve")),
                          ("pool", pb.n("pool"))]
                else:
                    w0 = [("dve", woevac[oc - 2])]
                for kc in range(NCH):
                    npe = pb.em("tensor", lambda e, kc=kc, oc=oc, bank=bank: e.matmul(
                        bank[:, 0:ST], wov[:, kc, 128 * oc:128 * oc + 128], yfv[:, kc, :],
                        start=(kc == 0), stop=(kc == NCH - 1)),
                        waits=(w0 if kc == 0 else []), inc="pe")
                woevac[oc] = pb.em("vector", lambda e, oc=oc, bank=bank: e.tensor_tensor(
                    out=gbv[:, oc, :], in0=bank[:, 0:ST], in1=g0v[:, oc, :], op=ALU.mult),
                    waits=[("pe", npe), ("lg0", n_g0)] + ([("pool", pb.n("pool"))] if oc == 0 else []),
                    inc="dve")
                mark["g0_dve"] = woevac[oc]
            npl = pb.em("gpsimd", lambda e: e.tensor_tensor(
                out=x1f[:], in0=gbuf[:], in1=xin[:], op=ALU.add),
                waits=[("dve", woevac[NCH - 1]), ("lx", n_x), ("sx1", mark["sx1"])],
                inc="pool")
            mark["x_pool"] = npl
            mark["wo_pe"] = pb.n("pe")
            mark["sx1"] = pb.dma("gpsimd", X1v[:, :, T0:T0 + ST], x1vv[:, :, :], inc="sx1",
                                 waits=[("pool", npl)])
            if s + 1 < NST:
                n_x = pb.dma("sync", xv[:, :, :], Xv[:, :, T0 + ST:T0 + 2 * ST], inc="lx",
                             waits=[("pool", npl)])
                n_g0 = pb.dma("sync", g0v[:, :, :], G0v[:, :, T0 + ST:T0 + 2 * ST], inc="lg0",
                              waits=[("dve", mark["g0_dve"])])
            if STAGE < 4:
                continue
            na1 = pb.em("scalar", lambda e: e.activation(out=x1b[:], in_=x1f[:], func=AF.Copy),
                        waits=[("pool", npl), ("pe", mark["wo_pe"]), ("sxn2", mark["sxn2"])], inc="act")
            na2 = pb.em("scalar", lambda e: e.activation(out=x1q[:], in_=x1f[:], func=AF.Square),
                        inc="act")
            for ch in range(NCH):
                pb.em("tensor", lambda e, ch=ch: e.matmul(
                    pyb[0:1, 0:ST], inv1024b[:], x1bv[:, ch, :], start=(ch == 0), stop=(ch == NCH - 1)),
                    waits=([("act", na1), ("dve", mark["yevac_dve"]),
                            ("act", max(mark.get("yevac_act_0", 0), mark.get("yevac_act_1", 0)))]
                           if ch == 0 else []), inc="pe")
            for ch in range(NCH):
                npe_st = pb.em("tensor", lambda e, ch=ch: e.matmul(
                    pyd[0:1, 0:ST], inv1024b[:], x1qv[:, ch, :], start=(ch == 0), stop=(ch == NCH - 1)),
                    waits=([("act", na2)] if ch == 0 else []), inc="pe")
            m_row = rows[0:1, 0:ST]
            r_row = rows[0:1, ST:2 * ST]
            v_row = rows[0:1, 2 * ST:3 * ST]
            t_row = rows[0:1, 3 * ST:4 * ST]
            na_m = pb.em("scalar", lambda e: e.activation(out=m_row, in_=pyb[0:1, 0:ST], func=AF.Copy),
                         waits=[("pe", npe_st)], inc="act")
            pb.em("vector", lambda e: e.tensor_tensor(out=t_row, in0=m_row, in1=m_row, op=ALU.mult),
                  waits=[("act", na_m)], inc="dve")
            pb.drain("vector")
            nd_var = pb.em("vector", lambda e: e.tensor_tensor(out=v_row, in0=pyd[0:1, 0:ST], in1=t_row, op=ALU.subtract),
                           inc="dve")
            pb.em("scalar", lambda e: e.activation(out=t_row, in_=v_row, func=AF.Ln, bias=cns[0:1, 2:3]),
                  waits=[("dve", nd_var)], inc="act")
            pb.drain("scalar")
            na_rows = pb.em("scalar", lambda e: e.activation(out=r_row, in_=t_row, func=AF.Exp, scale=-0.5),
                            inc="act")
            pb.em("tensor", lambda e: e.matmul(pyb[:, 0:ST], ones1f[:], m_row, start=True, stop=True),
                  waits=[("act", na_rows)], inc="pe")
            npe_bc = pb.em("tensor", lambda e: e.matmul(pyd[:, 0:ST], ones1f[:], r_row, start=True, stop=True),
                           inc="pe")
            for ch in range(NCH):
                pb.em("vector", lambda e, ch=ch: e.tensor_tensor(
                    out=gbv[:, ch, :], in0=x1vv[:, ch, :], in1=pyb[:, 0:ST], op=ALU.subtract),
                    waits=([("pe", npe_bc), ("pool", npl)] if ch == 0 else []), inc="dve")
            pb.drain("vector")
            for ch in range(NCH):
                nd_xn2 = pb.em("vector", lambda e, ch=ch: e.tensor_tensor(
                    out=xn2v[:, ch, :], in0=gbv[:, ch, :], in1=pyd[:, 0:ST], op=ALU.mult),
                    waits=([("sxn2", mark["sxn2"])] if ch == 0 else []), inc="dve")
            mark["sxn2"] = pb.dma("gpsimd", XN2v[:, :, T0:T0 + ST], xn2v[:, :, :], inc="sxn2",
                                  waits=[("dve", nd_xn2)])
            if STAGE < 5:
                continue
            gevac = {}
            for pi, (wv_, bh, sc_, dst) in enumerate([
                    (wgv, bgh, 0.5, tgv), (wv1v, bvc, 1.0, tvv), (wdv, bdh, 0.5, tdv)]):
                cvw = [cv_wg, cv_wv1, cv_wd][pi]
                for mc in range(NCH):
                    bank = [pya, pyc][mc % 2]
                    if pi == 0 and mc in (0, 1):
                        w0 = [(WCV, cvw), ("dve", nd_xn2),
                              ("dve", mark.get("yevac_dve_%d" % (mc % 2), 0)),
                              ("act", mark.get("yevac_act_%d" % (mc % 2), 0))]
                    else:
                        w0 = [("act", gevac.get((pi, mc - 2), gevac.get((pi - 1, mc + 6), 0)))]
                        w0 += [(WCV, cvw)] if mc == 0 else []
                    for kc in range(NCH):
                        npe = pb.em("tensor", lambda e, kc=kc, mc=mc, bank=bank, wv_=wv_: e.matmul(
                            bank[:, 0:ST], wv_[:, kc, 128 * mc:128 * mc + 128], xn2v[:, kc, :],
                            start=(kc == 0), stop=(kc == NCH - 1)),
                            waits=(w0 if kc == 0 else []), inc="pe")
                    gevac[(pi, mc)] = pb.em("scalar", lambda e, mc=mc, bank=bank, bh=bh, sc_=sc_, dst=dst: e.activation(
                        out=dst[:, mc, :], in_=bank[:, 0:ST], func=AF.Tanh,
                        scale=sc_, bias=bh[:, mc:mc + 1]),
                        waits=[("pe", npe)], inc="act")
            na_g = gevac[(2, NCH - 1)]
            nd_sg = pb.em("vector", lambda e: e.tensor_scalar(out=gbuf[:], in0=tg[:], scalar1=0.5,
                          scalar2=0.5, op0=ALU.mult, op1=ALU.add),
                          waits=[("act", na_g)], inc="dve")
            np2 = pb.em("gpsimd", lambda e: e.tensor_tensor(out=tv[:], in0=gbuf[:], in1=tv[:], op=ALU.mult),
                        waits=[("dve", nd_sg)], inc="pool")
            pb.drain("vector")
            nd_aa = pb.em("vector", lambda e: e.tensor_scalar(out=tg[:], in0=td[:], scalar1=0.499,
                          scalar2=0.5, op0=ALU.mult, op1=ALU.add), inc="dve")
            np3 = np2
            pb.drain("vector")
            if STAGE < 6:
                continue
            scan_args = (s, T0, np3)
            if s + 1 < NST:
                fr = fr_next
        if scan_args is not None:
            emit_scans(*scan_args)
        pb.em("sync", lambda e: e.nop(), waits=[(snm, pb.n(snm)) for snm in
              ("sx1", "sxn2", "ss1", "spp", "sah") if pb.n(snm) > 0])


def build_pass1(ST=256, STAGE=99, NSTLIM=99):
    nc = bass.Bass("TRN2", target_bir_lowering=False)
    io = {k: nc.dram_tensor(k, *P1_SHAPES[k], kind="ExternalInput").ap() for k in P1_IN}
    io.update({k: nc.dram_tensor(k, *P1_SHAPES[k], kind="ExternalOutput").ap() for k in P1_OUT})
    pb = PB(nc)
    with contextlib.ExitStack() as ctx:
        emit_pass1(nc, pb, ctx, io, ST=ST, STAGE=STAGE, NSTLIM=NSTLIM)
        pb.build(ctx)
    return nc



P2_IN = ["XN2", "X1", "S1", "PP", "AHG", "MK", "W1", "B1C", "W2", "B2C"]
P2_OUT = ["OUT"]
P2_SHAPES = {
    "XN2": ([128, NCH * TCORE], BF16), "X1": ([128, NCH * TCORE], F32),
    "S1": ([128, NCH * TCORE], F32), "PP": ([128, NCH * TCORE], F32),
    "AHG": ([512, 16], F32), "MK": ([128, 16], F32),
    "W1": ([D, 4096], F32), "B1C": ([128, 32], F32),
    "W2": ([4096, D], F32), "B2C": ([128, 8], F32),
    "OUT": ([128, NCH * TCORE], F32),
}


def emit_pass2(nc, pb, ctx, io, ST=256, NSTLIM=99):
    NST = min(TCORE // ST, NSTLIM)
    NHC = 32
    XN2, X1, S1, PP, AHG, MK, W1, B1C, W2, B2C = [io[k] for k in P2_IN]
    OUT = io["OUT"]

    XN2v = XN2.rearrange("p (c t) -> p c t", c=NCH)
    X1v = X1.rearrange("p (c t) -> p c t", c=NCH)
    S1v = S1.rearrange("p (c t) -> p c t", c=NCH)
    PPv = PP.rearrange("p (c t) -> p c t", c=NCH)
    OUTv = OUT.rearrange("p (c t) -> p c t", c=NCH)

    if True:
        sb = lambda nm, shp, dt=F32: ctx.enter_context(nc.sbuf_tensor(nm + "_2", shp, dt))
        ps = lambda nm, shp, dt=F32: ctx.enter_context(nc.psum_tensor(nm + "_2", shp, dt))

        stage = sb("stage", [128, 2048])
        xn2b = sb("xn2b", [128, NCH * ST], BF16)
        x1f = sb("x1f", [128, NCH * ST])
        s1f = sb("s1f", [128, NCH * ST])
        ppf = sb("ppf", [128, NCH * ST])
        hb = sb("hb", [128, NHC * ST], BF16)
        fsb = sb("fsb", [128, NCH * ST])
        tqb = sb("tqb", [128, NCH * ST])
        outf = sb("outf", [128, NCH * ST])
        ahg = sb("ahg", [128, 16])
        hin = sb("hin", [128, 8])
        abar = sb("abar", [128, 8])
        hbar = sb("hbar", [128, 8])
        tmp8 = sb("tmp8", [128, 8])
        mk = sb("mk", [128, 16])
        b1c = sb("b1c", [128, 32])
        b2c = sb("b2c", [128, 8])
        b2h = sb("b2h", [128, 8])

        p0 = ps("p0", [128, 512])
        p1 = ps("p1", [128, 512])

        xn2v = xn2b[:].rearrange("p (c t) -> p c t", c=NCH)
        x1vv = x1f[:].rearrange("p (c t) -> p c t", c=NCH)
        s1vv = s1f[:].rearrange("p (c t) -> p c t", c=NCH)
        ppvv = ppf[:].rearrange("p (c t) -> p c t", c=NCH)
        hbv = hb[:].rearrange("p (c t) -> p c t", c=NHC)
        fsv = fsb[:].rearrange("p (c t) -> p c t", c=NCH)
        tqv = tqb[:].rearrange("p (c t) -> p c t", c=NCH)
        outv = outf[:].rearrange("p (c t) -> p c t", c=NCH)

        pb.dma("sync", mk[:], MK, inc="ldc")
        pb.dma("sync", b1c[:], B1C, inc="ldc")
        pb.dma("sync", b2c[:], B2C, inc="ldc")
        LDC = pb.n("ldc")
        nd = pb.em("vector", lambda e: e.tensor_scalar(out=b2h[:], in0=b2c[:], scalar1=0.5,
                   scalar2=None, op0=ALU.mult), waits=[("ldc", LDC)], inc="dve")
        nld = 0
        for r in range(4):
            nld = pb.dma("sync", ahg[:].rearrange("p c -> p c")[:, :], AHG[128 * r:128 * (r + 1), :], inc="lah",
                         waits=[("pool", pb.n("pool"))] + (list(io.get("_ahg_waits", [])) if r == 0 else []))
            if r == 0:
                nd = pb.em("gpsimd", lambda e: e.tensor_scalar(
                    out=hin[:], in0=ahg[:, 8:16], scalar1=mk[:, 0:1], scalar2=None, op0=ALU.mult),
                    waits=[("lah", nld), ("ldc", LDC)], inc="pool")
                pb.drain("gpsimd")
            else:
                nd = pb.em("gpsimd", lambda e, r=r: e.tensor_scalar(
                    out=abar[:], in0=ahg[:, 0:8], scalar1=mk[:, r:r + 1],
                    scalar2=mk[:, 8 + r:9 + r], op0=ALU.mult, op1=ALU.add),
                    waits=[("lah", nld)], inc="pool")
                nd = pb.em("gpsimd", lambda e, r=r: e.tensor_scalar(
                    out=hbar[:], in0=ahg[:, 8:16], scalar1=mk[:, r:r + 1],
                    scalar2=None, op0=ALU.mult), inc="pool")
                pb.drain("gpsimd")
                nd = pb.em("gpsimd", lambda e: e.tensor_tensor(out=tmp8[:], in0=hin[:], in1=abar[:], op=ALU.mult),
                           inc="pool")
                pb.drain("gpsimd")
                nd = pb.em("gpsimd", lambda e: e.tensor_tensor(out=hin[:], in0=tmp8[:], in1=hbar[:], op=ALU.add),
                           inc="pool")
                pb.drain("gpsimd")
        nd_hin = nd

        w1 = ctx.enter_context(nc.sbuf_tensor("w1_bf_2", [128, 8 * 4096], BF16))
        w1v_ = w1[:].rearrange("p (k o) -> p k o", k=8)
        W1r = W1.rearrange("(k p) o -> p k o", p=128)
        cv_w1h = []
        for half in range(2):
            dn = 0
            for k in range(8):
                nl = pb.dma("scalar", stage[:, 0:2048], W1r[:, k, 2048 * half:2048 * (half + 1)],
                            inc="wld", waits=[("wcv", pb.n("wcv"))])
                dn = pb.em("vector", lambda e, k=k, half=half: e.tensor_copy(
                    w1v_[:, k, 2048 * half:2048 * (half + 1)], stage[:, 0:2048]),
                    waits=[("wld", nl)], inc="wcv")
            cv_w1h.append(dn)
        cv_w1 = cv_w1h[1]
        w2, cv_w2 = load_weight_bf16(pb, ctx, "w2", W2, (128, 32, D), stage)
        w1v = w1[:].rearrange("p (k o) -> p k o", k=8)
        w2v = w2[:].rearrange("p (k o) -> p k o", k=32)

        mark = {"xn_pe": 0, "x1_dve": 0, "s1_dve": 0, "pp_dve": 0, "sout": 0,
                "h_pe": 0}
        for s in range(NST):
            T0 = ST * s
            n_xn = pb.dma("sync", xn2v[:, :, :], XN2v[:, :, T0:T0 + ST], inc="lxn",
                          waits=[("pe", mark["xn_pe"])])
            n_x1 = pb.dma("sync", x1vv[:, :, :], X1v[:, :, T0:T0 + ST], inc="lx1",
                          waits=[("dve", mark["x1_dve"])])
            n_s1 = pb.dma("sync", s1vv[:, :, :], S1v[:, :, T0:T0 + ST], inc="ls1",
                          waits=[("dve", mark["s1_dve"])])
            n_pp = pb.dma("sync", ppvv[:, :, :], PPv[:, :, T0:T0 + ST], inc="lpp",
                          waits=[("dve", mark["pp_dve"])])
            hevac = {}
            for hc in range(NHC):
                bank = [p0, p1][hc % 2]
                if hc in (0, 1):
                    w0 = [("wcv", cv_w1h[0]), ("lxn", n_xn),
                          ("dve", mark.get("fevac_%d" % (hc % 2), 0)),
                          ("act", pb.n("act")), ("pe", mark["h_pe"])]
                elif hc in (16, 17):
                    w0 = [("wcv", cv_w1h[1]),
                          (("act", hevac[hc - 2][1]) if hevac[hc - 2][0] == "act"
                           else ("dve", hevac[hc - 2][1]))]
                else:
                    w0 = [(("act", hevac[hc - 2][1]) if hevac[hc - 2][0] == "act"
                           else ("dve", hevac[hc - 2][1]))]
                for kc in range(NCH):
                    npe = pb.em("tensor", lambda e, kc=kc, hc=hc, bank=bank: e.matmul(
                        bank[:, 0:ST], w1v[:, kc, 128 * hc:128 * hc + 128], xn2v[:, kc, :],
                        start=(kc == 0), stop=(kc == NCH - 1)),
                        waits=(w0 if kc == 0 else []), inc="pe")
                if hc % 2 == 0:
                    na = pb.em("scalar", lambda e, hc=hc, bank=bank: e.activation(
                        out=hbv[:, hc, :], in_=bank[:, 0:ST], func=AF.Relu,
                        bias=b1c[:, hc:hc + 1]), waits=[("pe", npe)], inc="act")
                    hevac[hc] = ("act", na)
                else:
                    nd = pb.em("vector", lambda e, hc=hc, bank=bank: e.tensor_scalar(
                        out=hbv[:, hc, :], in0=bank[:, 0:ST], scalar1=b1c[:, hc:hc + 1],
                        scalar2=0.0, op0=ALU.add, op1=ALU.max),
                        waits=[("pe", npe)], inc="dve")
                    hevac[hc] = ("dve", nd)
            fevac = {}
            for oc in range(NCH):
                bank = [p0, p1][oc % 2]
                if oc in (0, 1):
                    la, lb = hevac[NHC - 2], hevac[NHC - 1]
                    w0 = [("wcv", cv_w2),
                          (("act", la[1]) if la[0] == "act" else ("dve", la[1])),
                          (("act", lb[1]) if lb[0] == "act" else ("dve", lb[1]))]
                else:
                    w0 = [("dve", fevac[oc - 2])]
                for kc in range(NHC):
                    npe = pb.em("tensor", lambda e, kc=kc, oc=oc, bank=bank: e.matmul(
                        bank[:, 0:ST], w2v[:, kc, 128 * oc:128 * oc + 128], hbv[:, kc, :],
                        start=(kc == 0), stop=(kc == NHC - 1)),
                        waits=(w0 if kc == 0 else []), inc="pe")
                fevac[oc] = pb.em("vector", lambda e, oc=oc, bank=bank: e.tensor_scalar(
                    out=fsv[:, oc, :], in0=bank[:, 0:ST], scalar1=b2c[:, oc:oc + 1],
                    scalar2=None, op0=ALU.add),
                    waits=[("pe", npe)] + ([("dve", mark.get("nd1", 0))] if oc == 0 else []),
                    inc="dve")
                mark["fevac_%d" % (oc % 2)] = fevac[oc]
            mark["xn_pe"] = pb.n("pe")
            mark["h_pe"] = pb.n("pe")
            na_t = pb.em("scalar", lambda e: e.activation(out=tqb[:], in_=fsb[:], func=AF.Tanh, scale=0.5),
                         waits=[("dve", fevac[NCH - 1])], inc="act")
            pb.drain("vector")
            nd1 = pb.em("vector", lambda e: e.scalar_tensor_tensor(
                out=tqb[:], in0=tqb[:], scalar=1.0, in1=fsb[:], op0=ALU.add, op1=ALU.mult),
                waits=[("act", na_t)], inc="dve")
            mark["nd1"] = nd1
            for oc in range(NCH):
                nd2 = pb.em("vector", lambda e, oc=oc: e.scalar_tensor_tensor(
                    out=ppvv[:, oc, :], in0=ppvv[:, oc, :], scalar=hin[:, oc:oc + 1],
                    in1=s1vv[:, oc, :], op0=ALU.mult, op1=ALU.add),
                    waits=([("lpp", n_pp), ("ls1", n_s1), ("pool", nd_hin)] if oc == 0 else []), inc="dve")
            mark["s1_dve"] = nd2
            pb.drain("vector")
            nd3 = pb.em("vector", lambda e: e.tensor_tensor(out=tqb[:], in0=ppf[:], in1=tqb[:], op=ALU.mult),
                        inc="dve")
            mark["pp_dve"] = nd3
            pb.drain("vector")
            nd4 = pb.em("vector", lambda e: e.scalar_tensor_tensor(
                out=outf[:], in0=tqb[:], scalar=0.5, in1=x1f[:], op0=ALU.mult, op1=ALU.add),
                waits=[("lx1", n_x1), ("sout", mark["sout"])], inc="dve")
            mark["x1_dve"] = nd4
            mark["sout"] = pb.dma("gpsimd", OUTv[:, :, T0:T0 + ST], outv[:, :, :], inc="sout",
                                  waits=[("dve", nd4)])
        pb.em("sync", lambda e: e.nop(), waits=[("sout", pb.n("sout"))])


def build_pass2(ST=256, NSTLIM=99):
    nc = bass.Bass("TRN2", target_bir_lowering=False)
    io = {k: nc.dram_tensor(k, *P2_SHAPES[k], kind="ExternalInput").ap() for k in P2_IN}
    io.update({k: nc.dram_tensor(k, *P2_SHAPES[k], kind="ExternalOutput").ap() for k in P2_OUT})
    pb = PB(nc)
    with contextlib.ExitStack() as ctx:
        emit_pass2(nc, pb, ctx, io, ST=ST, NSTLIM=NSTLIM)
        pb.build(ctx)
    return nc


def host_inputs_pass2(inputs, core, r1):
    b, q = core // 4, core % 4
    mkv = np.zeros((128, 16), np.float32)
    for r in range(4):
        pred = r < q
        mkv[:, r] = 1.0 if pred else 0.0
        mkv[:, 8 + r] = 0.0 if pred else 1.0
    ahg = np.concatenate([np.asarray(r1[4 * b + j]["AH"]) for j in range(4)], axis=0)
    res = r1[core]
    return {
        "XN2": np.asarray(res["XN2"]), "X1": np.asarray(res["X1"]),
        "S1": np.asarray(res["S1"]), "PP": np.asarray(res["PP"]),
        "AHG": ahg, "MK": mkv,
        "W1": np.asarray(inputs["L1_ffn_W1"], np.float32),
        "B1C": np.asarray(inputs["L1_ffn_b1"], np.float32).reshape(32, 128).T.copy(),
        "W2": np.asarray(inputs["L1_ffn_W2"], np.float32),
        "B2C": np.asarray(inputs["L1_ffn_b2"], np.float32).reshape(8, 128).T.copy(),
    }


def host_inputs_pass0(inputs, core):
    b, q = core // 4, core % 4
    x = np.ascontiguousarray(np.asarray(inputs["L0_attn_x"], np.float32)[b, q * TCORE:(q + 1) * TCORE, :].T)
    W1 = np.asarray(inputs["L0_ffn_W1"], np.float32)
    b1 = np.asarray(inputs["L0_ffn_b1"], np.float32)
    W2 = np.asarray(inputs["L0_ffn_W2"], np.float32)
    w1f = np.zeros((128, 8), np.float32)
    b1f = np.zeros((128, 8), np.float32)
    w2bd = np.zeros((D, 64), np.float32)
    f = np.arange(1024)
    w1f[f % 128, f // 128] = W1[f // 16, f % 16]
    b1f[f % 128, f // 128] = b1[f // 16, f % 16]
    w2bd[f, f // 16] = W2[f // 16, f % 16]
    E = np.zeros((64, D), np.float32)
    E[f // 16, f] = 1.0
    return {
        "X": x,
        "WK": np.asarray(inputs["L0_attn_Wk"], np.float32),
        "WV": np.asarray(inputs["L0_attn_Wv"], np.float32),
        "WIN": np.asarray(inputs["L0_ffn_Win"], np.float32),
        "EB": E, "W1F": w1f, "B1F": b1f, "W2BD": w2bd,
        "WOUT": np.asarray(inputs["L0_ffn_Wout"], np.float32),
        "BIN": np.asarray(inputs["L0_ffn_bin_"], np.float32).reshape(64, 1),
        "B2": np.asarray(inputs["L0_ffn_b2"], np.float32).reshape(64, 1),
        "BOUTC": np.asarray(inputs["L0_ffn_bout"], np.float32).reshape(8, 128).T.copy(),
        "I64": np.eye(128, dtype=np.float32),
        "CONSTC": np.stack([np.full(128, 1.0 / 1024.0, np.float32), np.ones(128, np.float32), np.full(128, LN_EPS, np.float32), np.full(128, ATT_EPS, np.float32)], axis=1),
        "ONESR": np.ones((1, 128), np.float32),
    }


def host_inputs_pass1(inputs, core, r0):
    b, q = core // 4, core % 4
    x = np.ascontiguousarray(np.asarray(inputs["L0_attn_x"], np.float32)[b, q * TCORE:(q + 1) * TCORE, :].T)
    mkv = np.zeros((128, 16), np.float32)
    for r in range(4):
        pred = r < q
        mkv[:, r] = 1.0 if pred else 0.0
        mkv[:, 8 + r] = 0.0 if pred else 1.0
    res = r0[core]
    szg = np.concatenate([np.asarray(r0[4 * b + j]["SZ"]) for j in range(4)], axis=0)
    constc = np.stack([np.full(128, 1.0 / 1024.0, np.float32), np.ones(128, np.float32),
                       np.full(128, LN_EPS, np.float32), np.full(128, ATT_EPS, np.float32)], axis=1)
    col8 = lambda v: np.asarray(v, np.float32).reshape(8, 128).T.copy()
    mask = np.tril(np.ones((128, 128), np.float32)).T.copy()
    return {
        "X": x, "XN": np.asarray(res["XN"]), "KFM": np.asarray(res["KFM"]),
        "VTM": np.asarray(res["VTM"]), "G0": np.asarray(res["G0"]),
        "SNP": np.asarray(res["SNP"]), "SZG": szg, "MK": mkv,
        "WQ": np.asarray(inputs["L0_attn_Wq"], np.float32),
        "WO": np.asarray(inputs["L0_attn_Wo"], np.float32),
        "WG": np.asarray(inputs["L1_attn_Wg"], np.float32),
        "WV1": np.asarray(inputs["L1_attn_Wv"], np.float32),
        "WD": np.asarray(inputs["L1_attn_Wd"], np.float32),
        "BGC": col8(inputs["L1_attn_bg"]), "BVC": col8(inputs["L1_attn_bv"]),
        "BDC": col8(inputs["L1_attn_bd"]),
        "MASKUT": mask, "I128": np.eye(128, dtype=np.float32),
        "CONSTC": constc, "ONESR": np.ones((1, 128), np.float32),
    }


def pb_barrier(pb, name):
    for eng in ("scalar", "vector", "gpsimd"):
        pb.drain(eng)
    for eng in ENGINES:
        pb.em(eng, lambda e: e.nop(), inc=name)
    val = pb.n(name)
    for eng in ENGINES:
        pb.em(eng, lambda e: e.nop(), waits=[(name, val)])


FUSED_EXT_IN = (P0_IN + ["MK", "WQ", "WO", "WG", "WV1", "WD", "BGC", "BVC",
                         "BDC", "W1", "B1C", "W2", "B2C"])
FUSED_INTERNAL = ["XN", "KFM", "VTM", "G0", "SZ", "SNP", "X1", "XN2", "S1",
                  "PP", "AH"]


def build_fused():
    nc = bass.Bass("TRN2", target_bir_lowering=False)
    shapes = dict(P0_SHAPES); shapes.update(P1_SHAPES); shapes.update(P2_SHAPES)
    io = {}
    for k in FUSED_EXT_IN:
        io[k] = nc.dram_tensor(k, *shapes[k], kind="ExternalInput").ap()
    for k in FUSED_INTERNAL:
        io[k] = nc.dram_tensor(k, *shapes[k]).ap()
    io["OUT"] = nc.dram_tensor("OUT", *shapes["OUT"], kind="ExternalOutput").ap()
    io["SZG"] = nc.dram_tensor("SZG", [256, 1040], F32).ap()
    io["AHG"] = nc.dram_tensor("AHG", [512, 16], F32).ap()
    io["I128"] = io["I64"]
    io["MASKUT"] = nc.dram_tensor("MASKUT", [128, 128], F32, kind="ExternalInput").ap()
    rg = [[0, 1, 2, 3], [4, 5, 6, 7]]
    pb = PB(nc)
    with contextlib.ExitStack() as ctx:
        with contextlib.ExitStack() as c01:
            prestage = c01.enter_context(nc.sbuf_tensor("prestage", [128, 1024], F32))
            pre_tiles = {nm: c01.enter_context(nc.sbuf_tensor(nm + "_bf", [128, 8 * D], BF16))
                         for nm in ("wq", "wo")}
            preW = []

            def _preload(pb):
                for nm, ap in [("wq", io["WQ"]), ("wo", io["WO"])]:
                    preW.append(load_weight_bf16(pb, c01, nm, ap, (128, 8, D), prestage,
                                                 dma_eng="gpsimd", cv_eng="gpsimd", pre="p",
                                                 tile=pre_tiles[nm]))
            def _mid(pb):
                pb.em("gpsimd", lambda e: e.nop(), waits=[("ssz", pb.n("ssz"))])
                pb.em("gpsimd", lambda e: e.collective_compute(
                    "AllGather", ALU.bypass, replica_groups=rg,
                    ins=[io["SZ"].opt()], outs=[io["SZG"].opt()]), inc="cc")
            io["_mid_hook"] = _mid
            io["_preload"] = _preload
            with contextlib.ExitStack() as c0:
                emit_pass0(nc, pb, c0, io)
            for nm in ("wg", "wv1", "wd"):
                pre_tiles[nm] = c01.enter_context(nc.sbuf_tensor(nm + "_bf", [128, 8 * D], BF16))
            pb_barrier(pb, "bar0")
            for nm, ap in [("wg", io["WG"]), ("wv1", io["WV1"]), ("wd", io["WD"])]:
                preW.append(load_weight_bf16(pb, c01, nm, ap, (128, 8, D), prestage,
                                             dma_eng="gpsimd", cv_eng="vector", pre="p",
                                             tile=pre_tiles[nm]))
            io["_preW"] = preW
            io["_szg_waits"] = [("cc", pb.n("cc"))]
            with contextlib.ExitStack() as c1:
                emit_pass1(nc, pb, c1, io)
        pb_barrier(pb, "bar1")
        pb.em("gpsimd", lambda e: e.collective_compute(
            "AllGather", ALU.bypass, replica_groups=rg,
            ins=[io["AH"].opt()], outs=[io["AHG"].opt()]), inc="cc")
        io["_ahg_waits"] = [("cc", pb.n("cc"))]
        with contextlib.ExitStack() as c2:
            emit_pass2(nc, pb, c2, io)
        pb.build(ctx)
    return nc


def host_inputs_fused(inputs, core):
    h0 = host_inputs_pass0(inputs, core)
    b, q = core // 4, core % 4
    mkv = np.zeros((128, 16), np.float32)
    for r in range(4):
        pred = r < q
        mkv[:, r] = 1.0 if pred else 0.0
        mkv[:, 8 + r] = 0.0 if pred else 1.0
    col8 = lambda v: np.asarray(v, np.float32).reshape(8, 128).T.copy()
    h0.update({
        "MK": mkv,
        "MASKUT": np.tril(np.ones((128, 128), np.float32)).T.copy(),
        "WQ": np.asarray(inputs["L0_attn_Wq"], np.float32),
        "WO": np.asarray(inputs["L0_attn_Wo"], np.float32),
        "WG": np.asarray(inputs["L1_attn_Wg"], np.float32),
        "WV1": np.asarray(inputs["L1_attn_Wv"], np.float32),
        "WD": np.asarray(inputs["L1_attn_Wd"], np.float32),
        "BGC": col8(inputs["L1_attn_bg"]), "BVC": col8(inputs["L1_attn_bv"]),
        "BDC": col8(inputs["L1_attn_bd"]),
        "W1": np.asarray(inputs["L1_ffn_W1"], np.float32),
        "B1C": np.asarray(inputs["L1_ffn_b1"], np.float32).reshape(32, 128).T.copy(),
        "W2": np.asarray(inputs["L1_ffn_W2"], np.float32),
        "B2C": np.asarray(inputs["L1_ffn_b2"], np.float32).reshape(8, 128).T.copy(),
    })
    return h0


def kernel(**inputs):
    nc = build_fused()
    in_maps = [host_inputs_fused(inputs, c) for c in range(NCORES)]
    rr = run_bass_kernel_spmd(nc, in_maps, core_ids=list(range(NCORES))).results
    out = np.empty((2, 16384, 1024), np.float32)
    for c in range(NCORES):
        b, q = c // 4, c % 4
        o = np.asarray(rr[c]["OUT"]).reshape(128, NCH, TCORE)
        out[b, q * TCORE:(q + 1) * TCORE, :] = o.transpose(2, 1, 0).reshape(TCORE, D)
    return out


def kernel_unfused(**inputs):
    nc0 = build_pass0()
    in_maps0 = [host_inputs_pass0(inputs, c) for c in range(NCORES)]
    r0 = run_bass_kernel_spmd(nc0, in_maps0, core_ids=list(range(NCORES))).results
    nc1 = build_pass1()
    in_maps1 = [host_inputs_pass1(inputs, c, r0) for c in range(NCORES)]
    r1 = run_bass_kernel_spmd(nc1, in_maps1, core_ids=list(range(NCORES))).results
    nc2 = build_pass2()
    in_maps2 = [host_inputs_pass2(inputs, c, r1) for c in range(NCORES)]
    r2 = run_bass_kernel_spmd(nc2, in_maps2, core_ids=list(range(NCORES))).results
    out = np.empty((2, 16384, 1024), np.float32)
    for c in range(NCORES):
        b, q = c // 4, c % 4
        o = np.asarray(r2[c]["OUT"]).reshape(128, NCH, TCORE)
        out[b, q * TCORE:(q + 1) * TCORE, :] = o.transpose(2, 1, 0).reshape(TCORE, D)
    return out


# version 20
# speedup vs baseline: 1.2112x; 1.2112x over previous
import contextlib
import numpy as np
import ml_dtypes
import concourse.bass as bass
import concourse.mybir as mybir
from concourse.bass_utils import run_bass_kernel_spmd

F32 = mybir.dt.float32
BF16 = mybir.dt.bfloat16
AF = mybir.ActivationFunctionType
ALU = mybir.AluOpType

NCORES = 8
D = 1024
NCH = 8
TCORE = 4096
H = 16
HD = 64
LN_EPS = 1e-5
ATT_EPS = 1e-6
BF = ml_dtypes.bfloat16

ENGINES = ("sync", "scalar", "vector", "gpsimd", "tensor")


class PB:

    def __init__(self, nc):
        self.nc = nc
        self.items = {e: [] for e in ENGINES}
        self.cnt = {}

    def n(self, name):
        return self.cnt.get(name, 0)

    def em(self, eng, fn, waits=(), inc=None, inc_n=1):
        w = [(s, v) for (s, v) in waits if v > 0]
        self.items[eng].append((w, fn, inc, inc_n))
        if inc is not None:
            self.cnt[inc] = self.cnt.get(inc, 0) + inc_n
        return self.cnt[inc] if inc is not None else None

    def dma(self, eng, out, in_, inc, waits=()):
        return self.em(eng, lambda e, o=out, i=in_: e.dma_start(out=o, in_=i),
                       waits=waits, inc=inc, inc_n=16)

    def drain(self, eng):
        self.em(eng, lambda e: e.drain())

    def build(self, ctx):
        sems = {name: ctx.enter_context(self.nc.semaphore(name))
                for name in self.cnt}
        block = ctx.enter_context(self.nc.Block())

        def mk(engname):
            def run(engine):
                for waits, fn, inc, inc_n in self.items[engname]:
                    for (s, v) in waits:
                        engine.wait_ge(sems[s], v)
                    instr = fn(engine)
                    if inc is not None:
                        instr.then_inc(sems[inc], inc_n)
            return run

        block.sync(mk("sync"))
        block.scalar(mk("scalar"))
        block.vector(mk("vector"))
        block.gpsimd(mk("gpsimd"))
        block.tensor(mk("tensor"))


def load_weight_bf16(pb, ctx, name, dram_ap, shape3, stage,
                     dma_eng="scalar", cv_eng="vector", pre="w", tile=None):
    nc = pb.nc
    p, kc, o = shape3
    t = tile if tile is not None else ctx.enter_context(
        nc.sbuf_tensor(name + "_bf", [p, kc * o], BF16))
    tv = t[:].rearrange("p (k o) -> p k o", k=kc) if kc > 1 else None
    src = dram_ap.rearrange("(k p) o -> p k o", p=p) if kc > 1 else dram_ap
    scols = stage.shape[1] // 2
    done = 0
    scv = pre + "cv"
    i = 0
    for k in range(kc):
        off = 0
        while off < o:
            w = min(scols, o - off)
            par = i % 2
            half = par * scols
            sld = pre + "ld%d" % par
            s_ap = src[:, k, off:off + w] if kc > 1 else dram_ap[:, off:off + w]
            d_ap = tv[:, k, off:off + w] if kc > 1 else t[:, off:off + w]
            nld = pb.dma(dma_eng, stage[0:p, half:half + w], s_ap, inc=sld,
                         waits=[(scv, pb.n(scv) if i == 0 else max(0, pb.n(scv) - 1))])
            done = pb.em(cv_eng, lambda e, dst=d_ap, sp=stage[0:p, half:half + w]:
                         e.tensor_copy(dst, sp),
                         waits=[(sld, nld)], inc=scv)
            off += w
            i += 1
    return t, done


P0_IN = ["X", "WK", "WV", "WIN", "EB", "W1F", "B1F", "W2BD", "WOUT", "BIN",
         "B2", "BOUTC", "I64", "CONSTC", "ONESR"]
P0_OUT = ["XN", "KFM", "VTM", "G0", "SZ", "SNP"]
P0_SHAPES = {
    "X": ([D, TCORE], F32), "WK": ([D, D], F32), "WV": ([D, D], F32),
    "WIN": ([D, 64], F32), "EB": ([64, D], F32), "W1F": ([128, 8], F32),
    "B1F": ([128, 8], F32), "W2BD": ([D, 64], F32), "WOUT": ([64, D], F32),
    "BIN": ([64, 1], F32), "B2": ([64, 1], F32), "BOUTC": ([128, 8], F32),
    "I64": ([128, 128], F32), "CONSTC": ([128, 4], F32), "ONESR": ([1, 128], F32),
    "XN": ([128, NCH * TCORE], BF16), "KFM": ([128, NCH * TCORE], BF16),
    "VTM": ([128, (TCORE // 128) * 1040], BF16), "G0": ([128, NCH * TCORE], BF16),
    "SZ": ([64, 1040], F32), "SNP": ([64, (TCORE // 128) * 1040], F32),
}


def emit_pass0(nc, pb, ctx, io, ST=512, STAGE=99, NSTLIM=99):
    NST = min(TCORE // ST, NSTLIM)
    CH_T = ST // 128
    (X, WK, WV, WIN, EB, W1F, B1F, W2BD, WOUT, BIN, B2, BOUTC, I64, CONSTC,
     ONESR) = [io[k] for k in P0_IN]
    XN, KFM, VTM, G0, SZ, SNP = [io[k] for k in P0_OUT]
    SZr = SZ.rearrange("p (g k q e) -> q p g k e", g=2, k=4, q=2, e=65)
    SNPr = SNP.rearrange("p (cg g k q e) -> cg q p g k e", cg=TCORE // 128, g=2, k=4, q=2, e=65)

    XNv = XN.rearrange("p (c t) -> p c t", c=NCH)
    KFMv = KFM.rearrange("p (c t) -> p c t", c=NCH)
    G0v = G0.rearrange("p (c t) -> p c t", c=NCH)
    VTMv = VTM.rearrange("p (c o) -> p c o", c=TCORE // 128)
    Xv = X.rearrange("(c p) t -> p c t", p=128)

    if True:
        sb = lambda nm, shp, dt=F32: ctx.enter_context(nc.sbuf_tensor(nm, shp, dt))
        ps = lambda nm, shp, dt=F32: ctx.enter_context(nc.psum_tensor(nm, shp, dt))

        stage = sb("stage", [128, 2048])
        xin = [sb("xin0", [128, NCH * ST])] * 2
        xbf = sb("xbf", [128, NCH * ST], BF16)
        xsq = sb("xsq", [128, NCH * ST], BF16)
        xnf = sb("xnf", [128, NCH * ST])
        xnb = sb("xnb", [128, NCH * ST], BF16)
        kfm = sb("kfm", [128, NCH * ST], BF16)
        efp = sb("efp", [128, ST])
        em1 = sb("em1", [128, ST])
        ktm = sb("ktm", [128, CH_T * 1024], BF16)
        vtm = sb("vtm", [128, CH_T * 1040], BF16)
        sfb = sb("sfb", [64, ST], BF16)
        hfb = sb("hfb", [128, NCH * ST], BF16)
        gfb = sb("gfb", [64, ST], BF16)
        g0b = sb("g0b", [128, NCH * ST], BF16)
        rows = sb("rows", [1, 4 * ST])
        statef = sb("statef", [128, 520])
        inv1024 = sb("inv1024", [128, 1], BF16)
        cns = sb("cns", [128, 4])
        ones1f = sb("ones1f", [1, 128])
        i64b = sb("i64b", [128, 128], BF16)
        i64f = sb("i64f", [128, 128])
        binc = sb("binc", [64, 1])
        b2c = sb("b2c", [64, 1])
        boutc = sb("boutc", [128, 8])
        w1f = sb("w1f", [128, 8])
        b1f = sb("b1f", [128, 8])

        p0 = ps("p0", [128, 512])
        p1 = ps("p1", [128, 512])
        p2 = ps("p2", [128, 512])
        p3 = ps("p3", [128, 512])
        p4 = ps("p4", [128, 1024], BF16)
        pst = [ps("pst0", [128, 512]), ps("pst1", [128, 512])]

        xbv = xbf[:].rearrange("p (c t) -> p c t", c=NCH)
        xqv = xsq[:].rearrange("p (c t) -> p c t", c=NCH)
        xnfv = xnf[:].rearrange("p (c t) -> p c t", c=NCH)
        xnbv = xnb[:].rearrange("p (c t) -> p c t", c=NCH)
        kfv = kfm[:].rearrange("p (c t) -> p c t", c=NCH)
        hfv = hfb[:].rearrange("p (c t) -> p c t", c=NCH)
        g0v = g0b[:].rearrange("p (c t) -> p c t", c=NCH)
        ktv = ktm[:].rearrange("p (c o) -> p c o", c=CH_T)
        vtv = vtm[:].rearrange("p (c o) -> p c o", c=CH_T)

        pb.dma("sync", cns[:], CONSTC, inc="ldc")
        pb.dma("sync", i64f[:], I64, inc="ldc")
        pb.dma("sync", ones1f[:], ONESR, inc="ldc")
        for dst, srcap in [(binc, BIN), (b2c, B2), (boutc, BOUTC), (w1f, W1F), (b1f, B1F)]:
            pb.dma("sync", dst[:], srcap, inc="ldc")
        LDC = pb.n("ldc")
        pb.em("scalar", lambda e: e.activation(out=inv1024[:], in_=cns[:, 0:1], func=AF.Copy),
              waits=[("ldc", LDC)], inc="act")
        cv_i64 = pb.em("scalar", lambda e: e.activation(out=i64b[:], in_=i64f[:], func=AF.Copy),
                       inc="act")
        ld_ones = LDC
        ld_small = LDC

        wk, cv_wk = load_weight_bf16(pb, ctx, "wk", WK, (128, 8, D), stage)
        wv, cv_wv = load_weight_bf16(pb, ctx, "wv", WV, (128, 8, D), stage)
        win, cv_win = load_weight_bf16(pb, ctx, "win", WIN, (128, 8, 64), stage)
        eb, cv_eb = load_weight_bf16(pb, ctx, "eb", EB, (64, 1, D), stage)
        w2bd, cv_bd = load_weight_bf16(pb, ctx, "w2bd", W2BD, (128, 8, 64), stage)
        wout, cv_wo = load_weight_bf16(pb, ctx, "wout", WOUT, (64, 1, D), stage)
        if io.get("_preload") is not None:
            io["_preload"](pb)
        wkv = wk[:].rearrange("p (k o) -> p k o", k=8)
        wvv = wv[:].rearrange("p (k o) -> p k o", k=8)
        winv = win[:].rearrange("p (k o) -> p k o", k=8)
        w2bdv = w2bd[:].rearrange("p (k o) -> p k o", k=8)

        n_ones = pb.em("vector", lambda e: e.memset(
            vtm[:].rearrange("p (c h o) -> p c h o", c=CH_T, h=16)[:, :, :, 64:65], 1.0), inc="dve")
        pb.drain("vector")
        zc = ctx.enter_context(nc.sbuf_tensor("zc", [1, 640], BF16))
        nz = pb.em("vector", lambda e: e.memset(zc[:], 0.0), inc="dve")
        pb.drain("vector")
        nz = pb.em("vector", lambda e: e.memset(statef[:], 0.0), inc="dve")
        pb.drain("vector")

        first_m4 = [True, True, True]

        def state_slot(h):
            g, s_ = h // 8, h % 8
            return g, 64 * (s_ % 2), 65 * (s_ // 2)

        mark = {"xn_st": 0, "kfm_st": 0, "vtm_st": 0, "g0_st": 0,
                "xn_dve": 0, "state_dve": nz, "ktm_act": 0, "ffn_pe": 0,
                "ffnb_pe_0": 0, "ffnb_pe_1": 0}

        def front0(s):
            T0 = ST * s
            T0 = ST * s
            xb = xin[s % 2]
            xv = xb[:].rearrange("p (c t) -> p c t", c=NCH)
            lx = "lx%d" % (s % 2)
            nld = pb.dma("sync", xv[:, :, :], Xv[:, :, T0:T0 + ST], inc=lx,
                         waits=[("dve", mark["xn_dve"])])
            na1 = pb.em("scalar", lambda e, xb=xb: e.activation(out=xbf[:], in_=xb[:], func=AF.Copy),
                        waits=[(lx, nld), ("pe", pb.n("pe"))], inc="act")
            na2 = pb.em("scalar", lambda e, xb=xb: e.activation(out=xsq[:], in_=xb[:], func=AF.Square),
                        inc="act")
            for ch in range(NCH):
                pb.em("tensor", lambda e, ch=ch: e.matmul(
                    p2[0:1, 0:ST], inv1024[:], xbv[:, ch, :], start=(ch == 0), stop=(ch == NCH - 1)),
                    waits=([("act", na1), ("dve", pb.n("dve"))] if ch == 0 else []), inc="pe")
            for ch in range(NCH):
                npe_st = pb.em("tensor", lambda e, ch=ch: e.matmul(
                    p3[0:1, 0:ST], inv1024[:], xqv[:, ch, :], start=(ch == 0), stop=(ch == NCH - 1)),
                    waits=([("act", na2)] if ch == 0 else []), inc="pe")
            m_row = rows[0:1, 0:ST]
            r_row = rows[0:1, ST:2 * ST]
            v_row = rows[0:1, 2 * ST:3 * ST]
            t_row = rows[0:1, 3 * ST:4 * ST]
            na_m = pb.em("scalar", lambda e: e.activation(out=m_row, in_=p2[0:1, 0:ST], func=AF.Copy),
                         waits=[("pe", npe_st)], inc="act")
            pb.em("vector", lambda e: e.tensor_tensor(out=t_row, in0=m_row, in1=m_row, op=ALU.mult),
                  waits=[("act", na_m)], inc="dve")
            pb.drain("vector")
            nd_var = pb.em("vector", lambda e: e.tensor_tensor(out=v_row, in0=p3[0:1, 0:ST], in1=t_row, op=ALU.subtract),
                           inc="dve")
            pb.em("scalar", lambda e: e.activation(out=t_row, in_=v_row, func=AF.Ln, bias=cns[0:1, 2:3]),
                  waits=[("ldc", ld_ones), ("dve", nd_var)], inc="act")
            pb.drain("scalar")
            na_rows = pb.em("scalar", lambda e: e.activation(out=r_row, in_=t_row, func=AF.Exp, scale=-0.5),
                            inc="act")
            pb.em("tensor", lambda e: e.matmul(p2[:, 0:ST], ones1f[:], m_row, start=True, stop=True),
                  waits=[("act", na_rows), ("ldc", ld_ones)], inc="pe")
            npe_bc = pb.em("tensor", lambda e: e.matmul(p3[:, 0:ST], ones1f[:], r_row, start=True, stop=True),
                           inc="pe")
            for ch in range(NCH):
                pb.em("vector", lambda e, ch=ch, xv=xv: e.tensor_tensor(
                    out=xnfv[:, ch, :], in0=xv[:, ch, :], in1=p2[:, 0:ST], op=ALU.subtract),
                    waits=([("pe", npe_bc)] if ch == 0 else []), inc="dve")
            pb.drain("vector")
            xn_ch = {}
            for ch in range(NCH):
                nd_xn = pb.em("vector", lambda e, ch=ch: e.tensor_tensor(
                    out=xnbv[:, ch, :], in0=xnfv[:, ch, :], in1=p3[:, 0:ST], op=ALU.mult),
                    waits=([("sxn", mark["xn_st"]), ("pe", pb.n("pe"))] if ch == 0 else []), inc="dve")
                xn_ch[ch] = nd_xn
            mark["xn_dve"] = nd_xn
            mark["xn_st"] = pb.dma("gpsimd", XNv[:, :, T0:T0 + ST], xnbv[:, :, :], inc="sxn",
                                   waits=[("dve", nd_xn)])
            return xn_ch

        xn_ch = front0(0)
        for s in range(NST):
            T0 = ST * s
            if STAGE < 2:
                continue
            kevac = {}
            for mc in range(NCH):
                bank = [p0, p1][mc % 2]
                if mc in (0, 1):
                    w0 = [("wcv", cv_wk), ("act", pb.n("act"))]
                else:
                    w0 = [("dve", kevac[mc - 2])]
                for kc in range(NCH):
                    wk_ = (w0 if kc == 0 else [])
                    if mc in (0, 1):
                        wk_ = wk_ + [("dve", xn_ch[kc])]
                    npe = pb.em("tensor", lambda e, kc=kc, mc=mc, bank=bank: e.matmul(
                        bank[:, 0:ST], wkv[:, kc, 128 * mc:128 * mc + 128], xnbv[:, kc, :],
                        start=(kc == 0), stop=(kc == NCH - 1)),
                        waits=wk_, inc="pe")
                na = pb.em("scalar", lambda e, bank=bank: e.activation(out=efp[:], in_=bank[:, 0:ST], func=AF.Exp),
                           waits=[("pe", npe), ("dve", kevac.get(mc - 1, 0))], inc="act")
                na = pb.em("scalar", lambda e, bank=bank: e.activation(out=em1[:], in_=bank[:, 0:ST], func=AF.Relu),
                           inc="act")
                krelu = krelu if mc else {}
                krelu[mc] = na
                kevac[mc] = pb.em("vector", lambda e, mc=mc: e.scalar_tensor_tensor(
                    out=kfv[:, mc, :], in0=efp[:], scalar=1.0, in1=em1[:],
                    op0=ALU.min, op1=ALU.add),
                    waits=[("act", na), ("skf", mark["kfm_st"]) if mc == 0 else ("act", 0)], inc="dve")
            mark["kfm_st"] = pb.dma("gpsimd", KFMv[:, :, T0:T0 + ST], kfv[:, :, :], inc="skf",
                                    waits=[("dve", kevac[NCH - 1])])
            if STAGE < 3:
                continue
            for c in range(CH_T):
                for j in range(NCH):
                    trw = [("dve", kevac[j])] if c == 0 else []
                    if j == 0:
                        trw = trw + [("act", mark["ktm_act"])]
                    npe = pb.em("tensor", lambda e, c=c, j=j: e.transpose(
                        p4[:, 128 * j:128 * j + 128],
                        kfv[:, j, 128 * c:128 * c + 128],
                        i64b[:]),
                        waits=trw, inc="pe")
                mark["ktm_act"] = pb.em("scalar", lambda e, c=c: e.activation(
                    out=ktv[:, c, :], in_=p4[:, 0:1024], func=AF.Copy),
                    waits=[("pe", npe)], inc="act")
            if STAGE < 4:
                continue
            vevac = {}
            for c in range(CH_T):
                for half in range(2):
                    bank = [p0, p1][half]
                    if c == 0:
                        w0 = [("wcv", cv_wv), ("dve", kevac[NCH - 1]), ("act", pb.n("act"))]
                    else:
                        w0 = [("act", vevac[(c - 1, half)])]
                    for kc in range(NCH):
                        npe = pb.em("tensor", lambda e, kc=kc, c=c, half=half, bank=bank: e.matmul(
                            bank[:, 0:512],
                            xnbv[:, kc, 128 * c:128 * c + 128],
                            wvv[:, kc, 512 * half:512 * half + 512],
                            start=(kc == 0), stop=(kc == NCH - 1)),
                            waits=(w0 if kc == 0 else []), inc="pe")
                    vevac[(c, half)] = pb.em("scalar", lambda e, c=c, half=half, bank=bank: e.activation(
                        out=vtv[:, c, :].rearrange("p (h o) -> p h o", h=16)[:, 8 * half:8 * half + 8, 0:64],
                        in_=bank[:, 0:512].rearrange("p (h o) -> p h o", h=8),
                        func=AF.Copy),
                        waits=[("pe", npe)] + ([("svt", mark["vtm_st"]), ("dve", n_ones)] if (c == 0 and half == 0) else []),
                        inc="act")
            mark["vtm_st"] = pb.dma("gpsimd", VTMv[:, CH_T * s:CH_T * (s + 1), :], vtv[:, :, :],
                                    inc="svt", waits=[("act", pb.n("act"))])
            if STAGE < 5:
                continue
            for c in range(CH_T):
                cg = CH_T * s + c
                n_sn = 0
                for q in range(2):
                    n_sn = pb.dma(
                        "gpsimd", SNPr[cg, q],
                        statef[64 * q:64 * q + 64, :].rearrange("p (g k e) -> p g k e", g=2, k=4),
                        inc="ssn", waits=([("dve", mark["state_dve"])] if q == 0 else []))
                for g in range(2):
                    pb.em("tensor", lambda e, g=g: e.matmul(
                        pst[g][:, 0:512], zc[0:1, 0:128], zc[0:1, 128:640],
                        start=True, stop=False),
                        waits=([("dve", mark["state_dve"]), ("act", pb.n("act"))] if g == 0 else []),
                        inc="pe")
                for h in range(H):
                    g, prow, pcol = state_slot(h)
                    pb.em("tensor", lambda e, c=c, h=h, g=g, prow=prow, pcol=pcol: e.matmul(
                        pst[g][prow:prow + 64, pcol:pcol + 65],
                        ktv[:, c, 64 * h:64 * h + 64],
                        vtv[:, c, 65 * h:65 * h + 65],
                        start=False, stop=False), inc="pe")
                npe_cl = 0
                for g in range(2):
                    npe_cl = pb.em("tensor", lambda e, g=g: e.matmul(
                        pst[g][:, 0:512], zc[0:1, 0:128], zc[0:1, 128:640],
                        start=False, stop=True), inc="pe")
                for g in range(2):
                    mark["state_dve"] = pb.em("vector", lambda e, g=g: e.tensor_tensor(
                        out=statef[:, 260 * g:260 * g + 260], in0=pst[g][:, 0:260],
                        in1=statef[:, 260 * g:260 * g + 260], op=ALU.add),
                        waits=([("pe", npe_cl), ("ssn", n_sn)] if g == 0 else []), inc="dve")
                pb.drain("vector")
            if s + 1 < NST:
                xn_ch = front0(s + 1)
            pass


        for q in range(2):
            pb.dma("gpsimd", SZr[q],
                   statef[64 * q:64 * q + 64, :].rearrange("p (g k e) -> p g k e", g=2, k=4),
                   inc="ssz", waits=([("dve", mark["state_dve"])] if q == 0 else []))
        if io.get("_mid_hook") is not None:
            io["_mid_hook"](pb)
        xnb2vs = [xnbv, kfv]
        if STAGE >= 6:
          for s in range(NST):
            T0 = ST * s
            xnb2v = xnb2vs[s % 2]
            if s < 2:
                wlb = [("pe", pb.n("pe")), ("sxn", pb.n("sxn")), ("skf", pb.n("skf"))]
            else:
                wlb = [("pe", mark["ffnb_pe_%d" % (s % 2)])]
            nld_b = pb.dma("sync", xnb2v[:, :, :], XNv[:, :, T0:T0 + ST],
                           inc="lxb%d" % (s % 2), waits=wlb)
            lxb_sem = "lxb%d" % (s % 2)
            for kc in range(NCH):
                npe = pb.em("tensor", lambda e, kc=kc, xnb2v=xnb2v: e.matmul(
                    p2[0:64, 0:ST], winv[:, kc, :], xnb2v[:, kc, :],
                    start=(kc == 0), stop=(kc == NCH - 1)),
                    waits=([("wcv", cv_win), ("dve", pb.n("dve")), (lxb_sem, nld_b)] if kc == 0 else []), inc="pe")
            na_s = pb.em("vector", lambda e: e.tensor_scalar(out=sfb[:], in0=p2[0:64, 0:ST], scalar1=binc[:], scalar2=None, op0=ALU.add),
                         waits=[("pe", npe), ("ldc", ld_small)], inc="dve")
            hevac = {}
            for ch in range(NCH):
                bank = [p0, p1][ch % 2]
                npe = pb.em("tensor", lambda e, ch=ch, bank=bank: e.matmul(
                    bank[:, 0:ST], eb[0:64, 128 * ch:128 * ch + 128], sfb[:],
                    start=True, stop=True),
                    waits=([("dve", na_s), ("wcv", cv_eb), ("act", pb.n("act"))] if ch < 2 else [("act", hevac[ch - 2])]), inc="pe")
                hevac[ch] = pb.em("scalar", lambda e, ch=ch, bank=bank: e.activation(
                    out=hfv[:, ch, :], in_=bank[:, 0:ST], func=AF.Relu,
                    scale=w1f[:, ch:ch + 1], bias=b1f[:, ch:ch + 1]),
                    waits=[("pe", npe)], inc="act")
            for ch in range(NCH):
                npe = pb.em("tensor", lambda e, ch=ch: e.matmul(
                    p2[0:64, 0:ST], w2bdv[:, ch, :], hfv[:, ch, :],
                    start=(ch == 0), stop=(ch == NCH - 1)),
                    waits=([("act", hevac[NCH - 1]), ("wcv", cv_bd)] if ch == 0 else []), inc="pe")
            na_g = pb.em("vector", lambda e: e.tensor_scalar(out=gfb[:], in0=p2[0:64, 0:ST], scalar1=b2c[:], scalar2=None, op0=ALU.add),
                         waits=[("pe", npe)], inc="dve")
            gevac = {}
            for ch in range(NCH):
                bank = [p0, p1][ch % 2]
                npe = pb.em("tensor", lambda e, ch=ch, bank=bank: e.matmul(
                    bank[:, 0:ST], wout[0:64, 128 * ch:128 * ch + 128], gfb[:],
                    start=True, stop=True),
                    waits=([("dve", na_g), ("wcv", cv_wo)] if ch < 2 else [("act", gevac[ch - 2])]), inc="pe")
                gevac[ch] = pb.em("scalar", lambda e, ch=ch, bank=bank: e.activation(
                    out=g0v[:, ch, :], in_=bank[:, 0:ST], func=AF.Silu, bias=boutc[:, ch:ch + 1]),
                    waits=[("pe", npe)] + ([("sg0", mark["g0_st"])] if ch == 0 else []), inc="act")
            mark["g0_st"] = pb.dma("gpsimd", G0v[:, :, T0:T0 + ST], g0v[:, :, :], inc="sg0",
                                   waits=[("act", gevac[NCH - 1])])
            mark["ffnb_pe_%d" % (s % 2)] = pb.n("pe")

        pb.em("sync", lambda e: e.nop(), waits=[(snm, pb.n(snm)) for snm in
              ("sxn", "skf", "svt", "sg0", "ssz", "ssn") if pb.n(snm) > 0])


def build_pass0(ST=512, STAGE=99, NSTLIM=99):
    nc = bass.Bass("TRN2", target_bir_lowering=False)
    io = {k: nc.dram_tensor(k, *P0_SHAPES[k], kind="ExternalInput").ap() for k in P0_IN}
    io.update({k: nc.dram_tensor(k, *P0_SHAPES[k], kind="ExternalOutput").ap() for k in P0_OUT})
    pb = PB(nc)
    with contextlib.ExitStack() as ctx:
        emit_pass0(nc, pb, ctx, io, ST=ST, STAGE=STAGE, NSTLIM=NSTLIM)
        pb.build(ctx)
    return nc



P1_IN = ["X", "XN", "KFM", "VTM", "G0", "SNP", "SZG", "MK", "WQ", "WO", "WG",
         "WV1", "WD", "BGC", "BVC", "BDC", "MASKUT", "I128", "CONSTC", "ONESR"]
P1_OUT = ["X1", "XN2", "S1", "PP", "AH"]
P1_SHAPES = {
    "X": ([D, TCORE], F32), "XN": ([128, NCH * TCORE], BF16),
    "KFM": ([128, NCH * TCORE], BF16), "VTM": ([128, (TCORE // 128) * 1040], BF16),
    "G0": ([128, NCH * TCORE], BF16), "SNP": ([64, (TCORE // 128) * 1040], F32),
    "SZG": ([256, 1040], F32), "MK": ([128, 16], F32),
    "WQ": ([D, D], F32), "WO": ([D, D], F32), "WG": ([D, D], F32),
    "WV1": ([D, D], F32), "WD": ([D, D], F32),
    "BGC": ([128, 8], F32), "BVC": ([128, 8], F32), "BDC": ([128, 8], F32),
    "MASKUT": ([128, 128], F32), "I128": ([128, 128], F32),
    "CONSTC": ([128, 4], F32), "ONESR": ([1, 128], F32),
    "X1": ([128, NCH * TCORE], F32), "XN2": ([128, NCH * TCORE], BF16),
    "S1": ([128, NCH * TCORE], F32), "PP": ([128, NCH * TCORE], F32),
    "AH": ([128, 16], F32),
}


def emit_pass1(nc, pb, ctx, io, ST=256, STAGE=99, NSTLIM=99):
    NST = min(TCORE // ST, NSTLIM)
    CH_T = ST // 128
    NCHK = TCORE // 128
    (X, XN, KFM, VTM, G0, SNP, SZG, MK, WQ, WO, WG, WV1, WD, BGC, BVC, BDC,
     MASKUT, I128, CONSTC, ONESR) = [io[k] for k in P1_IN]
    X1, XN2, S1, PP, AH = [io[k] for k in P1_OUT]
    WCV = "pcv" if io.get("_preW") is not None else "wcv"

    Xv = X.rearrange("(c p) t -> p c t", p=128)
    XNv = XN.rearrange("p (c t) -> p c t", c=NCH)
    KFMh = KFM.rearrange("(q p) (c t) -> p c q t", q=2, p=64, c=NCH)
    VTMv = VTM.rearrange("p (c o) -> p c o", c=NCHK)
    SNPv = SNP.rearrange("p (c o) -> p c o", c=NCHK)
    G0v = G0.rearrange("p (c t) -> p c t", c=NCH)
    X1v = X1.rearrange("p (c t) -> p c t", c=NCH)
    XN2v = XN2.rearrange("p (c t) -> p c t", c=NCH)
    S1v = S1.rearrange("p (c t) -> p c t", c=NCH)
    PPv = PP.rearrange("p (c t) -> p c t", c=NCH)

    if True:
        sb = lambda nm, shp, dt=F32: ctx.enter_context(nc.sbuf_tensor(nm + "_1", shp, dt))
        ps = lambda nm, shp, dt=F32: ctx.enter_context(nc.psum_tensor(nm + "_1", shp, dt))

        stage = None if io.get("_preW") is not None else sb("stage", [128, 512])
        xin = sb("xin", [128, NCH * ST])
        xnb = sb("xnb", [128, NCH * ST], BF16)
        kfma = sb("kfma", [64, H * 128], BF16)
        kfmb = sb("kfmb", [64, H * 128], BF16)
        vtm = sb("vtm", [128, CH_T * 1040], BF16)
        vtm2 = sb("vtm2", [128, CH_T * 1040], BF16)
        g0b = sb("g0b", [128, NCH * ST], BF16)
        qfma = sb("qfma", [64, H * ST], BF16)
        szg = sb("szg", [64, 1040])
        sprev = sb("sprev", [64, 1040])
        snp = sprev
        snapb = sb("snapb", [64, 1040], BF16)
        sprevb = sb("sprevb", [64, 1040], BF16)
        smbf = sb("smbf", [128, 2 * 256], BF16)
        ytm = sb("ytm", [128, 1024], BF16)
        efp = ytm[0:64, 0:2 * ST]
        em1 = ytm[0:64, 2 * ST:4 * ST]
        yfm = sb("yfm", [128, NCH * ST], BF16)
        denr = sb("denr", [128, 16])
        x1f = sb("x1f", [128, NCH * ST])
        xn2b = sb("xn2b", [128, NCH * ST], BF16)
        x1b = xn2b
        x1q = yfm
        tg = sb("tg", [128, NCH * ST])
        tv = sb("tv", [128, NCH * ST])
        td = sb("td", [128, NCH * ST])
        gbuf = sb("gbuf", [128, NCH * ST])
        s1b = sb("s1b", [128, NCH * ST])
        ppb = sb("ppb", [128, NCH * ST])
        hprev = sb("hprev", [128, 8])
        pprev = sb("pprev", [128, 8])
        zer = sb("zer", [128, ST])
        zcol = sb("zcol", [128, 1])
        rows = szg
        cns = sb("cns", [128, 4])
        inv1024b = sb("inv1024b", [128, 1], BF16)
        ones1f = sb("ones1f", [1, 128])
        i128f = sb("i128f", [128, 128])
        i128b = sb("i128b", [128, 128], BF16)
        maskut = sb("maskut", [128, 256])
        bgc = sb("bgc", [128, 8])
        bvc = sb("bvc", [128, 8])
        bdc = sb("bdc", [128, 8])
        bgh = sb("bgh", [128, 8])
        bdh = sb("bdh", [128, 8])
        mk = sb("mk", [128, 16])
        ahsb = sb("ahsb", [128, 16])

        p0 = ps("p0", [128, 512])
        p1 = ps("p1", [128, 512])
        p2 = ps("p2", [128, 1024], BF16)
        pya = ps("pya", [128, 512])
        pyb = ps("pyb", [128, 512])
        pyc = ps("pyc", [128, 512])
        pyd = ps("pyd", [128, 512])

        xv = xin[:].rearrange("p (c t) -> p c t", c=NCH)
        xnv = xnb[:].rearrange("p (c t) -> p c t", c=NCH)
        kfvs = [kfma[:].rearrange("p (h t) -> p h t", h=H),
                kfmb[:].rearrange("p (h t) -> p h t", h=H)]
        kfbufs = [kfma, kfmb]
        qfv = qfma[:].rearrange("p (h t) -> p h t", h=H)
        vtvs = [vtm[:].rearrange("p (c o) -> p c o", c=CH_T),
                vtm2[:].rearrange("p (c o) -> p c o", c=CH_T)]
        g0v = g0b[:].rearrange("p (c t) -> p c t", c=NCH)
        yfv = yfm[:].rearrange("p (c t) -> p c t", c=NCH)
        x1vv = x1f[:].rearrange("p (c t) -> p c t", c=NCH)
        x1bv = x1b[:].rearrange("p (c t) -> p c t", c=NCH)
        x1qv = x1q[:].rearrange("p (c t) -> p c t", c=NCH)
        xn2v = xn2b[:].rearrange("p (c t) -> p c t", c=NCH)
        tgv = tg[:].rearrange("p (c t) -> p c t", c=NCH)
        tvv = tv[:].rearrange("p (c t) -> p c t", c=NCH)
        tdv = td[:].rearrange("p (c t) -> p c t", c=NCH)
        gbv = gbuf[:].rearrange("p (c t) -> p c t", c=NCH)
        s1vv = s1b[:].rearrange("p (c t) -> p c t", c=NCH)
        ppvv = ppb[:].rearrange("p (c t) -> p c t", c=NCH)

        pb.dma("sync", cns[:], CONSTC, inc="ldc")
        pb.dma("sync", i128f[:], I128, inc="ldc")
        pb.dma("sync", ones1f[:], ONESR, inc="ldc")
        pb.dma("sync", maskut[:, 0:128], MASKUT, inc="ldc")
        pb.dma("sync", maskut[:, 128:256], MASKUT, inc="ldc")
        for dst, srcap in [(bgc, BGC), (bvc, BVC), (bdc, BDC), (mk, MK)]:
            pb.dma("sync", dst[:], srcap, inc="ldc")
        LDC = pb.n("ldc")
        pb.em("scalar", lambda e: e.activation(out=i128b[:], in_=i128f[:], func=AF.Copy),
              waits=[("ldc", LDC)], inc="act")
        pb.em("scalar", lambda e: e.activation(out=inv1024b[:], in_=cns[:, 0:1], func=AF.Copy),
              inc="act")
        nd = pb.em("vector", lambda e: e.tensor_scalar(out=bgh[:], in0=bgc[:], scalar1=0.5,
                   scalar2=None, op0=ALU.mult), waits=[("ldc", LDC)], inc="dve")
        nd = pb.em("vector", lambda e: e.tensor_scalar(out=bdh[:], in0=bdc[:], scalar1=0.5,
                   scalar2=None, op0=ALU.mult), inc="dve")
        nd = 0
        for r in range(4):
            nl = pb.dma("sync", szg[:], SZG[64 * r:64 * (r + 1), :], inc="lsz",
                        waits=[("dve", nd)] + (list(io.get("_szg_waits", [])) if r == 0 else []))
            if r == 0:
                nd = pb.em("vector", lambda e: e.tensor_scalar(
                    out=sprev[:], in0=szg[:], scalar1=mk[0:64, 0:1], scalar2=None, op0=ALU.mult),
                    waits=[("lsz", nl), ("ldc", LDC)], inc="dve")
            else:
                nd = pb.em("vector", lambda e, r=r: e.scalar_tensor_tensor(
                    out=sprev[:], in0=szg[:], scalar=mk[0:64, r:r + 1],
                    in1=sprev[:], op0=ALU.mult, op1=ALU.add),
                    waits=[("lsz", nl)], inc="dve")
            pb.drain("vector")
        na_spv = pb.em("scalar", lambda e: e.activation(out=sprevb[:], in_=sprev[:], func=AF.Copy),
                       waits=[("dve", nd)], inc="act")
        nd = pb.em("vector", lambda e: e.memset(zer[:], 0.0), inc="dve")
        nd = pb.em("vector", lambda e: e.memset(zcol[:], 0.0), inc="dve")
        nd = pb.em("vector", lambda e: e.memset(pprev[:], 1.0), inc="dve")
        nd_pro = pb.em("vector", lambda e: e.memset(hprev[:], 0.0), inc="dve")
        pb.drain("vector")

        if io.get("_preW") is not None:
            (wq, cv_wq), (wo, cv_wo), (wg, cv_wg), (wv1, cv_wv1), (wd, cv_wd) = io["_preW"]
        else:
            wq, cv_wq = load_weight_bf16(pb, ctx, "wq", WQ, (128, 8, D), stage)
            wo, cv_wo = load_weight_bf16(pb, ctx, "wo", WO, (128, 8, D), stage)
            wg, cv_wg = load_weight_bf16(pb, ctx, "wg", WG, (128, 8, D), stage)
            wv1, cv_wv1 = load_weight_bf16(pb, ctx, "wv1", WV1, (128, 8, D), stage)
            wd, cv_wd = load_weight_bf16(pb, ctx, "wd", WD, (128, 8, D), stage)
        wqv = wq[:].rearrange("p (k o) -> p k o", k=8)
        wov = wo[:].rearrange("p (k o) -> p k o", k=8)
        wgv = wg[:].rearrange("p (k o) -> p k o", k=8)
        wv1v = wv1[:].rearrange("p (k o) -> p k o", k=8)
        wdv = wd[:].rearrange("p (k o) -> p k o", k=8)

        mark = {"x_pool": 0, "xn_pe": 0, "g0_dve": 0,
                "m1pe_0": 0, "m1pe_1": 0, "vt_pe_0": 0, "vt_pe_1": 0,
                "snp_act": na_spv, "m3_pe": 0, "sx1": 0, "sxn2": 0, "ss1": 0,
                "spp": 0, "scan_dve": nd_pro, "hp_dve": nd_pro}

        def emit_front(s):
            T0 = ST * s
            T0 = ST * s
            n_xn = pb.dma("sync", xnv[:, :, :], XNv[:, :, T0:T0 + ST], inc="lxn",
                          waits=[("pe", mark["xn_pe"])])
            n_vt = pb.dma("sync", vtvs[s % 2][:, :, :], VTMv[:, CH_T * s:CH_T * (s + 1), :],
                          inc="lvt", waits=[("pe", mark["vt_pe_%d" % (s % 2)])])
            qevac = {}
            for pj in range(8):
                bank = [p0, p1][pj % 2]
                if pj in (0, 1):
                    w0 = [(WCV, cv_wq), ("lxn", n_xn), ("dve", pb.n("dve")),
                          ("act", pb.n("act"))]
                else:
                    w0 = [("dve", qevac[pj - 2])]
                for hh in range(2):
                    h_ = 2 * pj + hh
                    for kc in range(NCH):
                        npe = pb.em("tensor", lambda e, kc=kc, h_=h_, hh=hh, bank=bank: e.matmul(
                            bank[0:64, ST * hh:ST * hh + ST],
                            wqv[:, kc, 64 * h_:64 * h_ + 64], xnv[:, kc, :],
                            start=(kc == 0), stop=(kc == NCH - 1)),
                            waits=(w0 if (hh == 0 and kc == 0) else []), inc="pe")
                na = pb.em("scalar", lambda e, bank=bank: e.activation(
                    out=efp, in_=bank[0:64, 0:2 * ST], func=AF.Exp),
                    waits=[("pe", npe), ("dve", qevac.get(pj - 1, 0))], inc="act")
                na = pb.em("scalar", lambda e, bank=bank: e.activation(
                    out=em1, in_=bank[0:64, 0:2 * ST], func=AF.Relu),
                    inc="act")
                qrelu = qrelu if pj else {}
                qrelu[pj] = na
                qevac[pj] = pb.em("vector", lambda e, pj=pj: e.scalar_tensor_tensor(
                    out=qfv[:, 2 * pj:2 * pj + 2, :], in0=efp.rearrange(
                        "p (h t) -> p h t", h=2), scalar=1.0,
                    in1=em1.rearrange("p (h t) -> p h t", h=2),
                    op0=ALU.min, op1=ALU.add),
                    waits=[("act", na)] + ([("pe", mark["m3_pe"])] if pj == 0 else []), inc="dve")
            mark["xn_pe"] = pb.n("pe")
            return dict(n_xn=n_xn, n_vt=n_vt, qevac=qevac, qrelu=qrelu)

        def emit_scans(s, T0, np3):
            pb.drain("vector")
            nds = pb.n("dve")
            for ch in range(NCH):
                init = zcol[:, 0:1] if s == 0 else hprev[:, ch:ch + 1]
                nds = pb.em("vector", lambda e, ch=ch, init=init: e.tensor_tensor_scan(
                    out=s1vv[:, ch, :], data0=tgv[:, ch, :], data1=tvv[:, ch, :],
                    initial=init, op0=ALU.mult, op1=ALU.add),
                    waits=([("pool", np3), ("ss1", mark["ss1"]), ("dve", mark["hp_dve"])] if ch == 0 else []),
                    inc="dve")
            for ch in range(NCH):
                init = cns[:, 1:2] if s == 0 else pprev[:, ch:ch + 1]
                nds = pb.em("vector", lambda e, ch=ch, init=init: e.tensor_tensor_scan(
                    out=ppvv[:, ch, :], data0=tgv[:, ch, :], data1=zer[:, 0:ST],
                    initial=init, op0=ALU.mult, op1=ALU.add),
                    waits=([("spp", mark["spp"])] if ch == 0 else []), inc="dve")
            pb.drain("vector")
            nd_h = pb.em("vector", lambda e: e.tensor_copy(
                hprev[:], s1vv[:, :, ST - 1:ST].rearrange("p c t -> p (c t)")), inc="dve")
            nd_p = pb.em("vector", lambda e: e.tensor_copy(
                pprev[:], ppvv[:, :, ST - 1:ST].rearrange("p c t -> p (c t)")), inc="dve")
            pb.drain("vector")
            mark["hp_dve"] = nd_p
            mark["scan_dve"] = nd_p
            mark["ss1"] = pb.dma("gpsimd", S1v[:, :, T0:T0 + ST], s1vv[:, :, :], inc="ss1",
                                 waits=[("dve", nd_p)])
            mark["spp"] = pb.dma("gpsimd", PPv[:, :, T0:T0 + ST], ppvv[:, :, :], inc="spp",
                                 waits=[("dve", nd_p)])
            if s == NST - 1:
                nd_ah = pb.em("vector", lambda e: e.tensor_copy(ahsb[:, 0:8], pprev[:]),
                              waits=[("dve", nd_p)], inc="dve")
                nd_ah = pb.em("vector", lambda e: e.tensor_copy(ahsb[:, 8:16], hprev[:]), inc="dve")
                pb.dma("gpsimd", AH, ahsb[:], inc="sah", waits=[("dve", nd_ah)])

        scan_args = None
        n_x = pb.dma("sync", xv[:, :, :], Xv[:, :, 0:ST], inc="lx",
                     waits=[("pool", 0)])
        n_g0 = pb.dma("sync", g0v[:, :, :], G0v[:, :, 0:ST], inc="lg0",
                      waits=[("dve", 0)])
        mark["kf_ld"] = ("lkf0", pb.dma("sync",
                               kfma[:].rearrange("p (c q t) -> p c q t", c=NCH, q=2),
                               KFMh[:, :, :, 0:128], inc="lkf0", waits=[]))
        fr = emit_front(0)
        for s in range(NST):
            T0 = ST * s
            n_xn, n_vt = fr["n_xn"], fr["n_vt"]
            qevac, qrelu = fr["qevac"], fr["qrelu"]
            if STAGE < 2:
                continue
            vtv_c = vtvs[s % 2]
            for c in range(CH_T):
                cg = CH_T * s + c
                T0c = T0 + 128 * c
                n_kf = mark["kf_ld"]
                kfv_c = kfvs[cg % 2]
                n_sn = pb.dma("sync", snp[:], SNPv[:, cg, :], inc="lsn",
                              waits=[("act", mark["snp_act"])])
                na_sn = pb.em("scalar", lambda e: e.activation(
                    out=snapb[:], in_=snp[:], func=AF.Copy),
                    waits=[("lsn", n_sn), ("pe", mark["m3_pe"])], inc="act")
                mark["snp_act"] = na_sn
                mevac = {}

                def emit_m1_pair(pr):
                    scb = [p0, p1][pr % 2]
                    w0 = [("dve", mevac.get(pr - 2, qevac[pr]))]
                    if pr in (0, 1) and c == 0:
                        w0 += [("act", qrelu[6 + pr])]
                    if pr == 0:
                        w0 += [n_kf]
                    npe1 = 0
                    for hh in range(2):
                        j = 2 * pr + hh
                        npe1 = pb.em("tensor", lambda e, j=j, c=c, scb=scb, hh=hh, kfv=kfv_c: e.matmul(
                            scb[:, 128 * hh:128 * hh + 128], kfv[:, j, :],
                            qfv[:, j, 128 * c:128 * c + 128], start=True, stop=True),
                            waits=(w0 if hh == 0 else []), inc="pe")
                    mevac[pr] = pb.em("vector", lambda e, pr=pr, scb=scb: e.tensor_tensor(
                        out=smbf[:, 256 * (pr % 2):256 * (pr % 2) + 256], in0=scb[:, 0:256],
                        in1=maskut[:], op=ALU.mult),
                        waits=[("pe", npe1), ("ldc", LDC)], inc="dve")

                emit_m1_pair(0)
                for j in range(H):
                    rr, jj = j // 8, j % 8
                    ypair = [(pya, pyb), (pyc, pyd)][rr % 2]
                    yb = ypair[0] if jj < 4 else ypair[1]
                    ycol = 128 * (jj % 4)
                    if j % 2 == 0 and j // 2 + 1 < 8:
                        emit_m1_pair(j // 2 + 1)
                        if j == 12:
                            mark["m1pe_%d" % (cg % 2)] = pb.n("pe")
                            if cg + 1 < NST * CH_T:
                                kp = (cg + 1) % 2
                                mark["kf_ld"] = ("lkf%d" % kp, pb.dma(
                                    "sync",
                                    kfbufs[kp][:].rearrange(
                                        "p (c q t) -> p c q t", c=NCH, q=2),
                                    KFMh[:, :, :, 128 * (cg + 1):128 * (cg + 1) + 128],
                                    inc="lkf%d" % kp,
                                    waits=[("pe", mark["m1pe_%d" % kp])]))
                    w2 = [("dve", mevac[j // 2]), ("lvt", n_vt)]
                    if jj in (0, 4):
                        w2 += [("dve", mark.get("yevac_dve_%d" % (rr % 2), 0)),
                               ("act", mark.get("yevac_act_%d" % (rr % 2), 0))]
                    npe2 = pb.em("tensor", lambda e, j=j, c=c, yb=yb, ycol=ycol, vtv=vtv_c: e.matmul(
                        yb[:, ycol:ycol + 65],
                        smbf[:, 256 * ((j // 2) % 2) + 128 * (j % 2):
                             256 * ((j // 2) % 2) + 128 * (j % 2) + 128],
                        vtv[:, c, 65 * j:65 * j + 65], start=True, stop=False),
                        waits=w2, inc="pe")
                    pb.em("tensor", lambda e, j=j, c=c, yb=yb, ycol=ycol: e.matmul(
                        yb[:, ycol:ycol + 65], qfv[:, j, 128 * c:128 * c + 128],
                        snapb[:, 65 * j:65 * j + 65], start=False, stop=False),
                        waits=([("act", na_sn)] if j == 0 else []), inc="pe")
                    npe3 = pb.em("tensor", lambda e, j=j, c=c, yb=yb, ycol=ycol: e.matmul(
                        yb[:, ycol:ycol + 65], qfv[:, j, 128 * c:128 * c + 128],
                        sprevb[:, 65 * j:65 * j + 65], start=False, stop=True),
                        inc="pe")
                    if jj == 7:
                        ya_, yb_ = ypair
                        nda = pb.em("vector", lambda e, ya_=ya_, rr=rr: e.tensor_scalar(
                            out=denr[:, 8 * rr:8 * rr + 4],
                            in0=ya_[:].rearrange("p (k t) -> p k t", k=4)[:, :, 64:65],
                            scalar1=ATT_EPS, scalar2=None, op0=ALU.add),
                            waits=[("pe", npe3)], inc="dve")
                        nda = pb.em("vector", lambda e, yb_=yb_, rr=rr: e.tensor_scalar(
                            out=denr[:, 8 * rr + 4:8 * rr + 8],
                            in0=yb_[:].rearrange("p (k t) -> p k t", k=4)[:, :, 64:65],
                            scalar1=ATT_EPS, scalar2=None, op0=ALU.add), inc="dve")
                        pb.drain("vector")
                        nda = pb.em("vector", lambda e, rr=rr: e.reciprocal(
                            out=denr[:, 8 * rr:8 * rr + 8], in_=denr[:, 8 * rr:8 * rr + 8]), inc="dve")
                        pb.drain("vector")
                        nact_y = 0
                        for j2 in range(8):
                            h2 = 8 * rr + j2
                            yb2 = ypair[0] if j2 < 4 else ypair[1]
                            yc2 = 128 * (j2 % 4)
                            if j2 % 2 == 0:
                                nda = pb.em("vector", lambda e, h2=h2, yb2=yb2, yc2=yc2: e.tensor_scalar(
                                    out=ytm[:, 64 * h2:64 * h2 + 64], in0=yb2[:, yc2:yc2 + 64],
                                    scalar1=denr[:, h2:h2 + 1], scalar2=None, op0=ALU.mult),
                                    inc="dve")
                            else:
                                nact_y = pb.em("scalar", lambda e, h2=h2, yb2=yb2, yc2=yc2: e.activation(
                                    out=ytm[:, 64 * h2:64 * h2 + 64], in_=yb2[:, yc2:yc2 + 64],
                                    func=AF.Copy, scale=denr[:, h2:h2 + 1]),
                                    waits=([("dve", nda)] if j2 == 1 else []), inc="act")
                        mark["yevac_dve_%d" % (rr % 2)] = nda
                        mark["yevac_act_%d" % (rr % 2)] = nact_y
                        mark["yevac_dve"] = nda
                for j2 in range(NCH):
                    npe = pb.em("tensor", lambda e, j2=j2: e.transpose(
                        p2[:, 128 * j2:128 * j2 + 128], ytm[:, 128 * j2:128 * j2 + 128], i128b[:]),
                        waits=([("dve", mark["yevac_dve"]),
                                ("act", max(mark.get("tre_act", 0), mark.get("yevac_act_0", 0),
                                            mark.get("yevac_act_1", 0)))] if j2 == 0 else []),
                        inc="pe")
                mark["tre_act"] = pb.em("scalar", lambda e, c=c: e.activation(
                    out=yfv[:, :, 128 * c:128 * c + 128],
                    in_=p2[:].rearrange("p (c t) -> p c t", c=NCH), func=AF.Copy),
                    waits=[("pe", npe)], inc="act")
                mark["m3_pe"] = pb.n("pe")
            mark["vt_pe_%d" % (s % 2)] = pb.n("pe")
            if scan_args is not None:
                emit_scans(*scan_args)
                scan_args = None
            if s + 1 < NST:
                fr_next = emit_front(s + 1)
            T0 = ST * s
            if STAGE < 3:
                continue
            woevac = {}
            for oc in range(NCH):
                bank = [p0, p1][oc % 2]
                if oc in (0, 1):
                    w0 = [(WCV, cv_wo), ("act", mark["tre_act"]), ("dve", pb.n("dve"))]
                else:
                    w0 = [("dve", woevac[oc - 2])]
                for kc in range(NCH):
                    npe = pb.em("tensor", lambda e, kc=kc, oc=oc, bank=bank: e.matmul(
                        bank[:, 0:ST], wov[:, kc, 128 * oc:128 * oc + 128], yfv[:, kc, :],
                        start=(kc == 0), stop=(kc == NCH - 1)),
                        waits=(w0 if kc == 0 else []), inc="pe")
                woevac[oc] = pb.em("vector", lambda e, oc=oc, bank=bank: e.tensor_tensor(
                    out=gbv[:, oc, :], in0=bank[:, 0:ST], in1=g0v[:, oc, :], op=ALU.mult),
                    waits=[("pe", npe), ("lg0", n_g0)] + ([("pool", pb.n("pool"))] if oc == 0 else []),
                    inc="dve")
                mark["g0_dve"] = woevac[oc]
            npl = pb.em("gpsimd", lambda e: e.tensor_tensor(
                out=x1f[:], in0=gbuf[:], in1=xin[:], op=ALU.add),
                waits=[("dve", woevac[NCH - 1]), ("lx", n_x), ("sx1", mark["sx1"])],
                inc="pool")
            mark["x_pool"] = npl
            mark["wo_pe"] = pb.n("pe")
            mark["sx1"] = pb.dma("gpsimd", X1v[:, :, T0:T0 + ST], x1vv[:, :, :], inc="sx1",
                                 waits=[("pool", npl)])
            if s + 1 < NST:
                n_x = pb.dma("sync", xv[:, :, :], Xv[:, :, T0 + ST:T0 + 2 * ST], inc="lx",
                             waits=[("pool", npl)])
                n_g0 = pb.dma("sync", g0v[:, :, :], G0v[:, :, T0 + ST:T0 + 2 * ST], inc="lg0",
                              waits=[("dve", mark["g0_dve"])])
            if STAGE < 4:
                continue
            na1 = pb.em("scalar", lambda e: e.activation(out=x1b[:], in_=x1f[:], func=AF.Copy),
                        waits=[("pool", npl), ("pe", mark["wo_pe"]), ("sxn2", mark["sxn2"])], inc="act")
            na2 = pb.em("scalar", lambda e: e.activation(out=x1q[:], in_=x1f[:], func=AF.Square),
                        inc="act")
            for ch in range(NCH):
                pb.em("tensor", lambda e, ch=ch: e.matmul(
                    pyb[0:1, 0:ST], inv1024b[:], x1bv[:, ch, :], start=(ch == 0), stop=(ch == NCH - 1)),
                    waits=([("act", na1), ("dve", mark["yevac_dve"]),
                            ("act", max(mark.get("yevac_act_0", 0), mark.get("yevac_act_1", 0)))]
                           if ch == 0 else []), inc="pe")
            for ch in range(NCH):
                npe_st = pb.em("tensor", lambda e, ch=ch: e.matmul(
                    pyd[0:1, 0:ST], inv1024b[:], x1qv[:, ch, :], start=(ch == 0), stop=(ch == NCH - 1)),
                    waits=([("act", na2)] if ch == 0 else []), inc="pe")
            m_row = rows[0:1, 0:ST]
            r_row = rows[0:1, ST:2 * ST]
            v_row = rows[0:1, 2 * ST:3 * ST]
            t_row = rows[0:1, 3 * ST:4 * ST]
            na_m = pb.em("scalar", lambda e: e.activation(out=m_row, in_=pyb[0:1, 0:ST], func=AF.Copy),
                         waits=[("pe", npe_st)], inc="act")
            pb.em("vector", lambda e: e.tensor_tensor(out=t_row, in0=m_row, in1=m_row, op=ALU.mult),
                  waits=[("act", na_m)], inc="dve")
            pb.drain("vector")
            nd_var = pb.em("vector", lambda e: e.tensor_tensor(out=v_row, in0=pyd[0:1, 0:ST], in1=t_row, op=ALU.subtract),
                           inc="dve")
            pb.em("scalar", lambda e: e.activation(out=t_row, in_=v_row, func=AF.Ln, bias=cns[0:1, 2:3]),
                  waits=[("dve", nd_var)], inc="act")
            pb.drain("scalar")
            na_rows = pb.em("scalar", lambda e: e.activation(out=r_row, in_=t_row, func=AF.Exp, scale=-0.5),
                            inc="act")
            pb.em("tensor", lambda e: e.matmul(pyb[:, 0:ST], ones1f[:], m_row, start=True, stop=True),
                  waits=[("act", na_rows)], inc="pe")
            npe_bc = pb.em("tensor", lambda e: e.matmul(pyd[:, 0:ST], ones1f[:], r_row, start=True, stop=True),
                           inc="pe")
            for ch in range(NCH):
                pb.em("vector", lambda e, ch=ch: e.tensor_tensor(
                    out=gbv[:, ch, :], in0=x1vv[:, ch, :], in1=pyb[:, 0:ST], op=ALU.subtract),
                    waits=([("pe", npe_bc), ("pool", npl)] if ch == 0 else []), inc="dve")
            pb.drain("vector")
            for ch in range(NCH):
                nd_xn2 = pb.em("vector", lambda e, ch=ch: e.tensor_tensor(
                    out=xn2v[:, ch, :], in0=gbv[:, ch, :], in1=pyd[:, 0:ST], op=ALU.mult),
                    waits=([("sxn2", mark["sxn2"])] if ch == 0 else []), inc="dve")
            mark["sxn2"] = pb.dma("gpsimd", XN2v[:, :, T0:T0 + ST], xn2v[:, :, :], inc="sxn2",
                                  waits=[("dve", nd_xn2)])
            if STAGE < 5:
                continue
            gevac = {}
            for pi, (wv_, bh, sc_, dst) in enumerate([
                    (wgv, bgh, 0.5, tgv), (wv1v, bvc, 1.0, tvv), (wdv, bdh, 0.5, tdv)]):
                cvw = [cv_wg, cv_wv1, cv_wd][pi]
                for mc in range(NCH):
                    bank = [pya, pyc][mc % 2]
                    if pi == 0 and mc in (0, 1):
                        w0 = [(WCV, cvw), ("dve", nd_xn2),
                              ("dve", mark.get("yevac_dve_%d" % (mc % 2), 0)),
                              ("act", mark.get("yevac_act_%d" % (mc % 2), 0))]
                    else:
                        w0 = [("act", gevac.get((pi, mc - 2), gevac.get((pi - 1, mc + 6), 0)))]
                        w0 += [(WCV, cvw)] if mc == 0 else []
                    for kc in range(NCH):
                        npe = pb.em("tensor", lambda e, kc=kc, mc=mc, bank=bank, wv_=wv_: e.matmul(
                            bank[:, 0:ST], wv_[:, kc, 128 * mc:128 * mc + 128], xn2v[:, kc, :],
                            start=(kc == 0), stop=(kc == NCH - 1)),
                            waits=(w0 if kc == 0 else []), inc="pe")
                    gevac[(pi, mc)] = pb.em("scalar", lambda e, mc=mc, bank=bank, bh=bh, sc_=sc_, dst=dst: e.activation(
                        out=dst[:, mc, :], in_=bank[:, 0:ST], func=AF.Tanh,
                        scale=sc_, bias=bh[:, mc:mc + 1]),
                        waits=[("pe", npe)], inc="act")
            na_g = gevac[(2, NCH - 1)]
            nd_sg = pb.em("vector", lambda e: e.tensor_scalar(out=gbuf[:], in0=tg[:], scalar1=0.5,
                          scalar2=0.5, op0=ALU.mult, op1=ALU.add),
                          waits=[("act", na_g)], inc="dve")
            np2 = pb.em("gpsimd", lambda e: e.tensor_tensor(out=tv[:], in0=gbuf[:], in1=tv[:], op=ALU.mult),
                        waits=[("dve", nd_sg)], inc="pool")
            pb.drain("vector")
            nd_aa = pb.em("vector", lambda e: e.tensor_scalar(out=tg[:], in0=td[:], scalar1=0.499,
                          scalar2=0.5, op0=ALU.mult, op1=ALU.add), inc="dve")
            np3 = np2
            pb.drain("vector")
            if STAGE < 6:
                continue
            scan_args = (s, T0, np3)
            if s + 1 < NST:
                fr = fr_next
        if scan_args is not None:
            emit_scans(*scan_args)
        pb.em("sync", lambda e: e.nop(), waits=[(snm, pb.n(snm)) for snm in
              ("sx1", "sxn2", "ss1", "spp", "sah") if pb.n(snm) > 0])


def build_pass1(ST=256, STAGE=99, NSTLIM=99):
    nc = bass.Bass("TRN2", target_bir_lowering=False)
    io = {k: nc.dram_tensor(k, *P1_SHAPES[k], kind="ExternalInput").ap() for k in P1_IN}
    io.update({k: nc.dram_tensor(k, *P1_SHAPES[k], kind="ExternalOutput").ap() for k in P1_OUT})
    pb = PB(nc)
    with contextlib.ExitStack() as ctx:
        emit_pass1(nc, pb, ctx, io, ST=ST, STAGE=STAGE, NSTLIM=NSTLIM)
        pb.build(ctx)
    return nc



P2_IN = ["XN2", "X1", "S1", "PP", "AHG", "MK", "W1", "B1C", "W2", "B2C"]
P2_OUT = ["OUT"]
P2_SHAPES = {
    "XN2": ([128, NCH * TCORE], BF16), "X1": ([128, NCH * TCORE], F32),
    "S1": ([128, NCH * TCORE], F32), "PP": ([128, NCH * TCORE], F32),
    "AHG": ([512, 16], F32), "MK": ([128, 16], F32),
    "W1": ([D, 4096], F32), "B1C": ([128, 32], F32),
    "W2": ([4096, D], F32), "B2C": ([128, 8], F32),
    "OUT": ([128, NCH * TCORE], F32),
}


def emit_pass2(nc, pb, ctx, io, ST=256, NSTLIM=99):
    NST = min(TCORE // ST, NSTLIM)
    NHC = 32
    XN2, X1, S1, PP, AHG, MK, W1, B1C, W2, B2C = [io[k] for k in P2_IN]
    OUT = io["OUT"]

    XN2v = XN2.rearrange("p (c t) -> p c t", c=NCH)
    X1v = X1.rearrange("p (c t) -> p c t", c=NCH)
    S1v = S1.rearrange("p (c t) -> p c t", c=NCH)
    PPv = PP.rearrange("p (c t) -> p c t", c=NCH)
    OUTv = OUT.rearrange("p (c t) -> p c t", c=NCH)

    if True:
        sb = lambda nm, shp, dt=F32: ctx.enter_context(nc.sbuf_tensor(nm + "_2", shp, dt))
        ps = lambda nm, shp, dt=F32: ctx.enter_context(nc.psum_tensor(nm + "_2", shp, dt))

        stage = sb("stage", [128, 2048])
        xn2b = sb("xn2b", [128, NCH * ST], BF16)
        x1f = sb("x1f", [128, NCH * ST])
        s1f = sb("s1f", [128, NCH * ST])
        ppf = sb("ppf", [128, NCH * ST])
        hb = sb("hb", [128, NHC * ST], BF16)
        fsb = sb("fsb", [128, NCH * ST])
        tqb = sb("tqb", [128, NCH * ST])
        outf = sb("outf", [128, NCH * ST])
        ahg = sb("ahg", [128, 16])
        hin = sb("hin", [128, 8])
        abar = sb("abar", [128, 8])
        hbar = sb("hbar", [128, 8])
        tmp8 = sb("tmp8", [128, 8])
        mk = sb("mk", [128, 16])
        b1c = sb("b1c", [128, 32])
        b2c = sb("b2c", [128, 8])
        b2h = sb("b2h", [128, 8])

        p0 = ps("p0", [128, 512])
        p1 = ps("p1", [128, 512])

        xn2v = xn2b[:].rearrange("p (c t) -> p c t", c=NCH)
        x1vv = x1f[:].rearrange("p (c t) -> p c t", c=NCH)
        s1vv = s1f[:].rearrange("p (c t) -> p c t", c=NCH)
        ppvv = ppf[:].rearrange("p (c t) -> p c t", c=NCH)
        hbv = hb[:].rearrange("p (c t) -> p c t", c=NHC)
        fsv = fsb[:].rearrange("p (c t) -> p c t", c=NCH)
        tqv = tqb[:].rearrange("p (c t) -> p c t", c=NCH)
        outv = outf[:].rearrange("p (c t) -> p c t", c=NCH)

        pb.dma("sync", mk[:], MK, inc="ldc")
        pb.dma("sync", b1c[:], B1C, inc="ldc")
        pb.dma("sync", b2c[:], B2C, inc="ldc")
        LDC = pb.n("ldc")
        nd = pb.em("vector", lambda e: e.tensor_scalar(out=b2h[:], in0=b2c[:], scalar1=0.5,
                   scalar2=None, op0=ALU.mult), waits=[("ldc", LDC)], inc="dve")
        nld = 0
        for r in range(4):
            nld = pb.dma("sync", ahg[:].rearrange("p c -> p c")[:, :], AHG[128 * r:128 * (r + 1), :], inc="lah",
                         waits=[("pool", pb.n("pool"))] + (list(io.get("_ahg_waits", [])) if r == 0 else []))
            if r == 0:
                nd = pb.em("gpsimd", lambda e: e.tensor_scalar(
                    out=hin[:], in0=ahg[:, 8:16], scalar1=mk[:, 0:1], scalar2=None, op0=ALU.mult),
                    waits=[("lah", nld), ("ldc", LDC)], inc="pool")
                pb.drain("gpsimd")
            else:
                nd = pb.em("gpsimd", lambda e, r=r: e.tensor_scalar(
                    out=abar[:], in0=ahg[:, 0:8], scalar1=mk[:, r:r + 1],
                    scalar2=mk[:, 8 + r:9 + r], op0=ALU.mult, op1=ALU.add),
                    waits=[("lah", nld)], inc="pool")
                nd = pb.em("gpsimd", lambda e, r=r: e.tensor_scalar(
                    out=hbar[:], in0=ahg[:, 8:16], scalar1=mk[:, r:r + 1],
                    scalar2=None, op0=ALU.mult), inc="pool")
                pb.drain("gpsimd")
                nd = pb.em("gpsimd", lambda e: e.tensor_tensor(out=tmp8[:], in0=hin[:], in1=abar[:], op=ALU.mult),
                           inc="pool")
                pb.drain("gpsimd")
                nd = pb.em("gpsimd", lambda e: e.tensor_tensor(out=hin[:], in0=tmp8[:], in1=hbar[:], op=ALU.add),
                           inc="pool")
                pb.drain("gpsimd")
        nd_hin = nd

        w1 = ctx.enter_context(nc.sbuf_tensor("w1_bf_2", [128, 8 * 4096], BF16))
        w1v_ = w1[:].rearrange("p (k o) -> p k o", k=8)
        W1r = W1.rearrange("(k p) o -> p k o", p=128)
        cv_w1h = []
        for half in range(2):
            dn = 0
            for k in range(8):
                nl = pb.dma("scalar", stage[:, 0:2048], W1r[:, k, 2048 * half:2048 * (half + 1)],
                            inc="wld", waits=[("wcv", pb.n("wcv"))])
                dn = pb.em("vector", lambda e, k=k, half=half: e.tensor_copy(
                    w1v_[:, k, 2048 * half:2048 * (half + 1)], stage[:, 0:2048]),
                    waits=[("wld", nl)], inc="wcv")
            cv_w1h.append(dn)
        cv_w1 = cv_w1h[1]
        w2, cv_w2 = load_weight_bf16(pb, ctx, "w2", W2, (128, 32, D), stage)
        w1v = w1[:].rearrange("p (k o) -> p k o", k=8)
        w2v = w2[:].rearrange("p (k o) -> p k o", k=32)

        mark = {"xn_pe": 0, "x1_dve": 0, "s1_dve": 0, "pp_dve": 0, "sout": 0,
                "h_pe": 0}
        for s in range(NST):
            T0 = ST * s
            n_xn = pb.dma("sync", xn2v[:, :, :], XN2v[:, :, T0:T0 + ST], inc="lxn",
                          waits=[("pe", mark["xn_pe"])])
            n_x1 = pb.dma("sync", x1vv[:, :, :], X1v[:, :, T0:T0 + ST], inc="lx1",
                          waits=[("dve", mark["x1_dve"])])
            n_s1 = pb.dma("sync", s1vv[:, :, :], S1v[:, :, T0:T0 + ST], inc="ls1",
                          waits=[("dve", mark["s1_dve"])])
            n_pp = pb.dma("sync", ppvv[:, :, :], PPv[:, :, T0:T0 + ST], inc="lpp",
                          waits=[("dve", mark["pp_dve"])])
            hevac = {}
            for hc in range(NHC):
                bank = [p0, p1][hc % 2]
                if hc in (0, 1):
                    w0 = [("wcv", cv_w1h[0]), ("lxn", n_xn),
                          ("dve", mark.get("fevac_%d" % (hc % 2), 0)),
                          ("act", pb.n("act")), ("pe", mark["h_pe"])]
                elif hc in (16, 17):
                    w0 = [("wcv", cv_w1h[1]),
                          (("act", hevac[hc - 2][1]) if hevac[hc - 2][0] == "act"
                           else ("dve", hevac[hc - 2][1]))]
                else:
                    w0 = [(("act", hevac[hc - 2][1]) if hevac[hc - 2][0] == "act"
                           else ("dve", hevac[hc - 2][1]))]
                for kc in range(NCH):
                    npe = pb.em("tensor", lambda e, kc=kc, hc=hc, bank=bank: e.matmul(
                        bank[:, 0:ST], w1v[:, kc, 128 * hc:128 * hc + 128], xn2v[:, kc, :],
                        start=(kc == 0), stop=(kc == NCH - 1)),
                        waits=(w0 if kc == 0 else []), inc="pe")
                if hc % 2 == 0:
                    na = pb.em("scalar", lambda e, hc=hc, bank=bank: e.activation(
                        out=hbv[:, hc, :], in_=bank[:, 0:ST], func=AF.Relu,
                        bias=b1c[:, hc:hc + 1]), waits=[("pe", npe)], inc="act")
                    hevac[hc] = ("act", na)
                else:
                    nd = pb.em("vector", lambda e, hc=hc, bank=bank: e.tensor_scalar(
                        out=hbv[:, hc, :], in0=bank[:, 0:ST], scalar1=b1c[:, hc:hc + 1],
                        scalar2=0.0, op0=ALU.add, op1=ALU.max),
                        waits=[("pe", npe)], inc="dve")
                    hevac[hc] = ("dve", nd)
            fevac = {}
            for oc in range(NCH):
                bank = [p0, p1][oc % 2]
                if oc in (0, 1):
                    la, lb = hevac[NHC - 2], hevac[NHC - 1]
                    w0 = [("wcv", cv_w2),
                          (("act", la[1]) if la[0] == "act" else ("dve", la[1])),
                          (("act", lb[1]) if lb[0] == "act" else ("dve", lb[1]))]
                else:
                    w0 = [("dve", fevac[oc - 2])]
                for kc in range(NHC):
                    npe = pb.em("tensor", lambda e, kc=kc, oc=oc, bank=bank: e.matmul(
                        bank[:, 0:ST], w2v[:, kc, 128 * oc:128 * oc + 128], hbv[:, kc, :],
                        start=(kc == 0), stop=(kc == NHC - 1)),
                        waits=(w0 if kc == 0 else []), inc="pe")
                fevac[oc] = pb.em("vector", lambda e, oc=oc, bank=bank: e.tensor_scalar(
                    out=fsv[:, oc, :], in0=bank[:, 0:ST], scalar1=b2c[:, oc:oc + 1],
                    scalar2=None, op0=ALU.add),
                    waits=[("pe", npe)] + ([("dve", mark.get("nd1", 0))] if oc == 0 else []),
                    inc="dve")
                mark["fevac_%d" % (oc % 2)] = fevac[oc]
            mark["xn_pe"] = pb.n("pe")
            mark["h_pe"] = pb.n("pe")
            na_t = pb.em("scalar", lambda e: e.activation(out=tqb[:], in_=fsb[:], func=AF.Tanh, scale=0.5),
                         waits=[("dve", fevac[NCH - 1])], inc="act")
            pb.drain("vector")
            nd1 = pb.em("vector", lambda e: e.scalar_tensor_tensor(
                out=tqb[:], in0=tqb[:], scalar=1.0, in1=fsb[:], op0=ALU.add, op1=ALU.mult),
                waits=[("act", na_t)], inc="dve")
            mark["nd1"] = nd1
            for oc in range(NCH):
                nd2 = pb.em("vector", lambda e, oc=oc: e.scalar_tensor_tensor(
                    out=ppvv[:, oc, :], in0=ppvv[:, oc, :], scalar=hin[:, oc:oc + 1],
                    in1=s1vv[:, oc, :], op0=ALU.mult, op1=ALU.add),
                    waits=([("lpp", n_pp), ("ls1", n_s1), ("pool", nd_hin)] if oc == 0 else []), inc="dve")
            mark["s1_dve"] = nd2
            pb.drain("vector")
            nd3 = pb.em("vector", lambda e: e.tensor_tensor(out=tqb[:], in0=ppf[:], in1=tqb[:], op=ALU.mult),
                        inc="dve")
            mark["pp_dve"] = nd3
            pb.drain("vector")
            nd4 = pb.em("vector", lambda e: e.scalar_tensor_tensor(
                out=outf[:], in0=tqb[:], scalar=0.5, in1=x1f[:], op0=ALU.mult, op1=ALU.add),
                waits=[("lx1", n_x1), ("sout", mark["sout"])], inc="dve")
            mark["x1_dve"] = nd4
            mark["sout"] = pb.dma("gpsimd", OUTv[:, :, T0:T0 + ST], outv[:, :, :], inc="sout",
                                  waits=[("dve", nd4)])
        pb.em("sync", lambda e: e.nop(), waits=[("sout", pb.n("sout"))])


def build_pass2(ST=256, NSTLIM=99):
    nc = bass.Bass("TRN2", target_bir_lowering=False)
    io = {k: nc.dram_tensor(k, *P2_SHAPES[k], kind="ExternalInput").ap() for k in P2_IN}
    io.update({k: nc.dram_tensor(k, *P2_SHAPES[k], kind="ExternalOutput").ap() for k in P2_OUT})
    pb = PB(nc)
    with contextlib.ExitStack() as ctx:
        emit_pass2(nc, pb, ctx, io, ST=ST, NSTLIM=NSTLIM)
        pb.build(ctx)
    return nc


def host_inputs_pass2(inputs, core, r1):
    b, q = core // 4, core % 4
    mkv = np.zeros((128, 16), np.float32)
    for r in range(4):
        pred = r < q
        mkv[:, r] = 1.0 if pred else 0.0
        mkv[:, 8 + r] = 0.0 if pred else 1.0
    ahg = np.concatenate([np.asarray(r1[4 * b + j]["AH"]) for j in range(4)], axis=0)
    res = r1[core]
    return {
        "XN2": np.asarray(res["XN2"]), "X1": np.asarray(res["X1"]),
        "S1": np.asarray(res["S1"]), "PP": np.asarray(res["PP"]),
        "AHG": ahg, "MK": mkv,
        "W1": np.asarray(inputs["L1_ffn_W1"], np.float32),
        "B1C": np.asarray(inputs["L1_ffn_b1"], np.float32).reshape(32, 128).T.copy(),
        "W2": np.asarray(inputs["L1_ffn_W2"], np.float32),
        "B2C": np.asarray(inputs["L1_ffn_b2"], np.float32).reshape(8, 128).T.copy(),
    }


def host_inputs_pass0(inputs, core):
    b, q = core // 4, core % 4
    x = np.ascontiguousarray(np.asarray(inputs["L0_attn_x"], np.float32)[b, q * TCORE:(q + 1) * TCORE, :].T)
    W1 = np.asarray(inputs["L0_ffn_W1"], np.float32)
    b1 = np.asarray(inputs["L0_ffn_b1"], np.float32)
    W2 = np.asarray(inputs["L0_ffn_W2"], np.float32)
    w1f = np.zeros((128, 8), np.float32)
    b1f = np.zeros((128, 8), np.float32)
    w2bd = np.zeros((D, 64), np.float32)
    f = np.arange(1024)
    w1f[f % 128, f // 128] = W1[f // 16, f % 16]
    b1f[f % 128, f // 128] = b1[f // 16, f % 16]
    w2bd[f, f // 16] = W2[f // 16, f % 16]
    E = np.zeros((64, D), np.float32)
    E[f // 16, f] = 1.0
    return {
        "X": x,
        "WK": np.asarray(inputs["L0_attn_Wk"], np.float32),
        "WV": np.asarray(inputs["L0_attn_Wv"], np.float32),
        "WIN": np.asarray(inputs["L0_ffn_Win"], np.float32),
        "EB": E, "W1F": w1f, "B1F": b1f, "W2BD": w2bd,
        "WOUT": np.asarray(inputs["L0_ffn_Wout"], np.float32),
        "BIN": np.asarray(inputs["L0_ffn_bin_"], np.float32).reshape(64, 1),
        "B2": np.asarray(inputs["L0_ffn_b2"], np.float32).reshape(64, 1),
        "BOUTC": np.asarray(inputs["L0_ffn_bout"], np.float32).reshape(8, 128).T.copy(),
        "I64": np.eye(128, dtype=np.float32),
        "CONSTC": np.stack([np.full(128, 1.0 / 1024.0, np.float32), np.ones(128, np.float32), np.full(128, LN_EPS, np.float32), np.full(128, ATT_EPS, np.float32)], axis=1),
        "ONESR": np.ones((1, 128), np.float32),
    }


def host_inputs_pass1(inputs, core, r0):
    b, q = core // 4, core % 4
    x = np.ascontiguousarray(np.asarray(inputs["L0_attn_x"], np.float32)[b, q * TCORE:(q + 1) * TCORE, :].T)
    mkv = np.zeros((128, 16), np.float32)
    for r in range(4):
        pred = r < q
        mkv[:, r] = 1.0 if pred else 0.0
        mkv[:, 8 + r] = 0.0 if pred else 1.0
    res = r0[core]
    szg = np.concatenate([np.asarray(r0[4 * b + j]["SZ"]) for j in range(4)], axis=0)
    constc = np.stack([np.full(128, 1.0 / 1024.0, np.float32), np.ones(128, np.float32),
                       np.full(128, LN_EPS, np.float32), np.full(128, ATT_EPS, np.float32)], axis=1)
    col8 = lambda v: np.asarray(v, np.float32).reshape(8, 128).T.copy()
    mask = np.tril(np.ones((128, 128), np.float32)).T.copy()
    return {
        "X": x, "XN": np.asarray(res["XN"]), "KFM": np.asarray(res["KFM"]),
        "VTM": np.asarray(res["VTM"]), "G0": np.asarray(res["G0"]),
        "SNP": np.asarray(res["SNP"]), "SZG": szg, "MK": mkv,
        "WQ": np.asarray(inputs["L0_attn_Wq"], np.float32),
        "WO": np.asarray(inputs["L0_attn_Wo"], np.float32),
        "WG": np.asarray(inputs["L1_attn_Wg"], np.float32),
        "WV1": np.asarray(inputs["L1_attn_Wv"], np.float32),
        "WD": np.asarray(inputs["L1_attn_Wd"], np.float32),
        "BGC": col8(inputs["L1_attn_bg"]), "BVC": col8(inputs["L1_attn_bv"]),
        "BDC": col8(inputs["L1_attn_bd"]),
        "MASKUT": mask, "I128": np.eye(128, dtype=np.float32),
        "CONSTC": constc, "ONESR": np.ones((1, 128), np.float32),
    }


def pb_barrier(pb, name):
    for eng in ("scalar", "vector", "gpsimd"):
        pb.drain(eng)
    for eng in ENGINES:
        pb.em(eng, lambda e: e.nop(), inc=name)
    val = pb.n(name)
    for eng in ENGINES:
        pb.em(eng, lambda e: e.nop(), waits=[(name, val)])


FUSED_EXT_IN = (P0_IN + ["MK", "WQ", "WO", "WG", "WV1", "WD", "BGC", "BVC",
                         "BDC", "W1", "B1C", "W2", "B2C"])
FUSED_INTERNAL = ["XN", "KFM", "VTM", "G0", "SZ", "SNP", "X1", "XN2", "S1",
                  "PP", "AH"]


def build_fused():
    nc = bass.Bass("TRN2", target_bir_lowering=False)
    shapes = dict(P0_SHAPES); shapes.update(P1_SHAPES); shapes.update(P2_SHAPES)
    io = {}
    for k in FUSED_EXT_IN:
        io[k] = nc.dram_tensor(k, *shapes[k], kind="ExternalInput").ap()
    for k in FUSED_INTERNAL:
        io[k] = nc.dram_tensor(k, *shapes[k]).ap()
    io["OUT"] = nc.dram_tensor("OUT", *shapes["OUT"], kind="ExternalOutput").ap()
    io["SZG"] = nc.dram_tensor("SZG", [256, 1040], F32).ap()
    io["AHG"] = nc.dram_tensor("AHG", [512, 16], F32).ap()
    io["I128"] = io["I64"]
    io["MASKUT"] = nc.dram_tensor("MASKUT", [128, 128], F32, kind="ExternalInput").ap()
    rg = [[0, 1, 2, 3], [4, 5, 6, 7]]
    pb = PB(nc)
    with contextlib.ExitStack() as ctx:
        with contextlib.ExitStack() as c01:
            prestage = c01.enter_context(nc.sbuf_tensor("prestage", [128, 1024], F32))
            pre_tiles = {nm: c01.enter_context(nc.sbuf_tensor(nm + "_bf", [128, 8 * D], BF16))
                         for nm in ("wq", "wo")}
            preW = []

            def _preload(pb):
                for nm, ap in [("wq", io["WQ"]), ("wo", io["WO"])]:
                    preW.append(load_weight_bf16(pb, c01, nm, ap, (128, 8, D), prestage,
                                                 dma_eng="gpsimd", cv_eng="gpsimd", pre="p",
                                                 tile=pre_tiles[nm]))
            def _mid(pb):
                pb.em("gpsimd", lambda e: e.nop(), waits=[("ssz", pb.n("ssz"))])
                pb.em("gpsimd", lambda e: e.collective_compute(
                    "AllGather", ALU.bypass, replica_groups=rg,
                    ins=[io["SZ"].opt()], outs=[io["SZG"].opt()]), inc="cc")
            io["_mid_hook"] = _mid
            io["_preload"] = _preload
            with contextlib.ExitStack() as c0:
                emit_pass0(nc, pb, c0, io)
            for nm in ("wg", "wv1", "wd"):
                pre_tiles[nm] = c01.enter_context(nc.sbuf_tensor(nm + "_bf", [128, 8 * D], BF16))
            pb_barrier(pb, "bar0")
            for nm, ap in [("wg", io["WG"]), ("wv1", io["WV1"]), ("wd", io["WD"])]:
                preW.append(load_weight_bf16(pb, c01, nm, ap, (128, 8, D), prestage,
                                             dma_eng="gpsimd", cv_eng="vector", pre="p",
                                             tile=pre_tiles[nm]))
            io["_preW"] = preW
            io["_szg_waits"] = [("cc", pb.n("cc"))]
            with contextlib.ExitStack() as c1:
                emit_pass1(nc, pb, c1, io)
        pb_barrier(pb, "bar1")
        pb.em("gpsimd", lambda e: e.collective_compute(
            "AllGather", ALU.bypass, replica_groups=rg,
            ins=[io["AH"].opt()], outs=[io["AHG"].opt()]), inc="cc")
        io["_ahg_waits"] = [("cc", pb.n("cc"))]
        with contextlib.ExitStack() as c2:
            emit_pass2(nc, pb, c2, io)
        pb.build(ctx)
    return nc


def host_inputs_fused(inputs, core):
    h0 = host_inputs_pass0(inputs, core)
    b, q = core // 4, core % 4
    mkv = np.zeros((128, 16), np.float32)
    for r in range(4):
        pred = r < q
        mkv[:, r] = 1.0 if pred else 0.0
        mkv[:, 8 + r] = 0.0 if pred else 1.0
    col8 = lambda v: np.asarray(v, np.float32).reshape(8, 128).T.copy()
    h0.update({
        "MK": mkv,
        "MASKUT": np.tril(np.ones((128, 128), np.float32)).T.copy(),
        "WQ": np.asarray(inputs["L0_attn_Wq"], np.float32),
        "WO": np.asarray(inputs["L0_attn_Wo"], np.float32),
        "WG": np.asarray(inputs["L1_attn_Wg"], np.float32),
        "WV1": np.asarray(inputs["L1_attn_Wv"], np.float32),
        "WD": np.asarray(inputs["L1_attn_Wd"], np.float32),
        "BGC": col8(inputs["L1_attn_bg"]), "BVC": col8(inputs["L1_attn_bv"]),
        "BDC": col8(inputs["L1_attn_bd"]),
        "W1": np.asarray(inputs["L1_ffn_W1"], np.float32),
        "B1C": np.asarray(inputs["L1_ffn_b1"], np.float32).reshape(32, 128).T.copy(),
        "W2": np.asarray(inputs["L1_ffn_W2"], np.float32),
        "B2C": np.asarray(inputs["L1_ffn_b2"], np.float32).reshape(8, 128).T.copy(),
    })
    return h0


def kernel(**inputs):
    nc = build_fused()
    in_maps = [host_inputs_fused(inputs, c) for c in range(NCORES)]
    rr = run_bass_kernel_spmd(nc, in_maps, core_ids=list(range(NCORES))).results
    out = np.empty((2, 16384, 1024), np.float32)
    for c in range(NCORES):
        b, q = c // 4, c % 4
        o = np.asarray(rr[c]["OUT"]).reshape(128, NCH, TCORE)
        out[b, q * TCORE:(q + 1) * TCORE, :] = o.transpose(2, 1, 0).reshape(TCORE, D)
    return out


def kernel_unfused(**inputs):
    nc0 = build_pass0()
    in_maps0 = [host_inputs_pass0(inputs, c) for c in range(NCORES)]
    r0 = run_bass_kernel_spmd(nc0, in_maps0, core_ids=list(range(NCORES))).results
    nc1 = build_pass1()
    in_maps1 = [host_inputs_pass1(inputs, c, r0) for c in range(NCORES)]
    r1 = run_bass_kernel_spmd(nc1, in_maps1, core_ids=list(range(NCORES))).results
    nc2 = build_pass2()
    in_maps2 = [host_inputs_pass2(inputs, c, r1) for c in range(NCORES)]
    r2 = run_bass_kernel_spmd(nc2, in_maps2, core_ids=list(range(NCORES))).results
    out = np.empty((2, 16384, 1024), np.float32)
    for c in range(NCORES):
        b, q = c // 4, c % 4
        o = np.asarray(r2[c]["OUT"]).reshape(128, NCH, TCORE)
        out[b, q * TCORE:(q + 1) * TCORE, :] = o.transpose(2, 1, 0).reshape(TCORE, D)
    return out


# version 21 (final)
# speedup vs baseline: 1.2566x; 1.0376x over previous
import contextlib
import numpy as np
import ml_dtypes
import concourse.bass as bass
import concourse.mybir as mybir
from concourse.bass_utils import run_bass_kernel_spmd

F32 = mybir.dt.float32
BF16 = mybir.dt.bfloat16
AF = mybir.ActivationFunctionType
ALU = mybir.AluOpType

NCORES = 8
D = 1024
NCH = 8
TCORE = 4096
H = 16
HD = 64
LN_EPS = 1e-5
ATT_EPS = 1e-6
BF = ml_dtypes.bfloat16

ENGINES = ("sync", "scalar", "vector", "gpsimd", "tensor")


class PB:

    def __init__(self, nc):
        self.nc = nc
        self.items = {e: [] for e in ENGINES}
        self.cnt = {}

    def n(self, name):
        return self.cnt.get(name, 0)

    def em(self, eng, fn, waits=(), inc=None, inc_n=1):
        w = [(s, v) for (s, v) in waits if v > 0]
        self.items[eng].append((w, fn, inc, inc_n))
        if inc is not None:
            self.cnt[inc] = self.cnt.get(inc, 0) + inc_n
        return self.cnt[inc] if inc is not None else None

    def dma(self, eng, out, in_, inc, waits=()):
        return self.em(eng, lambda e, o=out, i=in_: e.dma_start(out=o, in_=i),
                       waits=waits, inc=inc, inc_n=16)

    def drain(self, eng):
        self.em(eng, lambda e: e.drain())

    def build(self, ctx):
        sems = {name: ctx.enter_context(self.nc.semaphore(name))
                for name in self.cnt}
        block = ctx.enter_context(self.nc.Block())

        def mk(engname):
            def run(engine):
                for waits, fn, inc, inc_n in self.items[engname]:
                    for (s, v) in waits:
                        engine.wait_ge(sems[s], v)
                    instr = fn(engine)
                    if inc is not None:
                        instr.then_inc(sems[inc], inc_n)
            return run

        block.sync(mk("sync"))
        block.scalar(mk("scalar"))
        block.vector(mk("vector"))
        block.gpsimd(mk("gpsimd"))
        block.tensor(mk("tensor"))


def load_weight_bf16(pb, ctx, name, dram_ap, shape3, stage,
                     dma_eng="scalar", cv_eng="vector", pre="w", tile=None):
    nc = pb.nc
    p, kc, o = shape3
    t = tile if tile is not None else ctx.enter_context(
        nc.sbuf_tensor(name + "_bf", [p, kc * o], BF16))
    tv = t[:].rearrange("p (k o) -> p k o", k=kc) if kc > 1 else None
    src = dram_ap.rearrange("(k p) o -> p k o", p=p) if kc > 1 else dram_ap
    scols = stage.shape[1] // 2
    done = 0
    scv = pre + "cv"
    i = 0
    for k in range(kc):
        off = 0
        while off < o:
            w = min(scols, o - off)
            par = i % 2
            half = par * scols
            sld = pre + "ld%d" % par
            s_ap = src[:, k, off:off + w] if kc > 1 else dram_ap[:, off:off + w]
            d_ap = tv[:, k, off:off + w] if kc > 1 else t[:, off:off + w]
            nld = pb.dma(dma_eng, stage[0:p, half:half + w], s_ap, inc=sld,
                         waits=[(scv, pb.n(scv) if i == 0 else max(0, pb.n(scv) - 1))])
            done = pb.em(cv_eng, lambda e, dst=d_ap, sp=stage[0:p, half:half + w]:
                         e.tensor_copy(dst, sp),
                         waits=[(sld, nld)], inc=scv)
            off += w
            i += 1
    return t, done


P0_IN = ["X", "WK", "WV", "WIN", "EB", "W1F", "B1F", "W2BD", "WOUT", "BIN",
         "B2", "BOUTC", "I64", "CONSTC", "ONESR"]
P0_OUT = ["XN", "KFM", "VTM", "G0", "SZ", "SNP"]
P0_SHAPES = {
    "X": ([D, TCORE], F32), "WK": ([D, D], F32), "WV": ([D, D], F32),
    "WIN": ([D, 64], F32), "EB": ([64, D], F32), "W1F": ([128, 8], F32),
    "B1F": ([128, 8], F32), "W2BD": ([D, 64], F32), "WOUT": ([64, D], F32),
    "BIN": ([64, 1], F32), "B2": ([64, 1], F32), "BOUTC": ([128, 8], F32),
    "I64": ([128, 128], F32), "CONSTC": ([128, 4], F32), "ONESR": ([1, 128], F32),
    "XN": ([128, NCH * TCORE], BF16), "KFM": ([128, NCH * TCORE], BF16),
    "VTM": ([128, (TCORE // 128) * 1040], BF16), "G0": ([128, NCH * TCORE], BF16),
    "SZ": ([64, 1040], F32), "SNP": ([64, (TCORE // 128) * 1040], F32),
}


def emit_pass0(nc, pb, ctx, io, ST=512, STAGE=99, NSTLIM=99):
    NST = min(TCORE // ST, NSTLIM)
    CH_T = ST // 128
    (X, WK, WV, WIN, EB, W1F, B1F, W2BD, WOUT, BIN, B2, BOUTC, I64, CONSTC,
     ONESR) = [io[k] for k in P0_IN]
    XN, KFM, VTM, G0, SZ, SNP = [io[k] for k in P0_OUT]
    SZr = SZ.rearrange("p (g k q e) -> q p g k e", g=2, k=4, q=2, e=65)
    SNPr = SNP.rearrange("p (cg g k q e) -> cg q p g k e", cg=TCORE // 128, g=2, k=4, q=2, e=65)

    XNv = XN.rearrange("p (c t) -> p c t", c=NCH)
    KFMv = KFM.rearrange("p (c t) -> p c t", c=NCH)
    G0v = G0.rearrange("p (c t) -> p c t", c=NCH)
    VTMv = VTM.rearrange("p (c o) -> p c o", c=TCORE // 128)
    Xv = X.rearrange("(c p) t -> p c t", p=128)

    if True:
        sb = lambda nm, shp, dt=F32: ctx.enter_context(nc.sbuf_tensor(nm, shp, dt))
        ps = lambda nm, shp, dt=F32: ctx.enter_context(nc.psum_tensor(nm, shp, dt))

        stage = sb("stage", [128, 2048])
        xin = [sb("xin0", [128, NCH * ST])] * 2
        xbf = sb("xbf", [128, NCH * ST], BF16)
        xsq = sb("xsq", [128, NCH * ST], BF16)
        xnf = sb("xnf", [128, NCH * ST])
        xnb = sb("xnb", [128, NCH * ST], BF16)
        kfm = sb("kfm", [128, NCH * ST], BF16)
        efp = sb("efp", [128, ST])
        em1 = sb("em1", [128, ST])
        ktm = sb("ktm", [128, CH_T * 1024], BF16)
        vtm = sb("vtm", [128, CH_T * 1040], BF16)
        sfb = sb("sfb", [64, ST], BF16)
        hfb = sb("hfb", [128, NCH * ST], BF16)
        gfb = sb("gfb", [64, ST], BF16)
        g0b = sb("g0b", [128, NCH * ST], BF16)
        rows = sb("rows", [1, 4 * ST])
        statef = sb("statef", [128, 520])
        inv1024 = sb("inv1024", [128, 1], BF16)
        cns = sb("cns", [128, 4])
        ones1f = sb("ones1f", [1, 128])
        i64b = sb("i64b", [128, 128], BF16)
        i64f = sb("i64f", [128, 128])
        binc = sb("binc", [64, 1])
        b2c = sb("b2c", [64, 1])
        boutc = sb("boutc", [128, 8])
        w1f = sb("w1f", [128, 8])
        b1f = sb("b1f", [128, 8])

        p0 = ps("p0", [128, 512])
        p1 = ps("p1", [128, 512])
        p2 = ps("p2", [128, 512])
        p3 = ps("p3", [128, 512])
        p4 = ps("p4", [128, 1024], BF16)
        pst = [ps("pst0", [128, 512]), ps("pst1", [128, 512])]

        xbv = xbf[:].rearrange("p (c t) -> p c t", c=NCH)
        xqv = xsq[:].rearrange("p (c t) -> p c t", c=NCH)
        xnfv = xnf[:].rearrange("p (c t) -> p c t", c=NCH)
        xnbv = xnb[:].rearrange("p (c t) -> p c t", c=NCH)
        kfv = kfm[:].rearrange("p (c t) -> p c t", c=NCH)
        hfv = hfb[:].rearrange("p (c t) -> p c t", c=NCH)
        g0v = g0b[:].rearrange("p (c t) -> p c t", c=NCH)
        ktv = ktm[:].rearrange("p (c o) -> p c o", c=CH_T)
        vtv = vtm[:].rearrange("p (c o) -> p c o", c=CH_T)

        pb.dma("sync", cns[:], CONSTC, inc="ldc")
        pb.dma("sync", i64f[:], I64, inc="ldc")
        pb.dma("sync", ones1f[:], ONESR, inc="ldc")
        for dst, srcap in [(binc, BIN), (b2c, B2), (boutc, BOUTC), (w1f, W1F), (b1f, B1F)]:
            pb.dma("sync", dst[:], srcap, inc="ldc")
        LDC = pb.n("ldc")
        pb.em("scalar", lambda e: e.activation(out=inv1024[:], in_=cns[:, 0:1], func=AF.Copy),
              waits=[("ldc", LDC)], inc="act")
        cv_i64 = pb.em("scalar", lambda e: e.activation(out=i64b[:], in_=i64f[:], func=AF.Copy),
                       inc="act")
        ld_ones = LDC
        ld_small = LDC

        wk, cv_wk = load_weight_bf16(pb, ctx, "wk", WK, (128, 8, D), stage)
        wv, cv_wv = load_weight_bf16(pb, ctx, "wv", WV, (128, 8, D), stage)
        win, cv_win = load_weight_bf16(pb, ctx, "win", WIN, (128, 8, 64), stage)
        eb, cv_eb = load_weight_bf16(pb, ctx, "eb", EB, (64, 1, D), stage)
        w2bd, cv_bd = load_weight_bf16(pb, ctx, "w2bd", W2BD, (128, 8, 64), stage)
        wout, cv_wo = load_weight_bf16(pb, ctx, "wout", WOUT, (64, 1, D), stage)
        if io.get("_preload") is not None:
            io["_preload"](pb)
        wkv = wk[:].rearrange("p (k o) -> p k o", k=8)
        wvv = wv[:].rearrange("p (k o) -> p k o", k=8)
        winv = win[:].rearrange("p (k o) -> p k o", k=8)
        w2bdv = w2bd[:].rearrange("p (k o) -> p k o", k=8)

        n_ones = pb.em("vector", lambda e: e.memset(
            vtm[:].rearrange("p (c h o) -> p c h o", c=CH_T, h=16)[:, :, :, 64:65], 1.0), inc="dve")
        pb.drain("vector")
        zc = ctx.enter_context(nc.sbuf_tensor("zc", [1, 640], BF16))
        nz = pb.em("vector", lambda e: e.memset(zc[:], 0.0), inc="dve")
        pb.drain("vector")
        nz = pb.em("vector", lambda e: e.memset(statef[:], 0.0), inc="dve")
        pb.drain("vector")

        first_m4 = [True, True, True]

        def state_slot(h):
            g, s_ = h // 8, h % 8
            return g, 64 * (s_ % 2), 65 * (s_ // 2)

        mark = {"xn_st": 0, "kfm_st": 0, "vtm_st": 0, "g0_st": 0,
                "xn_dve": 0, "state_dve": nz, "ktm_act": 0, "ffn_pe": 0,
                "ffnb_pe_0": 0, "ffnb_pe_1": 0}

        def front0(s):
            T0 = ST * s
            T0 = ST * s
            xb = xin[s % 2]
            xv = xb[:].rearrange("p (c t) -> p c t", c=NCH)
            lx = "lx%d" % (s % 2)
            nld = pb.dma("sync", xv[:, :, :], Xv[:, :, T0:T0 + ST], inc=lx,
                         waits=[("dve", mark["xn_dve"])])
            na1 = pb.em("scalar", lambda e, xb=xb: e.activation(out=xbf[:], in_=xb[:], func=AF.Copy),
                        waits=[(lx, nld), ("pe", pb.n("pe"))], inc="act")
            na2 = pb.em("scalar", lambda e, xb=xb: e.activation(out=xsq[:], in_=xb[:], func=AF.Square),
                        inc="act")
            for ch in range(NCH):
                pb.em("tensor", lambda e, ch=ch: e.matmul(
                    p2[0:1, 0:ST], inv1024[:], xbv[:, ch, :], start=(ch == 0), stop=(ch == NCH - 1)),
                    waits=([("act", na1), ("dve", pb.n("dve"))] if ch == 0 else []), inc="pe")
            for ch in range(NCH):
                npe_st = pb.em("tensor", lambda e, ch=ch: e.matmul(
                    p3[0:1, 0:ST], inv1024[:], xqv[:, ch, :], start=(ch == 0), stop=(ch == NCH - 1)),
                    waits=([("act", na2)] if ch == 0 else []), inc="pe")
            m_row = rows[0:1, 0:ST]
            r_row = rows[0:1, ST:2 * ST]
            v_row = rows[0:1, 2 * ST:3 * ST]
            t_row = rows[0:1, 3 * ST:4 * ST]
            na_m = pb.em("scalar", lambda e: e.activation(out=m_row, in_=p2[0:1, 0:ST], func=AF.Copy),
                         waits=[("pe", npe_st)], inc="act")
            pb.em("vector", lambda e: e.tensor_tensor(out=t_row, in0=m_row, in1=m_row, op=ALU.mult),
                  waits=[("act", na_m)], inc="dve")
            pb.drain("vector")
            nd_var = pb.em("vector", lambda e: e.tensor_tensor(out=v_row, in0=p3[0:1, 0:ST], in1=t_row, op=ALU.subtract),
                           inc="dve")
            pb.em("scalar", lambda e: e.activation(out=t_row, in_=v_row, func=AF.Ln, bias=cns[0:1, 2:3]),
                  waits=[("ldc", ld_ones), ("dve", nd_var)], inc="act")
            pb.drain("scalar")
            na_rows = pb.em("scalar", lambda e: e.activation(out=r_row, in_=t_row, func=AF.Exp, scale=-0.5),
                            inc="act")
            pb.em("tensor", lambda e: e.matmul(p2[:, 0:ST], ones1f[:], m_row, start=True, stop=True),
                  waits=[("act", na_rows), ("ldc", ld_ones)], inc="pe")
            npe_bc = pb.em("tensor", lambda e: e.matmul(p3[:, 0:ST], ones1f[:], r_row, start=True, stop=True),
                           inc="pe")
            for ch in range(NCH):
                pb.em("vector", lambda e, ch=ch, xv=xv: e.tensor_tensor(
                    out=xnfv[:, ch, :], in0=xv[:, ch, :], in1=p2[:, 0:ST], op=ALU.subtract),
                    waits=([("pe", npe_bc)] if ch == 0 else []), inc="dve")
            pb.drain("vector")
            xn_ch = {}
            for ch in range(NCH):
                nd_xn = pb.em("vector", lambda e, ch=ch: e.tensor_tensor(
                    out=xnbv[:, ch, :], in0=xnfv[:, ch, :], in1=p3[:, 0:ST], op=ALU.mult),
                    waits=([("sxn", mark["xn_st"]), ("pe", pb.n("pe"))] if ch == 0 else []), inc="dve")
                xn_ch[ch] = nd_xn
            mark["xn_dve"] = nd_xn
            mark["xn_st"] = pb.dma("gpsimd", XNv[:, :, T0:T0 + ST], xnbv[:, :, :], inc="sxn",
                                   waits=[("dve", nd_xn)])
            return xn_ch

        xn_ch = front0(0)
        for s in range(NST):
            T0 = ST * s
            if STAGE < 2:
                continue
            kevac = {}
            for mc in range(NCH):
                bank = [p0, p1][mc % 2]
                if mc in (0, 1):
                    w0 = [("wcv", cv_wk), ("act", pb.n("act"))]
                else:
                    w0 = [("dve", kevac[mc - 2])]
                for kc in range(NCH):
                    wk_ = (w0 if kc == 0 else [])
                    if mc in (0, 1):
                        wk_ = wk_ + [("dve", xn_ch[kc])]
                    npe = pb.em("tensor", lambda e, kc=kc, mc=mc, bank=bank: e.matmul(
                        bank[:, 0:ST], wkv[:, kc, 128 * mc:128 * mc + 128], xnbv[:, kc, :],
                        start=(kc == 0), stop=(kc == NCH - 1)),
                        waits=wk_, inc="pe")
                na = pb.em("scalar", lambda e, bank=bank: e.activation(out=efp[:], in_=bank[:, 0:ST], func=AF.Exp),
                           waits=[("pe", npe), ("dve", kevac.get(mc - 1, 0))], inc="act")
                na = pb.em("scalar", lambda e, bank=bank: e.activation(out=em1[:], in_=bank[:, 0:ST], func=AF.Relu),
                           inc="act")
                krelu = krelu if mc else {}
                krelu[mc] = na
                kevac[mc] = pb.em("vector", lambda e, mc=mc: e.scalar_tensor_tensor(
                    out=kfv[:, mc, :], in0=efp[:], scalar=1.0, in1=em1[:],
                    op0=ALU.min, op1=ALU.add),
                    waits=[("act", na), ("skf", mark["kfm_st"]) if mc == 0 else ("act", 0)], inc="dve")
            mark["kfm_st"] = pb.dma("gpsimd", KFMv[:, :, T0:T0 + ST], kfv[:, :, :], inc="skf",
                                    waits=[("dve", kevac[NCH - 1])])
            if STAGE < 3:
                continue
            for c in range(CH_T):
                for j in range(NCH):
                    trw = [("dve", kevac[j])] if c == 0 else []
                    if j == 0:
                        trw = trw + [("act", mark["ktm_act"])]
                    npe = pb.em("tensor", lambda e, c=c, j=j: e.transpose(
                        p4[:, 128 * j:128 * j + 128],
                        kfv[:, j, 128 * c:128 * c + 128],
                        i64b[:]),
                        waits=trw, inc="pe")
                mark["ktm_act"] = pb.em("scalar", lambda e, c=c: e.activation(
                    out=ktv[:, c, :], in_=p4[:, 0:1024], func=AF.Copy),
                    waits=[("pe", npe)], inc="act")
            if STAGE < 4:
                continue
            vevac = {}
            for c in range(CH_T):
                for half in range(2):
                    bank = [p0, p1][half]
                    if c == 0:
                        w0 = [("wcv", cv_wv), ("dve", kevac[NCH - 1]), ("act", pb.n("act"))]
                    else:
                        w0 = [("act", vevac[(c - 1, half)])]
                    for kc in range(NCH):
                        npe = pb.em("tensor", lambda e, kc=kc, c=c, half=half, bank=bank: e.matmul(
                            bank[:, 0:512],
                            xnbv[:, kc, 128 * c:128 * c + 128],
                            wvv[:, kc, 512 * half:512 * half + 512],
                            start=(kc == 0), stop=(kc == NCH - 1)),
                            waits=(w0 if kc == 0 else []), inc="pe")
                    vevac[(c, half)] = pb.em("scalar", lambda e, c=c, half=half, bank=bank: e.activation(
                        out=vtv[:, c, :].rearrange("p (h o) -> p h o", h=16)[:, 8 * half:8 * half + 8, 0:64],
                        in_=bank[:, 0:512].rearrange("p (h o) -> p h o", h=8),
                        func=AF.Copy),
                        waits=[("pe", npe)] + ([("svt", mark["vtm_st"]), ("dve", n_ones)] if (c == 0 and half == 0) else []),
                        inc="act")
            mark["vtm_st"] = pb.dma("gpsimd", VTMv[:, CH_T * s:CH_T * (s + 1), :], vtv[:, :, :],
                                    inc="svt", waits=[("act", pb.n("act"))])
            if STAGE < 5:
                continue
            for c in range(CH_T):
                cg = CH_T * s + c
                n_sn = 0
                for q in range(2):
                    n_sn = pb.dma(
                        "gpsimd", SNPr[cg, q],
                        statef[64 * q:64 * q + 64, :].rearrange("p (g k e) -> p g k e", g=2, k=4),
                        inc="ssn", waits=([("dve", mark["state_dve"])] if q == 0 else []))
                for g in range(2):
                    pb.em("tensor", lambda e, g=g: e.matmul(
                        pst[g][:, 0:512], zc[0:1, 0:128], zc[0:1, 128:640],
                        start=True, stop=False),
                        waits=([("dve", mark["state_dve"]), ("act", pb.n("act"))] if g == 0 else []),
                        inc="pe")
                for h in range(H):
                    g, prow, pcol = state_slot(h)
                    pb.em("tensor", lambda e, c=c, h=h, g=g, prow=prow, pcol=pcol: e.matmul(
                        pst[g][prow:prow + 64, pcol:pcol + 65],
                        ktv[:, c, 64 * h:64 * h + 64],
                        vtv[:, c, 65 * h:65 * h + 65],
                        start=False, stop=False), inc="pe")
                npe_cl = 0
                for g in range(2):
                    npe_cl = pb.em("tensor", lambda e, g=g: e.matmul(
                        pst[g][:, 0:512], zc[0:1, 0:128], zc[0:1, 128:640],
                        start=False, stop=True), inc="pe")
                for g in range(2):
                    mark["state_dve"] = pb.em("vector", lambda e, g=g: e.tensor_tensor(
                        out=statef[:, 260 * g:260 * g + 260], in0=pst[g][:, 0:260],
                        in1=statef[:, 260 * g:260 * g + 260], op=ALU.add),
                        waits=([("pe", npe_cl), ("ssn", n_sn)] if g == 0 else []), inc="dve")
                pb.drain("vector")
            if s + 1 < NST:
                xn_ch = front0(s + 1)
            pass


        for q in range(2):
            pb.dma("gpsimd", SZr[q],
                   statef[64 * q:64 * q + 64, :].rearrange("p (g k e) -> p g k e", g=2, k=4),
                   inc="ssz", waits=([("dve", mark["state_dve"])] if q == 0 else []))
        if io.get("_mid_hook") is not None:
            io["_mid_hook"](pb)
        xnb2vs = [xnbv, kfv]
        if STAGE >= 6:
          for s in range(NST):
            T0 = ST * s
            xnb2v = xnb2vs[s % 2]
            if s < 2:
                wlb = [("pe", pb.n("pe")), ("sxn", pb.n("sxn")), ("skf", pb.n("skf"))]
            else:
                wlb = [("pe", mark["ffnb_pe_%d" % (s % 2)])]
            nld_b = pb.dma("sync", xnb2v[:, :, :], XNv[:, :, T0:T0 + ST],
                           inc="lxb%d" % (s % 2), waits=wlb)
            lxb_sem = "lxb%d" % (s % 2)
            for kc in range(NCH):
                npe = pb.em("tensor", lambda e, kc=kc, xnb2v=xnb2v: e.matmul(
                    p2[0:64, 0:ST], winv[:, kc, :], xnb2v[:, kc, :],
                    start=(kc == 0), stop=(kc == NCH - 1)),
                    waits=([("wcv", cv_win), ("dve", pb.n("dve")), (lxb_sem, nld_b)] if kc == 0 else []), inc="pe")
            na_s = pb.em("vector", lambda e: e.tensor_scalar(out=sfb[:], in0=p2[0:64, 0:ST], scalar1=binc[:], scalar2=None, op0=ALU.add),
                         waits=[("pe", npe), ("ldc", ld_small)], inc="dve")
            hevac = {}
            for ch in range(NCH):
                bank = [p0, p1][ch % 2]
                npe = pb.em("tensor", lambda e, ch=ch, bank=bank: e.matmul(
                    bank[:, 0:ST], eb[0:64, 128 * ch:128 * ch + 128], sfb[:],
                    start=True, stop=True),
                    waits=([("dve", na_s), ("wcv", cv_eb), ("act", pb.n("act"))] if ch < 2 else [("act", hevac[ch - 2])]), inc="pe")
                hevac[ch] = pb.em("scalar", lambda e, ch=ch, bank=bank: e.activation(
                    out=hfv[:, ch, :], in_=bank[:, 0:ST], func=AF.Relu,
                    scale=w1f[:, ch:ch + 1], bias=b1f[:, ch:ch + 1]),
                    waits=[("pe", npe)], inc="act")
            for ch in range(NCH):
                npe = pb.em("tensor", lambda e, ch=ch: e.matmul(
                    p2[0:64, 0:ST], w2bdv[:, ch, :], hfv[:, ch, :],
                    start=(ch == 0), stop=(ch == NCH - 1)),
                    waits=([("act", hevac[NCH - 1]), ("wcv", cv_bd)] if ch == 0 else []), inc="pe")
            na_g = pb.em("vector", lambda e: e.tensor_scalar(out=gfb[:], in0=p2[0:64, 0:ST], scalar1=b2c[:], scalar2=None, op0=ALU.add),
                         waits=[("pe", npe)], inc="dve")
            gevac = {}
            for ch in range(NCH):
                bank = [p0, p1][ch % 2]
                npe = pb.em("tensor", lambda e, ch=ch, bank=bank: e.matmul(
                    bank[:, 0:ST], wout[0:64, 128 * ch:128 * ch + 128], gfb[:],
                    start=True, stop=True),
                    waits=([("dve", na_g), ("wcv", cv_wo)] if ch < 2 else [("act", gevac[ch - 2])]), inc="pe")
                gevac[ch] = pb.em("scalar", lambda e, ch=ch, bank=bank: e.activation(
                    out=g0v[:, ch, :], in_=bank[:, 0:ST], func=AF.Silu, bias=boutc[:, ch:ch + 1]),
                    waits=[("pe", npe)] + ([("sg0", mark["g0_st"])] if ch == 0 else []), inc="act")
            mark["g0_st"] = pb.dma("gpsimd", G0v[:, :, T0:T0 + ST], g0v[:, :, :], inc="sg0",
                                   waits=[("act", gevac[NCH - 1])])
            mark["ffnb_pe_%d" % (s % 2)] = pb.n("pe")

        pb.em("sync", lambda e: e.nop(), waits=[(snm, pb.n(snm)) for snm in
              ("sxn", "skf", "svt", "sg0", "ssz", "ssn") if pb.n(snm) > 0])


def build_pass0(ST=512, STAGE=99, NSTLIM=99):
    nc = bass.Bass("TRN2", target_bir_lowering=False)
    io = {k: nc.dram_tensor(k, *P0_SHAPES[k], kind="ExternalInput").ap() for k in P0_IN}
    io.update({k: nc.dram_tensor(k, *P0_SHAPES[k], kind="ExternalOutput").ap() for k in P0_OUT})
    pb = PB(nc)
    with contextlib.ExitStack() as ctx:
        emit_pass0(nc, pb, ctx, io, ST=ST, STAGE=STAGE, NSTLIM=NSTLIM)
        pb.build(ctx)
    return nc



P1_IN = ["X", "XN", "KFM", "VTM", "G0", "SNP", "SZG", "MK", "WQ", "WO", "WG",
         "WV1", "WD", "BGC", "BVC", "BDC", "MASKUT", "I128", "CONSTC", "ONESR"]
P1_OUT = ["X1", "XN2", "S1", "PP", "AH"]
P1_SHAPES = {
    "X": ([D, TCORE], F32), "XN": ([128, NCH * TCORE], BF16),
    "KFM": ([128, NCH * TCORE], BF16), "VTM": ([128, (TCORE // 128) * 1040], BF16),
    "G0": ([128, NCH * TCORE], BF16), "SNP": ([64, (TCORE // 128) * 1040], F32),
    "SZG": ([256, 1040], F32), "MK": ([128, 16], F32),
    "WQ": ([D, D], F32), "WO": ([D, D], F32), "WG": ([D, D], F32),
    "WV1": ([D, D], F32), "WD": ([D, D], F32),
    "BGC": ([128, 8], F32), "BVC": ([128, 8], F32), "BDC": ([128, 8], F32),
    "MASKUT": ([128, 128], F32), "I128": ([128, 128], F32),
    "CONSTC": ([128, 4], F32), "ONESR": ([1, 128], F32),
    "X1": ([128, NCH * TCORE], F32), "XN2": ([128, NCH * TCORE], BF16),
    "S1": ([128, NCH * TCORE], F32), "PP": ([128, NCH * TCORE], F32),
    "AH": ([128, 16], F32),
}


def emit_pass1(nc, pb, ctx, io, ST=256, STAGE=99, NSTLIM=99):
    NST = min(TCORE // ST, NSTLIM)
    CH_T = ST // 128
    NCHK = TCORE // 128
    (X, XN, KFM, VTM, G0, SNP, SZG, MK, WQ, WO, WG, WV1, WD, BGC, BVC, BDC,
     MASKUT, I128, CONSTC, ONESR) = [io[k] for k in P1_IN]
    X1, XN2, S1, PP, AH = [io[k] for k in P1_OUT]
    WCV = "pcv" if io.get("_preW") is not None else "wcv"

    Xv = X.rearrange("(c p) t -> p c t", p=128)
    XNv = XN.rearrange("p (c t) -> p c t", c=NCH)
    KFMh = KFM.rearrange("(q p) (c t) -> p c q t", q=2, p=64, c=NCH)
    VTMv = VTM.rearrange("p (c o) -> p c o", c=NCHK)
    SNPv = SNP.rearrange("p (c o) -> p c o", c=NCHK)
    G0v = G0.rearrange("p (c t) -> p c t", c=NCH)
    X1v = X1.rearrange("p (c t) -> p c t", c=NCH)
    XN2v = XN2.rearrange("p (c t) -> p c t", c=NCH)
    S1v = S1.rearrange("p (c t) -> p c t", c=NCH)
    PPv = PP.rearrange("p (c t) -> p c t", c=NCH)

    if True:
        sb = lambda nm, shp, dt=F32: ctx.enter_context(nc.sbuf_tensor(nm + "_1", shp, dt))
        ps = lambda nm, shp, dt=F32: ctx.enter_context(nc.psum_tensor(nm + "_1", shp, dt))

        stage = None if io.get("_preW") is not None else sb("stage", [128, 512])
        xin = sb("xin", [128, NCH * ST])
        xnb = sb("xnb", [128, NCH * ST], BF16)
        kfma = sb("kfma", [64, H * 128], BF16)
        kfmb = sb("kfmb", [64, H * 128], BF16)
        vtm = sb("vtm", [128, CH_T * 1040], BF16)
        vtm2 = sb("vtm2", [128, CH_T * 1040], BF16)
        g0b = sb("g0b", [128, NCH * ST], BF16)
        qfma = sb("qfma", [64, H * ST], BF16)
        szg = sb("szg", [64, 1040])
        sprev = sb("sprev", [64, 1040])
        snp = sprev
        snapb = sb("snapb", [64, 1040], BF16)
        sprevb = sb("sprevb", [64, 1040], BF16)
        smbf = sb("smbf", [128, 2 * 256], BF16)
        ytm = sb("ytm", [128, 1024], BF16)
        efp = ytm[0:64, 0:2 * ST]
        em1 = ytm[0:64, 2 * ST:4 * ST]
        yfm = sb("yfm", [128, NCH * ST], BF16)
        denr = sb("denr", [128, 16])
        x1f = sb("x1f", [128, NCH * ST])
        xn2b = sb("xn2b", [128, NCH * ST], BF16)
        x1b = xn2b
        x1q = yfm
        tg = sb("tg", [128, NCH * ST])
        tv = sb("tv", [128, NCH * ST])
        td = sb("td", [128, NCH * ST])
        gbuf = sb("gbuf", [128, NCH * ST])
        s1b = sb("s1b", [128, NCH * ST])
        ppb = sb("ppb", [128, NCH * ST])
        hprev = sb("hprev", [128, 8])
        pprev = sb("pprev", [128, 8])
        zer = sb("zer", [128, ST])
        zcol = sb("zcol", [128, 1])
        rows = szg
        cns = sb("cns", [128, 4])
        inv1024b = sb("inv1024b", [128, 1], BF16)
        ones1f = sb("ones1f", [1, 128])
        i128f = sb("i128f", [128, 128])
        i128b = sb("i128b", [128, 128], BF16)
        maskut = sb("maskut", [128, 256])
        bgc = sb("bgc", [128, 8])
        bvc = sb("bvc", [128, 8])
        bdc = sb("bdc", [128, 8])
        bgh = sb("bgh", [128, 8])
        bdh = sb("bdh", [128, 8])
        mk = sb("mk", [128, 16])
        ahsb = sb("ahsb", [128, 16])

        p0 = ps("p0", [128, 512])
        p1 = ps("p1", [128, 512])
        p2 = ps("p2", [128, 1024], BF16)
        pya = ps("pya", [128, 512])
        pyb = ps("pyb", [128, 512])
        pyc = ps("pyc", [128, 512])
        pyd = ps("pyd", [128, 512])

        xv = xin[:].rearrange("p (c t) -> p c t", c=NCH)
        xnv = xnb[:].rearrange("p (c t) -> p c t", c=NCH)
        kfvs = [kfma[:].rearrange("p (h t) -> p h t", h=H),
                kfmb[:].rearrange("p (h t) -> p h t", h=H)]
        kfbufs = [kfma, kfmb]
        qfv = qfma[:].rearrange("p (h t) -> p h t", h=H)
        vtvs = [vtm[:].rearrange("p (c o) -> p c o", c=CH_T),
                vtm2[:].rearrange("p (c o) -> p c o", c=CH_T)]
        g0v = g0b[:].rearrange("p (c t) -> p c t", c=NCH)
        yfv = yfm[:].rearrange("p (c t) -> p c t", c=NCH)
        x1vv = x1f[:].rearrange("p (c t) -> p c t", c=NCH)
        x1bv = x1b[:].rearrange("p (c t) -> p c t", c=NCH)
        x1qv = x1q[:].rearrange("p (c t) -> p c t", c=NCH)
        xn2v = xn2b[:].rearrange("p (c t) -> p c t", c=NCH)
        tgv = tg[:].rearrange("p (c t) -> p c t", c=NCH)
        tvv = tv[:].rearrange("p (c t) -> p c t", c=NCH)
        tdv = td[:].rearrange("p (c t) -> p c t", c=NCH)
        gbv = gbuf[:].rearrange("p (c t) -> p c t", c=NCH)
        s1vv = s1b[:].rearrange("p (c t) -> p c t", c=NCH)
        ppvv = ppb[:].rearrange("p (c t) -> p c t", c=NCH)

        pb.dma("sync", cns[:], CONSTC, inc="ldc")
        pb.dma("sync", i128f[:], I128, inc="ldc")
        pb.dma("sync", ones1f[:], ONESR, inc="ldc")
        pb.dma("sync", maskut[:, 0:128], MASKUT, inc="ldc")
        pb.dma("sync", maskut[:, 128:256], MASKUT, inc="ldc")
        for dst, srcap in [(bgc, BGC), (bvc, BVC), (bdc, BDC), (mk, MK)]:
            pb.dma("sync", dst[:], srcap, inc="ldc")
        LDC = pb.n("ldc")
        pb.em("scalar", lambda e: e.activation(out=i128b[:], in_=i128f[:], func=AF.Copy),
              waits=[("ldc", LDC)], inc="act")
        pb.em("scalar", lambda e: e.activation(out=inv1024b[:], in_=cns[:, 0:1], func=AF.Copy),
              inc="act")
        nd = pb.em("vector", lambda e: e.tensor_scalar(out=bgh[:], in0=bgc[:], scalar1=0.5,
                   scalar2=None, op0=ALU.mult), waits=[("ldc", LDC)], inc="dve")
        nd = pb.em("vector", lambda e: e.tensor_scalar(out=bdh[:], in0=bdc[:], scalar1=0.5,
                   scalar2=None, op0=ALU.mult), inc="dve")
        nd = 0
        for r in range(4):
            nl = pb.dma("sync", szg[:], SZG[64 * r:64 * (r + 1), :], inc="lsz",
                        waits=[("dve", nd)] + (list(io.get("_szg_waits", [])) if r == 0 else []))
            if r == 0:
                nd = pb.em("vector", lambda e: e.tensor_scalar(
                    out=sprev[:], in0=szg[:], scalar1=mk[0:64, 0:1], scalar2=None, op0=ALU.mult),
                    waits=[("lsz", nl), ("ldc", LDC)], inc="dve")
            else:
                nd = pb.em("vector", lambda e, r=r: e.scalar_tensor_tensor(
                    out=sprev[:], in0=szg[:], scalar=mk[0:64, r:r + 1],
                    in1=sprev[:], op0=ALU.mult, op1=ALU.add),
                    waits=[("lsz", nl)], inc="dve")
            pb.drain("vector")
        na_spv = pb.em("scalar", lambda e: e.activation(out=sprevb[:], in_=sprev[:], func=AF.Copy),
                       waits=[("dve", nd)], inc="act")
        nd = pb.em("vector", lambda e: e.memset(zer[:], 0.0), inc="dve")
        nd = pb.em("vector", lambda e: e.memset(zcol[:], 0.0), inc="dve")
        nd = pb.em("vector", lambda e: e.memset(pprev[:], 1.0), inc="dve")
        nd_pro = pb.em("vector", lambda e: e.memset(hprev[:], 0.0), inc="dve")
        pb.drain("vector")

        if io.get("_preW") is not None:
            (wq, cv_wq), (wo, cv_wo), (wg, cv_wg), (wv1, cv_wv1), (wd, cv_wd) = io["_preW"]
        else:
            wq, cv_wq = load_weight_bf16(pb, ctx, "wq", WQ, (128, 8, D), stage)
            wo, cv_wo = load_weight_bf16(pb, ctx, "wo", WO, (128, 8, D), stage)
            wg, cv_wg = load_weight_bf16(pb, ctx, "wg", WG, (128, 8, D), stage)
            wv1, cv_wv1 = load_weight_bf16(pb, ctx, "wv1", WV1, (128, 8, D), stage)
            wd, cv_wd = load_weight_bf16(pb, ctx, "wd", WD, (128, 8, D), stage)
        wqv = wq[:].rearrange("p (k o) -> p k o", k=8)
        wov = wo[:].rearrange("p (k o) -> p k o", k=8)
        wgv = wg[:].rearrange("p (k o) -> p k o", k=8)
        wv1v = wv1[:].rearrange("p (k o) -> p k o", k=8)
        wdv = wd[:].rearrange("p (k o) -> p k o", k=8)

        mark = {"x_pool": 0, "xn_pe": 0, "g0_dve": 0, "mev_last": 0, "qrelu_last": 0,
                "m1pe_0": 0, "m1pe_1": 0, "vt_pe_0": 0, "vt_pe_1": 0,
                "snp_act": na_spv, "m3_pe": 0, "sx1": 0, "sxn2": 0, "ss1": 0,
                "spp": 0, "scan_dve": nd_pro, "hp_dve": nd_pro}

        def emit_front(s):
            T0 = ST * s
            T0 = ST * s
            n_xn = pb.dma("sync", xnv[:, :, :], XNv[:, :, T0:T0 + ST], inc="lxn",
                          waits=[("pe", mark["xn_pe"])])
            n_vt = pb.dma("sync", vtvs[s % 2][:, :, :], VTMv[:, CH_T * s:CH_T * (s + 1), :],
                          inc="lvt", waits=[("pe", mark["vt_pe_%d" % (s % 2)])])
            qevac = {}
            for pj in range(8):
                bank = [p0, p1][pj % 2]
                if pj in (0, 1):
                    w0 = [(WCV, cv_wq), ("lxn", n_xn), ("dve", mark["mev_last"]),
                          ("act", mark["qrelu_last"])]
                else:
                    w0 = [("dve", qevac[pj - 2])]
                for hh in range(2):
                    h_ = 2 * pj + hh
                    for kc in range(NCH):
                        npe = pb.em("tensor", lambda e, kc=kc, h_=h_, hh=hh, bank=bank: e.matmul(
                            bank[0:64, ST * hh:ST * hh + ST],
                            wqv[:, kc, 64 * h_:64 * h_ + 64], xnv[:, kc, :],
                            start=(kc == 0), stop=(kc == NCH - 1)),
                            waits=(w0 if (hh == 0 and kc == 0) else []), inc="pe")
                na = pb.em("scalar", lambda e, bank=bank: e.activation(
                    out=efp, in_=bank[0:64, 0:2 * ST], func=AF.Exp),
                    waits=[("pe", npe), ("dve", qevac.get(pj - 1, 0))], inc="act")
                na = pb.em("scalar", lambda e, bank=bank: e.activation(
                    out=em1, in_=bank[0:64, 0:2 * ST], func=AF.Relu),
                    inc="act")
                qrelu = qrelu if pj else {}
                qrelu[pj] = na
                qevac[pj] = pb.em("vector", lambda e, pj=pj: e.scalar_tensor_tensor(
                    out=qfv[:, 2 * pj:2 * pj + 2, :], in0=efp.rearrange(
                        "p (h t) -> p h t", h=2), scalar=1.0,
                    in1=em1.rearrange("p (h t) -> p h t", h=2),
                    op0=ALU.min, op1=ALU.add),
                    waits=[("act", na)] + ([("pe", mark["m3_pe"])] if pj == 0 else []), inc="dve")
            mark["xn_pe"] = pb.n("pe")
            mark["qrelu_last"] = qrelu[7]
            return dict(n_xn=n_xn, n_vt=n_vt, qevac=qevac, qrelu=qrelu)

        def emit_scans(s, T0, np3):
            pb.drain("vector")
            nds = pb.n("dve")
            for ch in range(NCH):
                init = zcol[:, 0:1] if s == 0 else hprev[:, ch:ch + 1]
                nds = pb.em("vector", lambda e, ch=ch, init=init: e.tensor_tensor_scan(
                    out=s1vv[:, ch, :], data0=tgv[:, ch, :], data1=tvv[:, ch, :],
                    initial=init, op0=ALU.mult, op1=ALU.add),
                    waits=([("pool", np3), ("ss1", mark["ss1"]), ("dve", mark["hp_dve"])] if ch == 0 else []),
                    inc="dve")
            for ch in range(NCH):
                init = cns[:, 1:2] if s == 0 else pprev[:, ch:ch + 1]
                nds = pb.em("vector", lambda e, ch=ch, init=init: e.tensor_tensor_scan(
                    out=ppvv[:, ch, :], data0=tgv[:, ch, :], data1=zer[:, 0:ST],
                    initial=init, op0=ALU.mult, op1=ALU.add),
                    waits=([("spp", mark["spp"])] if ch == 0 else []), inc="dve")
            pb.drain("vector")
            nd_h = pb.em("vector", lambda e: e.tensor_copy(
                hprev[:], s1vv[:, :, ST - 1:ST].rearrange("p c t -> p (c t)")), inc="dve")
            nd_p = pb.em("vector", lambda e: e.tensor_copy(
                pprev[:], ppvv[:, :, ST - 1:ST].rearrange("p c t -> p (c t)")), inc="dve")
            pb.drain("vector")
            mark["hp_dve"] = nd_p
            mark["scan_dve"] = nd_p
            mark["ss1"] = pb.dma("gpsimd", S1v[:, :, T0:T0 + ST], s1vv[:, :, :], inc="ss1",
                                 waits=[("dve", nd_p)])
            mark["spp"] = pb.dma("gpsimd", PPv[:, :, T0:T0 + ST], ppvv[:, :, :], inc="spp",
                                 waits=[("dve", nd_p)])
            if s == NST - 1:
                nd_ah = pb.em("vector", lambda e: e.tensor_copy(ahsb[:, 0:8], pprev[:]),
                              waits=[("dve", nd_p)], inc="dve")
                nd_ah = pb.em("vector", lambda e: e.tensor_copy(ahsb[:, 8:16], hprev[:]), inc="dve")
                pb.dma("gpsimd", AH, ahsb[:], inc="sah", waits=[("dve", nd_ah)])

        scan_args = None
        n_x = pb.dma("sync", xv[:, :, :], Xv[:, :, 0:ST], inc="lx",
                     waits=[("pool", 0)])
        n_g0 = pb.dma("sync", g0v[:, :, :], G0v[:, :, 0:ST], inc="lg0",
                      waits=[("dve", 0)])
        mark["kf_ld"] = ("lkf0", pb.dma("sync",
                               kfma[:].rearrange("p (c q t) -> p c q t", c=NCH, q=2),
                               KFMh[:, :, :, 0:128], inc="lkf0", waits=[]))
        fr = emit_front(0)
        for s in range(NST):
            T0 = ST * s
            n_xn, n_vt = fr["n_xn"], fr["n_vt"]
            qevac, qrelu = fr["qevac"], fr["qrelu"]
            if STAGE < 2:
                continue
            vtv_c = vtvs[s % 2]
            for c in range(CH_T):
                cg = CH_T * s + c
                T0c = T0 + 128 * c
                n_kf = mark["kf_ld"]
                kfv_c = kfvs[cg % 2]
                n_sn = pb.dma("sync", snp[:], SNPv[:, cg, :], inc="lsn",
                              waits=[("act", mark["snp_act"])])
                na_sn = pb.em("scalar", lambda e: e.activation(
                    out=snapb[:], in_=snp[:], func=AF.Copy),
                    waits=[("lsn", n_sn), ("pe", mark["m3_pe"])], inc="act")
                mark["snp_act"] = na_sn
                mevac = {}

                def emit_m1_pair(pr):
                    scb = [p0, p1][pr % 2]
                    w0 = [("dve", mevac.get(pr - 2, qevac[pr]))]
                    if pr in (0, 1) and c == 0:
                        w0 += [("act", qrelu[6 + pr])]
                    if pr == 0:
                        w0 += [n_kf]
                    npe1 = 0
                    for hh in range(2):
                        j = 2 * pr + hh
                        npe1 = pb.em("tensor", lambda e, j=j, c=c, scb=scb, hh=hh, kfv=kfv_c: e.matmul(
                            scb[:, 128 * hh:128 * hh + 128], kfv[:, j, :],
                            qfv[:, j, 128 * c:128 * c + 128], start=True, stop=True),
                            waits=(w0 if hh == 0 else []), inc="pe")
                    mevac[pr] = pb.em("vector", lambda e, pr=pr, scb=scb: e.tensor_tensor(
                        out=smbf[:, 256 * (pr % 2):256 * (pr % 2) + 256], in0=scb[:, 0:256],
                        in1=maskut[:], op=ALU.mult),
                        waits=[("pe", npe1), ("ldc", LDC)], inc="dve")

                emit_m1_pair(0)
                for j in range(H):
                    rr, jj = j // 8, j % 8
                    ypair = [(pya, pyb), (pyc, pyd)][rr % 2]
                    yb = ypair[0] if jj < 4 else ypair[1]
                    ycol = 128 * (jj % 4)
                    if j % 2 == 0 and j // 2 + 1 < 8:
                        emit_m1_pair(j // 2 + 1)
                        if j == 12:
                            mark["m1pe_%d" % (cg % 2)] = pb.n("pe")
                            if cg + 1 < NST * CH_T:
                                kp = (cg + 1) % 2
                                mark["kf_ld"] = ("lkf%d" % kp, pb.dma(
                                    "sync",
                                    kfbufs[kp][:].rearrange(
                                        "p (c q t) -> p c q t", c=NCH, q=2),
                                    KFMh[:, :, :, 128 * (cg + 1):128 * (cg + 1) + 128],
                                    inc="lkf%d" % kp,
                                    waits=[("pe", mark["m1pe_%d" % kp])]))
                    w2 = [("dve", mevac[j // 2]), ("lvt", n_vt)]
                    if jj in (0, 4):
                        w2 += [("dve", mark.get("yevac_dve_%d" % (rr % 2), 0)),
                               ("act", mark.get("yevac_act_%d" % (rr % 2), 0))]
                    npe2 = pb.em("tensor", lambda e, j=j, c=c, yb=yb, ycol=ycol, vtv=vtv_c: e.matmul(
                        yb[:, ycol:ycol + 65],
                        smbf[:, 256 * ((j // 2) % 2) + 128 * (j % 2):
                             256 * ((j // 2) % 2) + 128 * (j % 2) + 128],
                        vtv[:, c, 65 * j:65 * j + 65], start=True, stop=False),
                        waits=w2, inc="pe")
                    pb.em("tensor", lambda e, j=j, c=c, yb=yb, ycol=ycol: e.matmul(
                        yb[:, ycol:ycol + 65], qfv[:, j, 128 * c:128 * c + 128],
                        snapb[:, 65 * j:65 * j + 65], start=False, stop=False),
                        waits=([("act", na_sn)] if j == 0 else []), inc="pe")
                    npe3 = pb.em("tensor", lambda e, j=j, c=c, yb=yb, ycol=ycol: e.matmul(
                        yb[:, ycol:ycol + 65], qfv[:, j, 128 * c:128 * c + 128],
                        sprevb[:, 65 * j:65 * j + 65], start=False, stop=True),
                        inc="pe")
                    if jj == 7:
                        ya_, yb_ = ypair
                        nda = pb.em("vector", lambda e, ya_=ya_, rr=rr: e.tensor_scalar(
                            out=denr[:, 8 * rr:8 * rr + 4],
                            in0=ya_[:].rearrange("p (k t) -> p k t", k=4)[:, :, 64:65],
                            scalar1=ATT_EPS, scalar2=None, op0=ALU.add),
                            waits=[("pe", npe3)], inc="dve")
                        nda = pb.em("vector", lambda e, yb_=yb_, rr=rr: e.tensor_scalar(
                            out=denr[:, 8 * rr + 4:8 * rr + 8],
                            in0=yb_[:].rearrange("p (k t) -> p k t", k=4)[:, :, 64:65],
                            scalar1=ATT_EPS, scalar2=None, op0=ALU.add), inc="dve")
                        pb.drain("vector")
                        nda = pb.em("vector", lambda e, rr=rr: e.reciprocal(
                            out=denr[:, 8 * rr:8 * rr + 8], in_=denr[:, 8 * rr:8 * rr + 8]), inc="dve")
                        pb.drain("vector")
                        nact_y = 0
                        for j2 in range(8):
                            h2 = 8 * rr + j2
                            yb2 = ypair[0] if j2 < 4 else ypair[1]
                            yc2 = 128 * (j2 % 4)
                            if j2 % 2 == 0:
                                nda = pb.em("vector", lambda e, h2=h2, yb2=yb2, yc2=yc2: e.tensor_scalar(
                                    out=ytm[:, 64 * h2:64 * h2 + 64], in0=yb2[:, yc2:yc2 + 64],
                                    scalar1=denr[:, h2:h2 + 1], scalar2=None, op0=ALU.mult),
                                    inc="dve")
                            else:
                                nact_y = pb.em("scalar", lambda e, h2=h2, yb2=yb2, yc2=yc2: e.activation(
                                    out=ytm[:, 64 * h2:64 * h2 + 64], in_=yb2[:, yc2:yc2 + 64],
                                    func=AF.Copy, scale=denr[:, h2:h2 + 1]),
                                    waits=([("dve", nda)] if j2 == 1 else []), inc="act")
                        mark["yevac_dve_%d" % (rr % 2)] = nda
                        mark["yevac_act_%d" % (rr % 2)] = nact_y
                        mark["yevac_dve"] = nda
                for j2 in range(NCH):
                    npe = pb.em("tensor", lambda e, j2=j2: e.transpose(
                        p2[:, 128 * j2:128 * j2 + 128], ytm[:, 128 * j2:128 * j2 + 128], i128b[:]),
                        waits=([("dve", mark["yevac_dve"]),
                                ("act", max(mark.get("tre_act", 0), mark.get("yevac_act_0", 0),
                                            mark.get("yevac_act_1", 0)))] if j2 == 0 else []),
                        inc="pe")
                mark["tre_act"] = pb.em("scalar", lambda e, c=c: e.activation(
                    out=yfv[:, :, 128 * c:128 * c + 128],
                    in_=p2[:].rearrange("p (c t) -> p c t", c=NCH), func=AF.Copy),
                    waits=[("pe", npe)], inc="act")
                mark["m3_pe"] = pb.n("pe")
                mark["mev_last"] = mevac[7]
            mark["vt_pe_%d" % (s % 2)] = pb.n("pe")
            if scan_args is not None:
                emit_scans(*scan_args)
                scan_args = None
            if s + 1 < NST:
                fr_next = emit_front(s + 1)
            T0 = ST * s
            if STAGE < 3:
                continue
            woevac = {}
            npl_ch = {}
            for oc in range(NCH):
                bank = [p0, p1][oc % 2]
                if oc in (0, 1):
                    w0 = [(WCV, cv_wo), ("act", mark["tre_act"]), ("dve", pb.n("dve"))]
                else:
                    w0 = [("dve", woevac[oc - 2])]
                for kc in range(NCH):
                    npe = pb.em("tensor", lambda e, kc=kc, oc=oc, bank=bank: e.matmul(
                        bank[:, 0:ST], wov[:, kc, 128 * oc:128 * oc + 128], yfv[:, kc, :],
                        start=(kc == 0), stop=(kc == NCH - 1)),
                        waits=(w0 if kc == 0 else []), inc="pe")
                woevac[oc] = pb.em("vector", lambda e, oc=oc, bank=bank: e.tensor_tensor(
                    out=gbv[:, oc, :], in0=bank[:, 0:ST], in1=g0v[:, oc, :], op=ALU.mult),
                    waits=[("pe", npe), ("lg0", n_g0)] + ([("pool", pb.n("pool"))] if oc == 0 else []),
                    inc="dve")
                mark["g0_dve"] = woevac[oc]
                npl_ch[oc] = pb.em("gpsimd", lambda e, oc=oc: e.tensor_tensor(
                    out=x1vv[:, oc, :], in0=gbv[:, oc, :], in1=xv[:, oc, :], op=ALU.add),
                    waits=[("dve", woevac[oc])] +
                          ([("lx", n_x), ("sx1", mark["sx1"])] if oc == 0 else []),
                    inc="pool")
            npl = npl_ch[NCH - 1]
            mark["x_pool"] = npl
            mark["wo_pe"] = pb.n("pe")
            mark["sx1"] = pb.dma("gpsimd", X1v[:, :, T0:T0 + ST], x1vv[:, :, :], inc="sx1",
                                 waits=[("pool", npl)])
            if s + 1 < NST:
                n_x = pb.dma("sync", xv[:, :, :], Xv[:, :, T0 + ST:T0 + 2 * ST], inc="lx",
                             waits=[("pool", npl)])
                n_g0 = pb.dma("sync", g0v[:, :, :], G0v[:, :, T0 + ST:T0 + 2 * ST], inc="lg0",
                              waits=[("dve", mark["g0_dve"])])
            if STAGE < 4:
                continue
            na1c = {}
            na2c = {}
            for ch in range(NCH):
                na1c[ch] = pb.em("scalar", lambda e, ch=ch: e.activation(
                    out=x1bv[:, ch, :], in_=x1vv[:, ch, :], func=AF.Copy),
                    waits=[("pool", npl_ch[ch])] +
                          ([("pe", mark["wo_pe"]), ("sxn2", mark["sxn2"])] if ch == 0 else []),
                    inc="act")
                na2c[ch] = pb.em("scalar", lambda e, ch=ch: e.activation(
                    out=x1qv[:, ch, :], in_=x1vv[:, ch, :], func=AF.Square), inc="act")
            for ch in range(NCH):
                pb.em("tensor", lambda e, ch=ch: e.matmul(
                    pyb[0:1, 0:ST], inv1024b[:], x1bv[:, ch, :], start=(ch == 0), stop=(ch == NCH - 1)),
                    waits=[("act", na1c[ch])] +
                          ([("dve", mark["yevac_dve"]),
                            ("act", max(mark.get("yevac_act_0", 0), mark.get("yevac_act_1", 0)))]
                           if ch == 0 else []), inc="pe")
                npe_st = pb.em("tensor", lambda e, ch=ch: e.matmul(
                    pyd[0:1, 0:ST], inv1024b[:], x1qv[:, ch, :], start=(ch == 0), stop=(ch == NCH - 1)),
                    waits=[("act", na2c[ch])], inc="pe")
            m_row = rows[0:1, 0:ST]
            r_row = rows[0:1, ST:2 * ST]
            v_row = rows[0:1, 2 * ST:3 * ST]
            t_row = rows[0:1, 3 * ST:4 * ST]
            na_m = pb.em("scalar", lambda e: e.activation(out=m_row, in_=pyb[0:1, 0:ST], func=AF.Copy),
                         waits=[("pe", npe_st)], inc="act")
            pb.em("vector", lambda e: e.tensor_tensor(out=t_row, in0=m_row, in1=m_row, op=ALU.mult),
                  waits=[("act", na_m)], inc="dve")
            pb.drain("vector")
            nd_var = pb.em("vector", lambda e: e.tensor_tensor(out=v_row, in0=pyd[0:1, 0:ST], in1=t_row, op=ALU.subtract),
                           inc="dve")
            pb.em("scalar", lambda e: e.activation(out=t_row, in_=v_row, func=AF.Ln, bias=cns[0:1, 2:3]),
                  waits=[("dve", nd_var)], inc="act")
            pb.drain("scalar")
            na_rows = pb.em("scalar", lambda e: e.activation(out=r_row, in_=t_row, func=AF.Exp, scale=-0.5),
                            inc="act")
            pb.em("tensor", lambda e: e.matmul(pyb[:, 0:ST], ones1f[:], m_row, start=True, stop=True),
                  waits=[("act", na_rows)], inc="pe")
            npe_bc = pb.em("tensor", lambda e: e.matmul(pyd[:, 0:ST], ones1f[:], r_row, start=True, stop=True),
                           inc="pe")
            for ch in range(NCH):
                pb.em("vector", lambda e, ch=ch: e.tensor_tensor(
                    out=gbv[:, ch, :], in0=x1vv[:, ch, :], in1=pyb[:, 0:ST], op=ALU.subtract),
                    waits=([("pe", npe_bc), ("pool", npl)] if ch == 0 else []), inc="dve")
            pb.drain("vector")
            for ch in range(NCH):
                nd_xn2 = pb.em("vector", lambda e, ch=ch: e.tensor_tensor(
                    out=xn2v[:, ch, :], in0=gbv[:, ch, :], in1=pyd[:, 0:ST], op=ALU.mult),
                    waits=([("sxn2", mark["sxn2"])] if ch == 0 else []), inc="dve")
            mark["sxn2"] = pb.dma("gpsimd", XN2v[:, :, T0:T0 + ST], xn2v[:, :, :], inc="sxn2",
                                  waits=[("dve", nd_xn2)])
            if STAGE < 5:
                continue
            gevac = {}
            for pi, (wv_, bh, sc_, dst) in enumerate([
                    (wgv, bgh, 0.5, tgv), (wv1v, bvc, 1.0, tvv), (wdv, bdh, 0.5, tdv)]):
                cvw = [cv_wg, cv_wv1, cv_wd][pi]
                for mc in range(NCH):
                    bank = [pya, pyc][mc % 2]
                    if pi == 0 and mc in (0, 1):
                        w0 = [(WCV, cvw), ("dve", nd_xn2),
                              ("dve", mark.get("yevac_dve_%d" % (mc % 2), 0)),
                              ("act", mark.get("yevac_act_%d" % (mc % 2), 0))]
                    else:
                        w0 = [("act", gevac.get((pi, mc - 2), gevac.get((pi - 1, mc + 6), 0)))]
                        w0 += [(WCV, cvw)] if mc == 0 else []
                    for kc in range(NCH):
                        npe = pb.em("tensor", lambda e, kc=kc, mc=mc, bank=bank, wv_=wv_: e.matmul(
                            bank[:, 0:ST], wv_[:, kc, 128 * mc:128 * mc + 128], xn2v[:, kc, :],
                            start=(kc == 0), stop=(kc == NCH - 1)),
                            waits=(w0 if kc == 0 else []), inc="pe")
                    gevac[(pi, mc)] = pb.em("scalar", lambda e, mc=mc, bank=bank, bh=bh, sc_=sc_, dst=dst: e.activation(
                        out=dst[:, mc, :], in_=bank[:, 0:ST], func=AF.Tanh,
                        scale=sc_, bias=bh[:, mc:mc + 1]),
                        waits=[("pe", npe)], inc="act")
            na_g = gevac[(2, NCH - 1)]
            nd_sg = pb.em("vector", lambda e: e.tensor_scalar(out=gbuf[:], in0=tg[:], scalar1=0.5,
                          scalar2=0.5, op0=ALU.mult, op1=ALU.add),
                          waits=[("act", na_g)], inc="dve")
            np2 = pb.em("gpsimd", lambda e: e.tensor_tensor(out=tv[:], in0=gbuf[:], in1=tv[:], op=ALU.mult),
                        waits=[("dve", nd_sg)], inc="pool")
            pb.drain("vector")
            nd_aa = pb.em("vector", lambda e: e.tensor_scalar(out=tg[:], in0=td[:], scalar1=0.499,
                          scalar2=0.5, op0=ALU.mult, op1=ALU.add), inc="dve")
            np3 = np2
            pb.drain("vector")
            if STAGE < 6:
                continue
            scan_args = (s, T0, np3)
            if s + 1 < NST:
                fr = fr_next
        if scan_args is not None:
            emit_scans(*scan_args)
        pb.em("sync", lambda e: e.nop(), waits=[(snm, pb.n(snm)) for snm in
              ("sx1", "sxn2", "ss1", "spp", "sah") if pb.n(snm) > 0])


def build_pass1(ST=256, STAGE=99, NSTLIM=99):
    nc = bass.Bass("TRN2", target_bir_lowering=False)
    io = {k: nc.dram_tensor(k, *P1_SHAPES[k], kind="ExternalInput").ap() for k in P1_IN}
    io.update({k: nc.dram_tensor(k, *P1_SHAPES[k], kind="ExternalOutput").ap() for k in P1_OUT})
    pb = PB(nc)
    with contextlib.ExitStack() as ctx:
        emit_pass1(nc, pb, ctx, io, ST=ST, STAGE=STAGE, NSTLIM=NSTLIM)
        pb.build(ctx)
    return nc



P2_IN = ["XN2", "X1", "S1", "PP", "AHG", "MK", "W1", "B1C", "W2", "B2C"]
P2_OUT = ["OUT"]
P2_SHAPES = {
    "XN2": ([128, NCH * TCORE], BF16), "X1": ([128, NCH * TCORE], F32),
    "S1": ([128, NCH * TCORE], F32), "PP": ([128, NCH * TCORE], F32),
    "AHG": ([512, 16], F32), "MK": ([128, 16], F32),
    "W1": ([D, 4096], F32), "B1C": ([128, 32], F32),
    "W2": ([4096, D], F32), "B2C": ([128, 8], F32),
    "OUT": ([128, NCH * TCORE], F32),
}


def emit_pass2(nc, pb, ctx, io, ST=256, NSTLIM=99):
    NST = min(TCORE // ST, NSTLIM)
    NHC = 32
    XN2, X1, S1, PP, AHG, MK, W1, B1C, W2, B2C = [io[k] for k in P2_IN]
    OUT = io["OUT"]

    XN2v = XN2.rearrange("p (c t) -> p c t", c=NCH)
    X1v = X1.rearrange("p (c t) -> p c t", c=NCH)
    S1v = S1.rearrange("p (c t) -> p c t", c=NCH)
    PPv = PP.rearrange("p (c t) -> p c t", c=NCH)
    OUTv = OUT.rearrange("p (c t) -> p c t", c=NCH)

    if True:
        sb = lambda nm, shp, dt=F32: ctx.enter_context(nc.sbuf_tensor(nm + "_2", shp, dt))
        ps = lambda nm, shp, dt=F32: ctx.enter_context(nc.psum_tensor(nm + "_2", shp, dt))

        stage = sb("stage", [128, 2048])
        xn2b = sb("xn2b", [128, NCH * ST], BF16)
        x1f = sb("x1f", [128, NCH * ST])
        s1f = sb("s1f", [128, NCH * ST])
        ppf = sb("ppf", [128, NCH * ST])
        hb = sb("hb", [128, NHC * ST], BF16)
        fsb = sb("fsb", [128, NCH * ST])
        tqb = sb("tqb", [128, NCH * ST])
        outf = sb("outf", [128, NCH * ST])
        ahg = sb("ahg", [128, 16])
        hin = sb("hin", [128, 8])
        abar = sb("abar", [128, 8])
        hbar = sb("hbar", [128, 8])
        tmp8 = sb("tmp8", [128, 8])
        mk = sb("mk", [128, 16])
        b1c = sb("b1c", [128, 32])
        b2c = sb("b2c", [128, 8])
        b2h = sb("b2h", [128, 8])

        p0 = ps("p0", [128, 512])
        p1 = ps("p1", [128, 512])

        xn2v = xn2b[:].rearrange("p (c t) -> p c t", c=NCH)
        x1vv = x1f[:].rearrange("p (c t) -> p c t", c=NCH)
        s1vv = s1f[:].rearrange("p (c t) -> p c t", c=NCH)
        ppvv = ppf[:].rearrange("p (c t) -> p c t", c=NCH)
        hbv = hb[:].rearrange("p (c t) -> p c t", c=NHC)
        fsv = fsb[:].rearrange("p (c t) -> p c t", c=NCH)
        tqv = tqb[:].rearrange("p (c t) -> p c t", c=NCH)
        outv = outf[:].rearrange("p (c t) -> p c t", c=NCH)

        pb.dma("sync", mk[:], MK, inc="ldc")
        pb.dma("sync", b1c[:], B1C, inc="ldc")
        pb.dma("sync", b2c[:], B2C, inc="ldc")
        LDC = pb.n("ldc")
        nd = pb.em("vector", lambda e: e.tensor_scalar(out=b2h[:], in0=b2c[:], scalar1=0.5,
                   scalar2=None, op0=ALU.mult), waits=[("ldc", LDC)], inc="dve")
        nld = 0
        for r in range(4):
            nld = pb.dma("sync", ahg[:].rearrange("p c -> p c")[:, :], AHG[128 * r:128 * (r + 1), :], inc="lah",
                         waits=[("pool", pb.n("pool"))] + (list(io.get("_ahg_waits", [])) if r == 0 else []))
            if r == 0:
                nd = pb.em("gpsimd", lambda e: e.tensor_scalar(
                    out=hin[:], in0=ahg[:, 8:16], scalar1=mk[:, 0:1], scalar2=None, op0=ALU.mult),
                    waits=[("lah", nld), ("ldc", LDC)], inc="pool")
                pb.drain("gpsimd")
            else:
                nd = pb.em("gpsimd", lambda e, r=r: e.tensor_scalar(
                    out=abar[:], in0=ahg[:, 0:8], scalar1=mk[:, r:r + 1],
                    scalar2=mk[:, 8 + r:9 + r], op0=ALU.mult, op1=ALU.add),
                    waits=[("lah", nld)], inc="pool")
                nd = pb.em("gpsimd", lambda e, r=r: e.tensor_scalar(
                    out=hbar[:], in0=ahg[:, 8:16], scalar1=mk[:, r:r + 1],
                    scalar2=None, op0=ALU.mult), inc="pool")
                pb.drain("gpsimd")
                nd = pb.em("gpsimd", lambda e: e.tensor_tensor(out=tmp8[:], in0=hin[:], in1=abar[:], op=ALU.mult),
                           inc="pool")
                pb.drain("gpsimd")
                nd = pb.em("gpsimd", lambda e: e.tensor_tensor(out=hin[:], in0=tmp8[:], in1=hbar[:], op=ALU.add),
                           inc="pool")
                pb.drain("gpsimd")
        nd_hin = nd

        w1 = ctx.enter_context(nc.sbuf_tensor("w1_bf_2", [128, 8 * 4096], BF16))
        w1v_ = w1[:].rearrange("p (k o) -> p k o", k=8)
        W1r = W1.rearrange("(k p) o -> p k o", p=128)
        cv_w1h = []
        for half in range(2):
            dn = 0
            for k in range(8):
                nl = pb.dma("scalar", stage[:, 0:2048], W1r[:, k, 2048 * half:2048 * (half + 1)],
                            inc="wld", waits=[("wcv", pb.n("wcv"))])
                dn = pb.em("vector", lambda e, k=k, half=half: e.tensor_copy(
                    w1v_[:, k, 2048 * half:2048 * (half + 1)], stage[:, 0:2048]),
                    waits=[("wld", nl)], inc="wcv")
            cv_w1h.append(dn)
        cv_w1 = cv_w1h[1]
        w2, cv_w2 = load_weight_bf16(pb, ctx, "w2", W2, (128, 32, D), stage)
        w1v = w1[:].rearrange("p (k o) -> p k o", k=8)
        w2v = w2[:].rearrange("p (k o) -> p k o", k=32)

        mark = {"xn_pe": 0, "x1_dve": 0, "s1_dve": 0, "pp_dve": 0, "sout": 0,
                "h_pe": 0}
        for s in range(NST):
            T0 = ST * s
            n_xn = pb.dma("sync", xn2v[:, :, :], XN2v[:, :, T0:T0 + ST], inc="lxn",
                          waits=[("pe", mark["xn_pe"])])
            n_x1 = pb.dma("sync", x1vv[:, :, :], X1v[:, :, T0:T0 + ST], inc="lx1",
                          waits=[("dve", mark["x1_dve"])])
            n_s1 = pb.dma("sync", s1vv[:, :, :], S1v[:, :, T0:T0 + ST], inc="ls1",
                          waits=[("dve", mark["s1_dve"])])
            n_pp = pb.dma("sync", ppvv[:, :, :], PPv[:, :, T0:T0 + ST], inc="lpp",
                          waits=[("dve", mark["pp_dve"])])
            hevac = {}
            for hc in range(NHC):
                bank = [p0, p1][hc % 2]
                if hc in (0, 1):
                    w0 = [("wcv", cv_w1h[0]), ("lxn", n_xn),
                          ("dve", mark.get("fevac_%d" % (hc % 2), 0)),
                          ("act", pb.n("act")), ("pe", mark["h_pe"])]
                elif hc in (16, 17):
                    w0 = [("wcv", cv_w1h[1]),
                          (("act", hevac[hc - 2][1]) if hevac[hc - 2][0] == "act"
                           else ("dve", hevac[hc - 2][1]))]
                else:
                    w0 = [(("act", hevac[hc - 2][1]) if hevac[hc - 2][0] == "act"
                           else ("dve", hevac[hc - 2][1]))]
                for kc in range(NCH):
                    npe = pb.em("tensor", lambda e, kc=kc, hc=hc, bank=bank: e.matmul(
                        bank[:, 0:ST], w1v[:, kc, 128 * hc:128 * hc + 128], xn2v[:, kc, :],
                        start=(kc == 0), stop=(kc == NCH - 1)),
                        waits=(w0 if kc == 0 else []), inc="pe")
                if hc % 2 == 0:
                    na = pb.em("scalar", lambda e, hc=hc, bank=bank: e.activation(
                        out=hbv[:, hc, :], in_=bank[:, 0:ST], func=AF.Relu,
                        bias=b1c[:, hc:hc + 1]), waits=[("pe", npe)], inc="act")
                    hevac[hc] = ("act", na)
                else:
                    nd = pb.em("vector", lambda e, hc=hc, bank=bank: e.tensor_scalar(
                        out=hbv[:, hc, :], in0=bank[:, 0:ST], scalar1=b1c[:, hc:hc + 1],
                        scalar2=0.0, op0=ALU.add, op1=ALU.max),
                        waits=[("pe", npe)], inc="dve")
                    hevac[hc] = ("dve", nd)
            fevac = {}
            for oc in range(NCH):
                bank = [p0, p1][oc % 2]
                if oc in (0, 1):
                    la, lb = hevac[NHC - 2], hevac[NHC - 1]
                    w0 = [("wcv", cv_w2),
                          (("act", la[1]) if la[0] == "act" else ("dve", la[1])),
                          (("act", lb[1]) if lb[0] == "act" else ("dve", lb[1]))]
                else:
                    w0 = [("dve", fevac[oc - 2])]
                for kc in range(NHC):
                    npe = pb.em("tensor", lambda e, kc=kc, oc=oc, bank=bank: e.matmul(
                        bank[:, 0:ST], w2v[:, kc, 128 * oc:128 * oc + 128], hbv[:, kc, :],
                        start=(kc == 0), stop=(kc == NHC - 1)),
                        waits=(w0 if kc == 0 else []), inc="pe")
                fevac[oc] = pb.em("vector", lambda e, oc=oc, bank=bank: e.tensor_scalar(
                    out=fsv[:, oc, :], in0=bank[:, 0:ST], scalar1=b2c[:, oc:oc + 1],
                    scalar2=None, op0=ALU.add),
                    waits=[("pe", npe)] + ([("dve", mark.get("nd1", 0))] if oc == 0 else []),
                    inc="dve")
                mark["fevac_%d" % (oc % 2)] = fevac[oc]
            mark["xn_pe"] = pb.n("pe")
            mark["h_pe"] = pb.n("pe")
            na_t = pb.em("scalar", lambda e: e.activation(out=tqb[:], in_=fsb[:], func=AF.Tanh, scale=0.5),
                         waits=[("dve", fevac[NCH - 1])], inc="act")
            pb.drain("vector")
            nd1 = pb.em("vector", lambda e: e.scalar_tensor_tensor(
                out=tqb[:], in0=tqb[:], scalar=1.0, in1=fsb[:], op0=ALU.add, op1=ALU.mult),
                waits=[("act", na_t)], inc="dve")
            mark["nd1"] = nd1
            for oc in range(NCH):
                nd2 = pb.em("vector", lambda e, oc=oc: e.scalar_tensor_tensor(
                    out=ppvv[:, oc, :], in0=ppvv[:, oc, :], scalar=hin[:, oc:oc + 1],
                    in1=s1vv[:, oc, :], op0=ALU.mult, op1=ALU.add),
                    waits=([("lpp", n_pp), ("ls1", n_s1), ("pool", nd_hin)] if oc == 0 else []), inc="dve")
            mark["s1_dve"] = nd2
            pb.drain("vector")
            nd3 = pb.em("vector", lambda e: e.tensor_tensor(out=tqb[:], in0=ppf[:], in1=tqb[:], op=ALU.mult),
                        inc="dve")
            mark["pp_dve"] = nd3
            pb.drain("vector")
            nd4 = pb.em("vector", lambda e: e.scalar_tensor_tensor(
                out=outf[:], in0=tqb[:], scalar=0.5, in1=x1f[:], op0=ALU.mult, op1=ALU.add),
                waits=[("lx1", n_x1), ("sout", mark["sout"])], inc="dve")
            mark["x1_dve"] = nd4
            mark["sout"] = pb.dma("gpsimd", OUTv[:, :, T0:T0 + ST], outv[:, :, :], inc="sout",
                                  waits=[("dve", nd4)])
        pb.em("sync", lambda e: e.nop(), waits=[("sout", pb.n("sout"))])


def build_pass2(ST=256, NSTLIM=99):
    nc = bass.Bass("TRN2", target_bir_lowering=False)
    io = {k: nc.dram_tensor(k, *P2_SHAPES[k], kind="ExternalInput").ap() for k in P2_IN}
    io.update({k: nc.dram_tensor(k, *P2_SHAPES[k], kind="ExternalOutput").ap() for k in P2_OUT})
    pb = PB(nc)
    with contextlib.ExitStack() as ctx:
        emit_pass2(nc, pb, ctx, io, ST=ST, NSTLIM=NSTLIM)
        pb.build(ctx)
    return nc


def host_inputs_pass2(inputs, core, r1):
    b, q = core // 4, core % 4
    mkv = np.zeros((128, 16), np.float32)
    for r in range(4):
        pred = r < q
        mkv[:, r] = 1.0 if pred else 0.0
        mkv[:, 8 + r] = 0.0 if pred else 1.0
    ahg = np.concatenate([np.asarray(r1[4 * b + j]["AH"]) for j in range(4)], axis=0)
    res = r1[core]
    return {
        "XN2": np.asarray(res["XN2"]), "X1": np.asarray(res["X1"]),
        "S1": np.asarray(res["S1"]), "PP": np.asarray(res["PP"]),
        "AHG": ahg, "MK": mkv,
        "W1": np.asarray(inputs["L1_ffn_W1"], np.float32),
        "B1C": np.asarray(inputs["L1_ffn_b1"], np.float32).reshape(32, 128).T.copy(),
        "W2": np.asarray(inputs["L1_ffn_W2"], np.float32),
        "B2C": np.asarray(inputs["L1_ffn_b2"], np.float32).reshape(8, 128).T.copy(),
    }


def host_inputs_pass0(inputs, core):
    b, q = core // 4, core % 4
    x = np.ascontiguousarray(np.asarray(inputs["L0_attn_x"], np.float32)[b, q * TCORE:(q + 1) * TCORE, :].T)
    W1 = np.asarray(inputs["L0_ffn_W1"], np.float32)
    b1 = np.asarray(inputs["L0_ffn_b1"], np.float32)
    W2 = np.asarray(inputs["L0_ffn_W2"], np.float32)
    w1f = np.zeros((128, 8), np.float32)
    b1f = np.zeros((128, 8), np.float32)
    w2bd = np.zeros((D, 64), np.float32)
    f = np.arange(1024)
    w1f[f % 128, f // 128] = W1[f // 16, f % 16]
    b1f[f % 128, f // 128] = b1[f // 16, f % 16]
    w2bd[f, f // 16] = W2[f // 16, f % 16]
    E = np.zeros((64, D), np.float32)
    E[f // 16, f] = 1.0
    return {
        "X": x,
        "WK": np.asarray(inputs["L0_attn_Wk"], np.float32),
        "WV": np.asarray(inputs["L0_attn_Wv"], np.float32),
        "WIN": np.asarray(inputs["L0_ffn_Win"], np.float32),
        "EB": E, "W1F": w1f, "B1F": b1f, "W2BD": w2bd,
        "WOUT": np.asarray(inputs["L0_ffn_Wout"], np.float32),
        "BIN": np.asarray(inputs["L0_ffn_bin_"], np.float32).reshape(64, 1),
        "B2": np.asarray(inputs["L0_ffn_b2"], np.float32).reshape(64, 1),
        "BOUTC": np.asarray(inputs["L0_ffn_bout"], np.float32).reshape(8, 128).T.copy(),
        "I64": np.eye(128, dtype=np.float32),
        "CONSTC": np.stack([np.full(128, 1.0 / 1024.0, np.float32), np.ones(128, np.float32), np.full(128, LN_EPS, np.float32), np.full(128, ATT_EPS, np.float32)], axis=1),
        "ONESR": np.ones((1, 128), np.float32),
    }


def host_inputs_pass1(inputs, core, r0):
    b, q = core // 4, core % 4
    x = np.ascontiguousarray(np.asarray(inputs["L0_attn_x"], np.float32)[b, q * TCORE:(q + 1) * TCORE, :].T)
    mkv = np.zeros((128, 16), np.float32)
    for r in range(4):
        pred = r < q
        mkv[:, r] = 1.0 if pred else 0.0
        mkv[:, 8 + r] = 0.0 if pred else 1.0
    res = r0[core]
    szg = np.concatenate([np.asarray(r0[4 * b + j]["SZ"]) for j in range(4)], axis=0)
    constc = np.stack([np.full(128, 1.0 / 1024.0, np.float32), np.ones(128, np.float32),
                       np.full(128, LN_EPS, np.float32), np.full(128, ATT_EPS, np.float32)], axis=1)
    col8 = lambda v: np.asarray(v, np.float32).reshape(8, 128).T.copy()
    mask = np.tril(np.ones((128, 128), np.float32)).T.copy()
    return {
        "X": x, "XN": np.asarray(res["XN"]), "KFM": np.asarray(res["KFM"]),
        "VTM": np.asarray(res["VTM"]), "G0": np.asarray(res["G0"]),
        "SNP": np.asarray(res["SNP"]), "SZG": szg, "MK": mkv,
        "WQ": np.asarray(inputs["L0_attn_Wq"], np.float32),
        "WO": np.asarray(inputs["L0_attn_Wo"], np.float32),
        "WG": np.asarray(inputs["L1_attn_Wg"], np.float32),
        "WV1": np.asarray(inputs["L1_attn_Wv"], np.float32),
        "WD": np.asarray(inputs["L1_attn_Wd"], np.float32),
        "BGC": col8(inputs["L1_attn_bg"]), "BVC": col8(inputs["L1_attn_bv"]),
        "BDC": col8(inputs["L1_attn_bd"]),
        "MASKUT": mask, "I128": np.eye(128, dtype=np.float32),
        "CONSTC": constc, "ONESR": np.ones((1, 128), np.float32),
    }


def pb_barrier(pb, name):
    for eng in ("scalar", "vector", "gpsimd"):
        pb.drain(eng)
    for eng in ENGINES:
        pb.em(eng, lambda e: e.nop(), inc=name)
    val = pb.n(name)
    for eng in ENGINES:
        pb.em(eng, lambda e: e.nop(), waits=[(name, val)])


FUSED_EXT_IN = (P0_IN + ["MK", "WQ", "WO", "WG", "WV1", "WD", "BGC", "BVC",
                         "BDC", "W1", "B1C", "W2", "B2C"])
FUSED_INTERNAL = ["XN", "KFM", "VTM", "G0", "SZ", "SNP", "X1", "XN2", "S1",
                  "PP", "AH"]


def build_fused():
    nc = bass.Bass("TRN2", target_bir_lowering=False)
    shapes = dict(P0_SHAPES); shapes.update(P1_SHAPES); shapes.update(P2_SHAPES)
    io = {}
    for k in FUSED_EXT_IN:
        io[k] = nc.dram_tensor(k, *shapes[k], kind="ExternalInput").ap()
    for k in FUSED_INTERNAL:
        io[k] = nc.dram_tensor(k, *shapes[k]).ap()
    io["OUT"] = nc.dram_tensor("OUT", *shapes["OUT"], kind="ExternalOutput").ap()
    io["SZG"] = nc.dram_tensor("SZG", [256, 1040], F32).ap()
    io["AHG"] = nc.dram_tensor("AHG", [512, 16], F32).ap()
    io["I128"] = io["I64"]
    io["MASKUT"] = nc.dram_tensor("MASKUT", [128, 128], F32, kind="ExternalInput").ap()
    rg = [[0, 1, 2, 3], [4, 5, 6, 7]]
    pb = PB(nc)
    with contextlib.ExitStack() as ctx:
        with contextlib.ExitStack() as c01:
            prestage = c01.enter_context(nc.sbuf_tensor("prestage", [128, 1024], F32))
            pre_tiles = {nm: c01.enter_context(nc.sbuf_tensor(nm + "_bf", [128, 8 * D], BF16))
                         for nm in ("wq", "wo")}
            preW = []

            def _preload(pb):
                for nm, ap in [("wq", io["WQ"]), ("wo", io["WO"])]:
                    preW.append(load_weight_bf16(pb, c01, nm, ap, (128, 8, D), prestage,
                                                 dma_eng="gpsimd", cv_eng="gpsimd", pre="p",
                                                 tile=pre_tiles[nm]))
            def _mid(pb):
                pb.em("gpsimd", lambda e: e.nop(), waits=[("ssz", pb.n("ssz"))])
                pb.em("gpsimd", lambda e: e.collective_compute(
                    "AllGather", ALU.bypass, replica_groups=rg,
                    ins=[io["SZ"].opt()], outs=[io["SZG"].opt()]), inc="cc")
            io["_mid_hook"] = _mid
            io["_preload"] = _preload
            with contextlib.ExitStack() as c0:
                emit_pass0(nc, pb, c0, io)
            for nm in ("wg", "wv1", "wd"):
                pre_tiles[nm] = c01.enter_context(nc.sbuf_tensor(nm + "_bf", [128, 8 * D], BF16))
            pb_barrier(pb, "bar0")
            for nm, ap in [("wg", io["WG"]), ("wv1", io["WV1"]), ("wd", io["WD"])]:
                preW.append(load_weight_bf16(pb, c01, nm, ap, (128, 8, D), prestage,
                                             dma_eng="gpsimd", cv_eng="vector", pre="p",
                                             tile=pre_tiles[nm]))
            io["_preW"] = preW
            io["_szg_waits"] = [("cc", pb.n("cc"))]
            with contextlib.ExitStack() as c1:
                emit_pass1(nc, pb, c1, io)
        pb_barrier(pb, "bar1")
        pb.em("gpsimd", lambda e: e.collective_compute(
            "AllGather", ALU.bypass, replica_groups=rg,
            ins=[io["AH"].opt()], outs=[io["AHG"].opt()]), inc="cc")
        io["_ahg_waits"] = [("cc", pb.n("cc"))]
        with contextlib.ExitStack() as c2:
            emit_pass2(nc, pb, c2, io)
        pb.build(ctx)
    return nc


def host_inputs_fused(inputs, core):
    h0 = host_inputs_pass0(inputs, core)
    b, q = core // 4, core % 4
    mkv = np.zeros((128, 16), np.float32)
    for r in range(4):
        pred = r < q
        mkv[:, r] = 1.0 if pred else 0.0
        mkv[:, 8 + r] = 0.0 if pred else 1.0
    col8 = lambda v: np.asarray(v, np.float32).reshape(8, 128).T.copy()
    h0.update({
        "MK": mkv,
        "MASKUT": np.tril(np.ones((128, 128), np.float32)).T.copy(),
        "WQ": np.asarray(inputs["L0_attn_Wq"], np.float32),
        "WO": np.asarray(inputs["L0_attn_Wo"], np.float32),
        "WG": np.asarray(inputs["L1_attn_Wg"], np.float32),
        "WV1": np.asarray(inputs["L1_attn_Wv"], np.float32),
        "WD": np.asarray(inputs["L1_attn_Wd"], np.float32),
        "BGC": col8(inputs["L1_attn_bg"]), "BVC": col8(inputs["L1_attn_bv"]),
        "BDC": col8(inputs["L1_attn_bd"]),
        "W1": np.asarray(inputs["L1_ffn_W1"], np.float32),
        "B1C": np.asarray(inputs["L1_ffn_b1"], np.float32).reshape(32, 128).T.copy(),
        "W2": np.asarray(inputs["L1_ffn_W2"], np.float32),
        "B2C": np.asarray(inputs["L1_ffn_b2"], np.float32).reshape(8, 128).T.copy(),
    })
    return h0


def kernel(**inputs):
    nc = build_fused()
    in_maps = [host_inputs_fused(inputs, c) for c in range(NCORES)]
    rr = run_bass_kernel_spmd(nc, in_maps, core_ids=list(range(NCORES))).results
    out = np.empty((2, 16384, 1024), np.float32)
    for c in range(NCORES):
        b, q = c // 4, c % 4
        o = np.asarray(rr[c]["OUT"]).reshape(128, NCH, TCORE)
        out[b, q * TCORE:(q + 1) * TCORE, :] = o.transpose(2, 1, 0).reshape(TCORE, D)
    return out


def kernel_unfused(**inputs):
    nc0 = build_pass0()
    in_maps0 = [host_inputs_pass0(inputs, c) for c in range(NCORES)]
    r0 = run_bass_kernel_spmd(nc0, in_maps0, core_ids=list(range(NCORES))).results
    nc1 = build_pass1()
    in_maps1 = [host_inputs_pass1(inputs, c, r0) for c in range(NCORES)]
    r1 = run_bass_kernel_spmd(nc1, in_maps1, core_ids=list(range(NCORES))).results
    nc2 = build_pass2()
    in_maps2 = [host_inputs_pass2(inputs, c, r1) for c in range(NCORES)]
    r2 = run_bass_kernel_spmd(nc2, in_maps2, core_ids=list(range(NCORES))).results
    out = np.empty((2, 16384, 1024), np.float32)
    for c in range(NCORES):
        b, q = c // 4, c % 4
        o = np.asarray(r2[c]["OUT"]).reshape(128, NCH, TCORE)
        out[b, q * TCORE:(q + 1) * TCORE, :] = o.transpose(2, 1, 0).reshape(TCORE, D)
    return out
